# version 34
# baseline (speedup 1.0000x reference)
import sys
sys.path.insert(0, "/opt/trn_rl_repo")
from contextlib import ExitStack

import numpy as np
import concourse.bass as bass
import concourse.mybir as mybir
from concourse.bass_utils import run_bass_kernel_spmd

F32 = mybir.dt.float32
U8 = mybir.dt.uint8
BF16 = mybir.dt.bfloat16
I32 = mybir.dt.int32
AF = mybir.ActivationFunctionType
ALU = mybir.AluOpType
AP = bass.AP

T = 2048
DM = 2048
DIN = 4096
NH = 32
CHUNK = 256
NCH = 8
CONVD = 4224
DIP = 8352
INTER = 5504
EPS = 1e-5
NKD = 16
NKI = 32
NMO = 43
NMP = 66

_CACHE = {}


def build(stage="full", dump=()):
    nc = bass.Bass("TRN2", target_bir_lowering=False)

    u_in = nc.dram_tensor("u", [T, DM], F32, kind="ExternalInput")
    w_in = nc.dram_tensor("w_in", [DM, DIP], F32, kind="ExternalInput")
    conv_w = nc.dram_tensor("conv_w", [CONVD, 4], F32, kind="ExternalInput")
    conv_b = nc.dram_tensor("conv_b", [CONVD, 1], F32, kind="ExternalInput")
    dt_bias = nc.dram_tensor("dt_bias", [NH, 1], F32, kind="ExternalInput")
    a_log = nc.dram_tensor("a_log", [NH, 1], F32, kind="ExternalInput")
    d_in = nc.dram_tensor("d_vec", [1, NH], F32, kind="ExternalInput")
    norm_w = nc.dram_tensor("norm_w", [DIN, 1], F32, kind="ExternalInput")
    w_out = nc.dram_tensor("w_out", [DIN, DM], F32, kind="ExternalInput")
    lut_in = nc.dram_tensor("lut", [1, 16], F32, kind="ExternalInput")
    gidx = nc.dram_tensor("gate_idx", [INTER, DM], I32, kind="ExternalInput")
    uidx = nc.dram_tensor("up_idx", [INTER, DM], I32, kind="ExternalInput")
    didx = nc.dram_tensor("down_idx", [DM, INTER], I32, kind="ExternalInput")
    out = nc.dram_tensor("out", [T, DM], F32, kind="ExternalOutput")

    uT_s = nc.dram_tensor("uT_s", [NKD, 128, DM], F32)
    zxb_s = nc.dram_tensor("zxb_s", [DIN + CONVD, T], BF16)
    ynorm_s = nc.dram_tensor("ynorm_s", [DIN, T], BF16)
    x1f_s = nc.dram_tensor("x1f_s", [NKD, 128, T], F32)
    x1b_s = nc.dram_tensor("x1b_s", [NKD, 128, T], BF16)
    h_s = nc.dram_tensor("h_s", [NMO, 128, T], BF16)
    wgT_s = nc.dram_tensor("wgT_s", [NMO, 128, DM], BF16)
    wuT_s = nc.dram_tensor("wuT_s", [NMO, 128, DM], BF16)
    wdT_s = nc.dram_tensor("wdT_s", [NKD, 128, INTER], BF16)
    wob_s = nc.dram_tensor("wob_s", [NKD, 128, DIN], BF16)
    acr_s = nc.dram_tensor("acr_s", [1, NH * CHUNK], F32)
    row_s = nc.dram_tensor("row_s", [4, 2048], F32)

    dump_specs = {"zxb": zxb_s, "ynorm": ynorm_s, "uT": uT_s, "x1f": x1f_s,
                  "h": h_s, "wgT": wgT_s, "wuT": wuT_s, "wdT": wdT_s, "wob": wob_s}
    dump_outs = {}
    row_srcs = {}

    es = ExitStack()
    with es:
        ident_f = es.enter_context(nc.sbuf_tensor("ident_f", [128, 128], F32))
        ident_b = es.enter_context(nc.sbuf_tensor("ident_b", [128, 128], BF16))
        lutc = es.enter_context(nc.sbuf_tensor("lutc", [128, 16], F32))
        treec = es.enter_context(nc.sbuf_tensor("treec", [128, 16], F32))
        umask = es.enter_context(nc.sbuf_tensor("umask", [128, 2 * CHUNK], U8))
        zero_b = es.enter_context(nc.sbuf_tensor("zero_b", [128, CHUNK], BF16))
        onecol = es.enter_context(nc.sbuf_tensor("onecol", [128, 1], BF16))
        ceps = es.enter_context(nc.sbuf_tensor("ceps", [128, 1], F32))
        chalf = es.enter_context(nc.sbuf_tensor("chalf", [128, 1], F32))
        cidin = es.enter_context(nc.sbuf_tensor("cidin", [128, 1], F32))
        cwcol = es.enter_context(nc.sbuf_tensor("cwcol", [128, 33 * 4], F32))
        cbcol = es.enter_context(nc.sbuf_tensor("cbcol", [128, 33], F32))
        dtbc = es.enter_context(nc.sbuf_tensor("dtbc", [NH, 1], F32))
        nega = es.enter_context(nc.sbuf_tensor("nega", [NH, 1], F32))
        dcol128 = es.enter_context(nc.sbuf_tensor("dcol128", [128, NH], F32))
        nwcol = es.enter_context(nc.sbuf_tensor("nwcol", [128, NKI], F32))

        class Sem:
            def __init__(self, name):
                self.h = es.enter_context(nc.semaphore(name))
                self.v = 0

        sQ, aQ, gQ = Sem("sQ"), Sem("aQ"), Sem("gQ")
        sV, sS, sP, sG = Sem("sV"), Sem("sS"), Sem("sP"), Sem("sG")

        class DStream:

            def __init__(self, name, n=2):
                self.sems = [es.enter_context(nc.semaphore(f"{name}{i}")) for i in range(n)]
                self.v = [0] * n
                self.n = n
                self.i = 0

            def dma(self, q, out, in_, **kw):
                k = self.i % self.n
                q.wait_ge(self.sems[k], self.v[k])
                q.dma_start(out, in_, **kw).then_inc(self.sems[k], 16)
                self.v[k] += 16
                self.i += 1
                return (self.sems[k], self.v[k])

            def last(self):
                k = (self.i - 1) % self.n
                return (self.sems[k], self.v[k])

            def flush(self, eng):
                for k in range(self.n):
                    eng.wait_ge(self.sems[k], self.v[k])

        def wt(eng, tok):
            eng.wait_ge(tok[0], tok[1])

        DS = {}
        for nm, nn in [("gix", 2), ("gst", 2), ("prm", 1), ("prm2", 1), ("ul", 2),
                       ("ust", 2), ("win", 2), ("zst", 2), ("dmp", 1), ("wol", 2),
                       ("wos", 2), ("acr", 1), ("bcx", 1), ("cse", 1), ("xl", 2),
                       ("zl", 2), ("rs", 2), ("rb2", 1), ("rbc", 1), ("yst", 4),
                       ("ynl", 4), ("wom", 4), ("ub", 4), ("x1s", 2), ("r3b", 2),
                       ("bc3", 2), ("wg", 2), ("wu", 2), ("hst", 2), ("wd", 4),
                       ("hl", 4), ("xfl", 2), ("r4b", 2), ("bc4", 2), ("out", 2), ("x1b", 2), ("x1l", 2)]:
            DS[nm] = DStream(nm, nn)

        def vi(inst):
            inst.then_inc(sV.h, 1); sV.v += 1; return sV.v

        def si(inst):
            inst.then_inc(sS.h, 1); sS.v += 1; return sS.v

        def pi(inst):
            inst.then_inc(sP.h, 1); sP.v += 1; return sP.v

        def gi(inst):
            inst.then_inc(sG.h, 1); sG.v += 1; return sG.v

        def qi(inst, sem):
            inst.then_inc(sem.h, 16); sem.v += 16; return sem.v

        gtiles = []
        for r in range(NMO):
            gtiles.append((gidx, wgT_s, r, 0, 2048))
        for r in range(NMO):
            gtiles.append((uidx, wuT_s, r, 0, 2048))
        for rd in range(NKD):
            for cb, F in ((0, 2048), (2048, 2048), (4096, 1408)):
                gtiles.append((didx, wdT_s, rd, cb, F))
        n_gt = len(gtiles)
        gst = dict(load=0, tree=0, tp=0, evac=0, store=0, tree_ready=0, tree_readyQ=None,
                   tg=None)
        g_load_v, g_tree_v, g_tp_v, g_evac_v, g_store_v = [], [], [], [], []
        wu_store_v = {}
        wg_store_v = {}
        gb = {}

        gcnt = [0]

        def galloc(stk, nsl=5, pslots=2):
            n = gcnt[0]
            gcnt[0] += 1
            gb["nsl"] = nsl
            gb["pslots"] = pslots
            gb["idx"] = stk.enter_context(nc.sbuf_tensor("g%d_idx" % n, [128, 4 * 2048], BF16))
            gb["b3"] = stk.enter_context(nc.sbuf_tensor("g%d_b3" % n, [128, 2048], U8))
            gb["b2"] = stk.enter_context(nc.sbuf_tensor("g%d_b2" % n, [128, 2048], U8))
            gb["b1"] = stk.enter_context(nc.sbuf_tensor("g%d_b1" % n, [128, 2048], U8))
            gb["s"] = stk.enter_context(nc.sbuf_tensor("g%d_s" % n, [128, 2048], BF16))
            gb["v"] = stk.enter_context(nc.sbuf_tensor("g%d_v" % n, [128, 6 * 2048], BF16))
            gb["stg"] = stk.enter_context(nc.sbuf_tensor("g%d_stg" % n, [128, nsl * 2048], BF16))
            gb["ps"] = stk.enter_context(nc.psum_tensor("g%d_ps" % n, [128, pslots * 2048], BF16))

        def g_load(g, n):
            for _ in range(n):
                i = gst["load"]
                if i >= n_gt or i >= len(g_tree_v) + 4:
                    return
                src, dst, r, cb, F = gtiles[i]
                par = i % 4
                if i >= 4:
                    g.wait_ge(sV.h, g_tree_v[i - 4])
                g_load_v.append(DS["gix"].dma(
                    g, gb["idx"][:, par * 2048:par * 2048 + F],
                    AP(src, (r * 128) * src.shape[1] + cb,
                       [[src.shape[1], 128], [1, F]])))
                gst["load"] += 1

        def _tree_prog(i):
            src, dst, r, cb, F = gtiles[i]
            par = i % 4
            res = i % 2
            P = []

            def first(v):
                wt(v, g_load_v[i])
                v.wait_ge(sV.h, gst["tree_ready"])
                if gst["tree_readyQ"]:
                    wt(v, gst["tree_readyQ"])
                if i >= 2:
                    v.wait_ge(sP.h, g_tp_v[i - 2])
                ix = gb["idx"][:, par * 2048:par * 2048 + F]
                v.wait_ge(sV.h, vi(v.tensor_single_scalar(gb["b3"][:, :F], ix, 8.0, ALU.is_ge)))
            P.append(first)
            P.append(lambda v: v.wait_ge(sV.h, vi(v.scalar_tensor_tensor(
                gb["s"][:, :F], gb["b3"][:, :F], -8.0,
                gb["idx"][:, par * 2048:par * 2048 + F], ALU.mult, ALU.add))))
            P.append(lambda v: v.wait_ge(sV.h, vi(v.tensor_single_scalar(
                gb["b2"][:, :F], gb["s"][:, :F], 4.0, ALU.is_ge))))
            P.append(lambda v: v.wait_ge(sV.h, vi(v.scalar_tensor_tensor(
                gb["s"][:, :F], gb["b2"][:, :F], -4.0, gb["s"][:, :F], ALU.mult, ALU.add))))
            P.append(lambda v: v.wait_ge(sV.h, vi(v.tensor_single_scalar(
                gb["b1"][:, :F], gb["s"][:, :F], 2.0, ALU.is_ge))))
            P.append(lambda v: v.wait_ge(sV.h, vi(v.scalar_tensor_tensor(
                gb["s"][:, :F], gb["b1"][:, :F], -2.0, gb["s"][:, :F], ALU.mult, ALU.add))))
            for g8, sl4 in ((4, (2, 3, 4, 5)), (0, (res, 3, 4, 5))):
                for jj in range(4):
                    P.append(lambda v, jj=jj, g8=g8, sl4=sl4, last=(jj == 3): (
                        vi(v.tensor_scalar(gb["v"][:, sl4[jj] * 2048:sl4[jj] * 2048 + F],
                                           gb["s"][:, :F],
                                           treec[:, g8 + jj:g8 + jj + 1],
                                           treec[:, 8 + g8 + jj:9 + g8 + jj],
                                           ALU.mult, ALU.add)),
                        v.wait_ge(sV.h, sV.v) if last else None))
                P.append(lambda v, sl4=sl4: vi(v.copy_predicated(
                    gb["v"][:, sl4[0] * 2048:sl4[0] * 2048 + F], gb["b1"][:, :F],
                    gb["v"][:, sl4[1] * 2048:sl4[1] * 2048 + F])))
                P.append(lambda v, sl4=sl4: (
                    vi(v.copy_predicated(
                        gb["v"][:, sl4[2] * 2048:sl4[2] * 2048 + F], gb["b1"][:, :F],
                        gb["v"][:, sl4[3] * 2048:sl4[3] * 2048 + F])),
                    v.wait_ge(sV.h, sV.v)))
                P.append(lambda v, sl4=sl4: (
                    vi(v.copy_predicated(
                        gb["v"][:, sl4[0] * 2048:sl4[0] * 2048 + F], gb["b2"][:, :F],
                        gb["v"][:, sl4[2] * 2048:sl4[2] * 2048 + F])),
                    v.wait_ge(sV.h, sV.v)))

            def final(v):
                vi(v.copy_predicated(gb["v"][:, res * 2048:res * 2048 + F], gb["b3"][:, :F],
                                     gb["v"][:, 2 * 2048:2 * 2048 + F]))
                v.wait_ge(sV.h, sV.v)
            P.append(final)
            return P

        def g_tree_ops(v, k):
            while k > 0:
                if gst["tg"] is None:
                    i = gst["tree"]
                    if i >= n_gt or i >= gst["load"] or i >= len(g_tp_v) + 2:
                        return
                    gst["tg"] = (_tree_prog(i), 0)
                prog, pos = gst["tg"]
                while k > 0 and pos < len(prog):
                    prog[pos](v)
                    pos += 1
                    k -= 1
                if pos == len(prog):
                    gst["tg"] = None
                    g_tree_v.append(sV.v)
                    gst["tree"] += 1
                else:
                    gst["tg"] = (prog, pos)

        def g_tree(v, n):
            g_tree_ops(v, 21 * n)

        def g_tp(t, n):
            psl = gb["pslots"]
            for _ in range(n):
                i = gst["tp"]
                if (i >= n_gt or gst["tree"] < i + 1
                        or (gst["tree"] < min(i + 2, n_gt)
                            and gst["tree"] < gst["load"])
                        or i >= gst["evac"] + psl):
                    return
                src, dst, r, cb, F = gtiles[i]
                par = i % psl
                res = i % 2
                t.wait_ge(sV.h, g_tree_v[i])
                if i >= psl:
                    t.wait_ge(sV.h, g_evac_v[i - psl])
                for j in range(F // 128):
                    mm = t.transpose(gb["ps"][:, par * 2048 + j * 128:par * 2048 + (j + 1) * 128],
                                     gb["v"][:, res * 2048 + j * 128:res * 2048 + (j + 1) * 128],
                                     ident_b[:, :])
                pi(mm)
                g_tp_v.append(sP.v)
                gst["tp"] += 1

        def g_evac(v, n):
            for _ in range(n):
                i = gst["evac"]
                if i >= n_gt or i >= gst["tp"]:
                    return
                src, dst, r, cb, F = gtiles[i]
                nsl = gb["nsl"]
                sl = i % nsl
                par = i % gb["pslots"]
                v.wait_ge(sP.h, g_tp_v[i])
                if i >= nsl:
                    wt(v, g_store_v[i - nsl])
                vi(v.tensor_copy(gb["stg"][:, sl * 2048:sl * 2048 + F],
                                 gb["ps"][:, par * 2048:par * 2048 + F]))
                v.wait_ge(sV.h, sV.v)
                g_evac_v.append(sV.v)
                gst["evac"] += 1

        def g_store(sc, n, lag=4):
            lag = min(lag, gb["nsl"] - 1)
            for _ in range(n):
                i = gst["store"]
                if i >= n_gt or i >= gst["evac"] - lag:
                    return
                src, dst, r, cb, F = gtiles[i]
                sl = i % gb["nsl"]
                sc.wait_ge(sV.h, g_evac_v[i])
                tok = DS["gst"].dma(sc, AP(dst, r * 128 * dst.shape[2] + cb,
                                           [[dst.shape[2], 128], [1, F]]),
                                    gb["stg"][:, sl * 2048:sl * 2048 + F])
                g_store_v.append(tok)
                if dst is wgT_s:
                    wg_store_v[r] = tok
                if dst is wuT_s:
                    wu_store_v[r] = tok
                gst["store"] += 1

        def g_step(blk, n):
            for _ in range(n):
                @blk.gpsimd
                def _(g):
                    g_load(g, 1 + (1 if gst["load"] < 3 else 0))

                @blk.tensor
                def _(t):
                    g_tp(t, 1)

                @blk.scalar
                def _(sc):
                    g_store(sc, 1)

                @blk.vector
                def _(v):
                    g_tree(v, 1)
                    g_evac(v, 1)

        def flush_stores(blk, names):
            @blk.sync
            def _(s):
                for nm in names:
                    DS[nm].flush(s)

        def g_drain(blk, all_tiles=False):
            guard = 0
            target = n_gt if all_tiles else None
            while (gst["store"] < (target if all_tiles else gst["load"])
                   and guard < 250):
                if all_tiles:
                    @blk.gpsimd
                    def _(g):
                        g_load(g, 2)

                @blk.tensor
                def _(t):
                    g_tp(t, 1)

                @blk.scalar
                def _(sc):
                    g_store(sc, 2, lag=0)

                @blk.vector
                def _(v):
                    g_tree(v, 1)
                    g_evac(v, 1)
                guard += 1
            assert gst["store"] == (target if all_tiles else gst["load"]), gst

        stDT = ExitStack()
        dtv = stDT.enter_context(nc.sbuf_tensor("dtv", [NH, T], F32, side="right"))
        lndt = stDT.enter_context(nc.sbuf_tensor("lndt", [NH, T], F32, side="right"))
        acum = stDT.enter_context(nc.sbuf_tensor("acum", [NH, T], F32, side="right"))
        dtdec = stDT.enter_context(nc.sbuf_tensor("dtdec", [NH, T], F32, side="right"))
        dtraw = sp1 = sp2 = None
        row_srcs = {"dtv": dtv, "acum": acum, "dtdec": dtdec, "lndt": lndt}
        for name in dump:
            if name in dump_specs:
                src = dump_specs[name]
                dump_outs[name] = nc.dram_tensor("dump_" + name, list(src.shape), src.dtype,
                                                 kind="ExternalOutput")
            elif name in row_srcs:
                dump_outs[name] = nc.dram_tensor("dump_" + name, [NH, T], F32,
                                                 kind="ExternalOutput")

        stU = ExitStack()
        uTb = stU.enter_context(nc.sbuf_tensor("uTb", [128, NKD * T], BF16))
        wobb = stU.enter_context(nc.sbuf_tensor("wobb", [128, 2 * 1024], BF16))
        wob_items = [(mb, kh) for mb in range(NKD) for kh in range(4)]
        wob_ld_v, wob_st_v = [], []
        wst = dict(ld=0, st=0)

        def wob_load(g, n):
            for _ in range(n):
                i = wst["ld"]
                if i >= len(wob_items) or i >= wst["st"] + 2:
                    return
                mb, kh = wob_items[i]
                if i >= 2:
                    wt(g, wob_st_v[i - 2])
                wob_ld_v.append(DS["wol"].dma(
                    g, wobb[:, (i % 2) * 1024:(i % 2 + 1) * 1024]
                    .rearrange("p (k c) -> p k c", k=8),
                    AP(w_out, (kh * 8 * 128) * DM + mb * 128,
                       [[DM, 128], [DM * 128, 8], [1, 128]])))
                wst["ld"] += 1

        def wob_store(sc, n):
            for _ in range(n):
                i = wst["st"]
                if i >= len(wob_items) or i >= wst["ld"]:
                    return
                mb, kh = wob_items[i]
                wt(sc, wob_ld_v[i])
                wob_st_v.append(DS["wos"].dma(
                    sc, AP(wob_s, mb * 128 * DIN + kh * 1024, [[DIN, 128], [1, 1024]]),
                    wobb[:, (i % 2) * 1024:(i % 2 + 1) * 1024]))
                wst["st"] += 1
        with ExitStack() as st0:
            utile = st0.enter_context(nc.sbuf_tensor("utile", [128, 2 * DM], F32))
            ustg = st0.enter_context(nc.sbuf_tensor("ustg", [128, 2 * 1024], F32))
            iscr = st0.enter_context(nc.sbuf_tensor("iscr", [128, 256], I32))
            psTa = st0.enter_context(nc.psum_tensor("psTa", [128, 1024], F32))
            psTb = st0.enter_context(nc.psum_tensor("psTb", [128, 1024], F32))
            galloc(st0)

            with nc.Block() as blk:
                @blk.gpsimd
                def _(g):
                    gi(g.iota(iscr[:, :128], [[1, 128]], base=0, channel_multiplier=-1))
                v_iota1 = sG.v

                @blk.vector
                def _(v):
                    v.wait_ge(sG.h, v_iota1)
                    v.wait_ge(sV.h, vi(v.tensor_single_scalar(ident_f[:, :], iscr[:, :128], 0.0, ALU.is_equal)))
                    vi(v.tensor_copy(ident_b[:, :], ident_f[:, :]))
                v_ident = sV.v

                @blk.gpsimd
                def _(g):
                    g.wait_ge(sV.h, v_ident)
                    gi(g.iota(iscr[:, :256], [[-1, 256]], base=0, channel_multiplier=1))
                    gi(g.memset(zero_b[:, :], 0.0))
                    gi(g.memset(onecol[:, :], 1.0))
                    gi(g.memset(ceps[:, :], EPS))
                    gi(g.memset(chalf[:, :], -0.5))
                    gi(g.memset(cidin[:, :], 1.0 / DIN))
                v_iota2 = sG.v

                @blk.vector
                def _(v):
                    v.wait_ge(sG.h, v_iota2)
                    vi(v.tensor_single_scalar(umask[:, 0:CHUNK], iscr[:, :256], 0.0, ALU.is_gt))
                    vi(v.tensor_single_scalar(umask[:, CHUNK:2 * CHUNK], iscr[:, :256], -128.0, ALU.is_gt))

                @blk.sync
                def _(s):
                    DS["prm"].dma(s, cwcol[:, :].rearrange("p (g j) -> p g j", g=33),
                                  AP(conv_w, 0, [[4, 128], [512, 33], [1, 4]]))
                    DS["prm"].dma(s, dtbc[:, :], dt_bias[:, :])
                    DS["prm"].dma(s, nega[:, :], a_log[:, :])
                    DS["prm"].dma(s, dcol128[:, :], AP(d_in, 0, [[0, 128], [1, NH]]))
                    DS["prm"].dma(s, lutc[:, :], AP(lut_in, 0, [[0, 128], [1, 16]]))

                @blk.gpsimd
                def _(g):
                    with nc.allow_non_contiguous_dma(reason="tiny one-time param loads"):
                        DS["prm2"].dma(g, cbcol[:, :], AP(conv_b, 0, [[1, 128], [128, 33]]))
                        DS["prm2"].dma(g, nwcol[:, :], AP(norm_w, 0, [[1, 128], [128, NKI]]))
                v_paramsG = DS["prm2"].last()
                v_params = DS["prm"].last()

                @blk.vector
                def _(v):
                    wt(v, v_params)
                    vi(v.tensor_sub(treec[:, 0:8], lutc[:, 1:16:2], lutc[:, 0:16:2]))
                    vi(v.tensor_copy(treec[:, 8:16], lutc[:, 0:16:2]))
                gst["tree_ready"] = sV.v
                gst["tree_readyQ"] = v_params

                u_load_v, half_tp_v, half_st_v, half_ev_v = [], [], [], []

                @blk.sync
                def _(s):
                    for ti in range(2):
                        u_load_v.append(DS["ul"].dma(
                            s, utile[:, (ti % 2) * DM:(ti % 2 + 1) * DM],
                            u_in[ti * 128:(ti + 1) * 128, :]))

                for ti in range(NKD):
                    for half in range(2):
                        hidx = 2 * ti + half
                        ps = psTa if hidx % 2 == 0 else psTb

                        @blk.tensor
                        def _(t, ti=ti, half=half, hidx=hidx, ps=ps):
                            t.wait_ge(sV.h, v_ident)
                            wt(t, u_load_v[ti])
                            if hidx >= 2:
                                t.wait_ge(sV.h, half_ev_v[hidx - 2][0])
                                t.wait_ge(sS.h, half_ev_v[hidx - 2][1])
                            for jl in range(8):
                                j = half * 8 + jl
                                mm = t.transpose(ps[:, jl * 128:(jl + 1) * 128],
                                                 utile[:, (ti % 2) * DM + j * 128:(ti % 2) * DM + (j + 1) * 128],
                                                 ident_f[:, :])
                            pi(mm)
                            half_tp_v.append(sP.v)

                        @blk.vector
                        def _(v, ti=ti, half=half, hidx=hidx, ps=ps):
                            v.wait_ge(sP.h, half_tp_v[hidx])
                            vi(v.tensor_copy(
                                uTb[:, :].rearrange("p (j t) -> p j t", j=NKD)[
                                    :, half * 8:half * 8 + 8, ti * 128:(ti + 1) * 128],
                                ps[:, :].rearrange("p (j q) -> p j q", j=8)))
                            half_ev_v.append([sV.v, None])

                        @blk.scalar
                        def _(sc, ti=ti, half=half, hidx=hidx, ps=ps):
                            sc.wait_ge(sP.h, half_tp_v[hidx])
                            if hidx >= 2:
                                wt(sc, half_st_v[hidx - 2])
                            si(sc.copy(ustg[:, (hidx % 2) * 1024:(hidx % 2 + 1) * 1024], ps[:, :]))
                            half_ev_v[hidx][1] = sS.v

                        @blk.sync
                        def _(s, ti=ti, half=half, hidx=hidx):
                            s.wait_ge(sS.h, half_ev_v[hidx][1])
                            half_st_v.append(DS["ust"].dma(
                                s, AP(uT_s, ti * 128 * DM + half * 1024,
                                      [[DM, 128], [1, 1024]]),
                                ustg[:, (hidx % 2) * 1024:(hidx % 2 + 1) * 1024]))
                            if half == 1 and ti + 2 < NKD:
                                s.wait_ge(sP.h, half_tp_v[2 * ti + 1])
                                u_load_v.append(DS["ul"].dma(
                                    s, utile[:, (ti % 2) * DM:(ti % 2 + 1) * DM],
                                    u_in[(ti + 2) * 128:(ti + 3) * 128, :]))

                    if ti % 3 == 2:
                        g_step(blk, 1)
                g_drain(blk)
                flush_stores(blk, ["gst", "ust", "gix"])

        if stage != "b0":
            with ExitStack() as st1:
                wcolb = st1.enter_context(nc.sbuf_tensor("wcolb", [128, 2 * 2048], BF16))
                zstg = st1.enter_context(nc.sbuf_tensor("zstg", [128, 2 * 1024], BF16))
                dtraw = st1.enter_context(nc.sbuf_tensor("dtraw", [NH, T], F32))
                sp1 = st1.enter_context(nc.sbuf_tensor("sp1", [NH, T], F32))
                sp2 = st1.enter_context(nc.sbuf_tensor("sp2", [NH, T], F32))
                psWa = st1.enter_context(nc.psum_tensor("psWa", [128, 1024], F32))
                psWb = st1.enter_context(nc.psum_tensor("psWb", [128, 1024], F32))
                galloc(st1, nsl=4)

                with nc.Block() as blk:
                    w_load_v, mm_v, ev_v, st_v = [], [], [], []

                    @blk.gpsimd
                    def _(g):
                        for m in range(2):
                            M = 128 if m < NMP - 1 else 32
                            w_load_v.append(DS["win"].dma(
                                g, wcolb[:, (m % 2) * 2048:(m % 2) * 2048 + 16 * M]
                                .rearrange("p (k c) -> p k c", k=16),
                                AP(w_in, m * 128, [[DIP, 128], [DIP * 128, 16], [1, M]])))

                    for m in range(NMP):
                        M = 128 if m < NMP - 1 else 32
                        for half in range(2):
                            hidx = 2 * m + half
                            ps = psWa if hidx % 2 == 0 else psWb

                            @blk.tensor
                            def _(t, m=m, M=M, half=half, hidx=hidx, ps=ps):
                                wt(t, w_load_v[m])
                                if hidx >= 2:
                                    t.wait_ge(sS.h, ev_v[hidx - 2])
                                for k in range(16):
                                    for tt in range(2):
                                        mm = t.matmul(
                                            ps[0:M, tt * 512:(tt + 1) * 512],
                                            wcolb[:, (m % 2) * 2048 + k * M:(m % 2) * 2048 + (k + 1) * M],
                                            uTb[:, k * T + half * 1024 + tt * 512:
                                                k * T + half * 1024 + (tt + 1) * 512],
                                            start=(k == 0), stop=(k == 15))
                                pi(mm)
                                mm_v.append(sP.v)

                            @blk.scalar
                            def _(sc, m=m, M=M, half=half, hidx=hidx, ps=ps):
                                sc.wait_ge(sP.h, mm_v[hidx])
                                if hidx >= 2 and st_v[hidx - 2] is not None:
                                    wt(sc, st_v[hidx - 2])
                                if m < 32:
                                    si(sc.copy(zstg[0:M, (hidx % 2) * 1024:(hidx % 2 + 1) * 1024],
                                               ps[0:M, :]))
                                elif m < NMP - 1:
                                    si(sc.copy(zstg[0:M, (hidx % 2) * 1024:(hidx % 2 + 1) * 1024],
                                               ps[0:M, :]))
                                else:
                                    si(sc.copy(dtraw[:, half * 1024:(half + 1) * 1024], ps[0:NH, :]))
                                ev_v.append(sS.v)

                            @blk.sync
                            def _(s, m=m, M=M, half=half, hidx=hidx):
                                if m < NMP - 1:
                                    s.wait_ge(sS.h, ev_v[hidx])
                                    st_v.append(DS["zst"].dma(
                                        s, AP(zxb_s, m * 128 * T + half * 1024,
                                              [[T, M], [1, 1024]]),
                                        zstg[0:M, (hidx % 2) * 1024:(hidx % 2 + 1) * 1024]))
                                else:
                                    st_v.append(None)

                        @blk.gpsimd
                        def _(g, m=m):
                            wob_load(g, 1)

                        @blk.scalar
                        def _(sc, m=m):
                            wob_store(sc, 1)

                        @blk.gpsimd
                        def _(g, m=m):
                            wob_load(g, 1)

                        @blk.scalar
                        def _(sc, m=m):
                            wob_store(sc, 1)

                        @blk.gpsimd
                        def _(g, m=m):
                            if m + 2 < NMP:
                                mn = m + 2
                                Mn = 128 if mn < NMP - 1 else 32
                                g.wait_ge(sP.h, mm_v[2 * m + 1])
                                w_load_v.append(DS["win"].dma(
                                    g, wcolb[:, (mn % 2) * 2048:(mn % 2) * 2048 + 16 * Mn]
                                    .rearrange("p (k c) -> p k c", k=16),
                                    AP(w_in, mn * 128, [[DIP, 128], [DIP * 128, 16], [1, Mn]])))

                        if m % 2 == 1:
                            g_step(blk, 1)

                    g_drain(blk)

                    v_dtraw = sS.v

                    @blk.vector
                    def _(v):
                        v.wait_ge(sS.h, v_dtraw)
                        wt(v, v_params)
                        vi(v.tensor_scalar(sp1[:, :], dtraw[:, :], dtbc[:, 0:1], 0.0, ALU.add, ALU.max))
                    v_sp1 = sV.v

                    @blk.scalar
                    def _(sc):
                        sc.wait_ge(sS.h, v_dtraw)
                        wt(sc, v_params)
                        sc.wait_ge(sS.h, si(sc.activation(sp2[:, :], dtraw[:, :], AF.Abs, bias=dtbc[:, 0:1])))
                        sc.wait_ge(sS.h, si(sc.activation(sp2[:, :], sp2[:, :], AF.Exp, scale=-1.0)))
                        si(sc.activation(sp2[:, :], sp2[:, :], AF.Ln, bias=1.0))
                    v_sp2 = sS.v

                    @blk.vector
                    def _(v):
                        v.wait_ge(sV.h, v_sp1)
                        v.wait_ge(sS.h, v_sp2)
                        vi(v.tensor_add(dtv[:, :], sp1[:, :], sp2[:, :]))
                    v_dtv = sV.v

                    @blk.scalar
                    def _(sc):
                        sc.wait_ge(sV.h, v_dtv)
                        si(sc.activation(lndt[:, :], dtv[:, :], AF.Ln))
                        sc.wait_ge(sS.h, si(sc.activation(nega[:, :], nega[:, :], AF.Exp)))
                    v_lndt = sS.v

                    @blk.vector
                    def _(v):
                        v.wait_ge(sS.h, v_lndt)
                        v.wait_ge(sV.h, vi(v.tensor_scalar_mul(nega[:, :], nega[:, :], -1.0)))
                        v.wait_ge(sV.h, vi(v.tensor_scalar_mul(sp1[:, :], dtv[:, :], nega[:, 0:1])))
                        for c in range(NCH):
                            sl = slice(c * CHUNK, (c + 1) * CHUNK)
                            vi(v.tensor_tensor_scan(acum[:, sl], sp1[:, sl], sp1[:, sl],
                                                    0.0, ALU.add, ALU.bypass))
                        v.wait_ge(sV.h, sV.v)
                        for c in range(NCH):
                            sl = slice(c * CHUNK, (c + 1) * CHUNK)
                            vi(v.tensor_single_scalar(sp2[:, sl], acum[:, sl],
                                                      acum[:, c * CHUNK + 255:c * CHUNK + 256],
                                                      ALU.subtract))
                    v_acum = sV.v

                    @blk.scalar
                    def _(sc):
                        sc.wait_ge(sV.h, v_acum)
                        si(sc.activation(sp2[:, :], sp2[:, :], AF.Exp, scale=-1.0))
                    v_dec = sS.v

                    @blk.vector
                    def _(v):
                        v.wait_ge(sS.h, v_dec)
                        vi(v.tensor_mul(dtdec[:, :], sp2[:, :], dtv[:, :]))
                    v_ddec = sV.v

                    @blk.sync
                    def _(s):
                        s.wait_ge(sV.h, v_ddec)
                        for name, dst in dump_outs.items():
                            if name in row_srcs:
                                DS["dmp"].dma(s, dst.ap(), row_srcs[name][:, :])

                    assert wst["st"] == len(wob_items), wst
                    flush_stores(blk, ["gst", "zst", "dmp", "gix", "win", "wol", "wos"])

        stU.close()

        if stage in ("b2", "b3", "full"):
            with ExitStack() as st2:
                bcx = st2.enter_context(nc.sbuf_tensor("bcx", [128, 260], BF16))
                bcc = st2.enter_context(nc.sbuf_tensor("bcc", [128, 256], BF16))
                cset = st2.enter_context(nc.sbuf_tensor("cset", [64, 256], BF16))
                colsb = st2.enter_context(nc.sbuf_tensor("colsb", [128, 192], F32))
                nscol = st2.enter_context(nc.sbuf_tensor("nscol", [128, 64], F32))
                btokb = st2.enter_context(nc.sbuf_tensor("btokb", [128, 128], BF16))
                bdb = st2.enter_context(nc.sbuf_tensor("bdb", [128, 256], BF16))
                cbts = st2.enter_context(nc.sbuf_tensor("cbts", [128, 2 * 256], BF16))
                a64b = st2.enter_context(nc.sbuf_tensor("a64b", [64, NH], F32))
                rsubb = st2.enter_context(nc.sbuf_tensor("rsubb", [128, 2 * 1024], F32))
                dmbb = st2.enter_context(nc.sbuf_tensor("dmbb", [128, 4 * 1024], F32))
                ebuf = st2.enter_context(nc.sbuf_tensor("ebuf", [128, 4 * 1024], BF16))
                slb = st2.enter_context(nc.sbuf_tensor("slb", [128, 4 * 1024], BF16))
                e2b = st2.enter_context(nc.sbuf_tensor("e2b", [64, 1024], BF16))
                cscb = st2.enter_context(nc.sbuf_tensor("cscb", [64, 1024], BF16))
                xbufs = st2.enter_context(nc.sbuf_tensor("xbufs", [128, 2 * 1040], BF16))
                xcs = st2.enter_context(nc.sbuf_tensor("xcs", [128, 2 * 1024], BF16))
                xtokb = st2.enter_context(nc.sbuf_tensor("xtokb", [128, 2 * 1024], BF16))
                zsb = st2.enter_context(nc.sbuf_tensor("zsb", [128, 2 * 1024], BF16))
                sqbb = st2.enter_context(nc.sbuf_tensor("sqbb", [128, 2 * 1024], BF16))
                ybuf = st2.enter_context(nc.sbuf_tensor("ybuf", [128, NH * CHUNK], BF16))
                prevf = st2.enter_context(nc.sbuf_tensor("prevf", [64, NH * 128], F32))
                prevbb = st2.enter_context(nc.sbuf_tensor("prevbb", [64, NH * 128], BF16))
                rstd1 = st2.enter_context(nc.sbuf_tensor("rstd1", [1, 256], F32))
                rbcb = st2.enter_context(nc.sbuf_tensor("rbcb", [128, 256], F32))
                psC = st2.enter_context(nc.psum_tensor("psC", [128, 512], F32))
                psY = psC
                psSt = st2.enter_context(nc.psum_tensor("psSt", [64, 512], F32))
                psX = st2.enter_context(nc.psum_tensor("psX", [128, 2 * 1024], BF16))
                psP4V = st2.enter_context(nc.psum_tensor("psP4V", [128, 448], F32))
                psP4 = psP4V[:, 0:192]
                psBt = st2.enter_context(nc.psum_tensor("psBt", [128, 128], BF16))
                galloc(st2, nsl=3, pslots=1)
                psV = psP4V[0:1, 192:448]

                with nc.Block() as blk:
                    @blk.vector
                    def _(v):
                        vi(v.memset(prevf[:, :], 0.0))
                        vi(v.memset(prevbb[:, :], 0.0))
                    prevcp = sV.v

                    prev = {"endV": sV.v, "endS": sS.v, "endP": sP.v, "prevcp": prevcp}
                    for c in range(NCH):
                        R = {}

                        @blk.tensor
                        def _(t):
                            t.wait_ge(sV.h, prev["endV"])
                            t.wait_ge(sS.h, prev["endS"])
                            DS["cse"].flush(t)

                        @blk.vector
                        def _(v):
                            v.wait_ge(sS.h, prev["endS"])
                            v.wait_ge(sP.h, prev["endP"])
                            DS["cse"].flush(v)
                            DS["rbc"].flush(v)
                            if c == 0:
                                wt(v, v_paramsG)
                                wt(v, v_params)

                        @blk.scalar
                        def _(sc):
                            sc.wait_ge(sV.h, prev["endV"])
                            sc.wait_ge(sP.h, prev["endP"])
                            DS["yst"].flush(sc)
                            if c == 0:
                                wt(sc, v_paramsG)
                                wt(sc, v_params)

                        @blk.gpsimd
                        def _(g, c=c):
                            g.wait_ge(sV.h, prev["endV"])
                            DS["rs"].flush(g)
                            R["acr"] = DS["acr"].dma(
                                g, acr_s.ap().rearrange("x (p n) -> p (x n)", p=NH),
                                acum[:, c * CHUNK:(c + 1) * CHUNK])

                        if c == 0:
                            @blk.gpsimd
                            def _(g):
                                gi(g.memset(bcx[:, 0:3], 0.0))
                                R["bcx0"] = sG.v

                        @blk.sync
                        def _(s, c=c):
                            s.wait_ge(sV.h, prev["endV"])
                            s.wait_ge(sP.h, prev["endP"])
                            s.wait_ge(sS.h, prev["endS"])
                            if c == 0:
                                DS["zst"].flush(s)
                                s.wait_ge(sG.h, R["bcx0"])
                                R["bcx"] = DS["bcx"].dma(
                                    s, bcx[:, 3:259],
                                    AP(zxb_s, (DIN + 4096) * T, [[T, 128], [1, 256]]))
                            else:
                                R["bcx"] = DS["bcx"].dma(
                                    s, bcx[:, 0:259],
                                    AP(zxb_s, (DIN + 4096) * T + c * CHUNK - 3,
                                       [[T, 128], [1, 259]]))

                        @blk.vector
                        def _(v, c=c):
                            wt(v, R["bcx"])
                            if c == 0:
                                v.wait_ge(sG.h, R["bcx0"])
                            gg = 32
                            v.wait_ge(sV.h, vi(v.tensor_scalar_mul(bcc[:, :], bcx[:, 0:256],
                                                                   cwcol[:, gg * 4:gg * 4 + 1])))
                            for j in range(1, 4):
                                v.wait_ge(sV.h, vi(v.scalar_tensor_tensor(
                                    bcc[:, :], bcx[:, j:j + 256], cwcol[:, gg * 4 + j:gg * 4 + j + 1],
                                    bcc[:, :], ALU.mult, ALU.add)))
                            R["bccv"] = sV.v

                        @blk.scalar
                        def _(sc, c=c):
                            sc.wait_ge(sV.h, R["bccv"])
                            si(sc.activation(bcc[:, :], bcc[:, :], AF.Silu, bias=cbcol[:, 32:33]))
                            R["bcc"] = sS.v

                        @blk.gpsimd
                        def _(g, c=c):
                            g.wait_ge(sS.h, R["bcc"])
                            R["cset"] = DS["cse"].dma(g, cset[:, :], bcc[64:128, :])

                        @blk.tensor
                        def _(t, c=c):
                            t.wait_ge(sV.h, prev["endV"])
                            for sb in range(2):
                                sl = slice(c * CHUNK + sb * 128, c * CHUNK + (sb + 1) * 128)
                                mm = t.transpose(psP4[:, sb * 32:(sb + 1) * 32], acum[:, sl], ident_f[0:32, 0:32])
                                mm = t.transpose(psP4[:, 64 + sb * 32:64 + (sb + 1) * 32], lndt[:, sl], ident_f[0:32, 0:32])
                                mm = t.transpose(psP4[:, 128 + sb * 32:128 + (sb + 1) * 32], dtdec[:, sl], ident_f[0:32, 0:32])
                            pi(mm)
                            R["p4"] = sP.v

                        @blk.vector
                        def _(v, c=c):
                            v.wait_ge(sP.h, R["p4"])
                            v.wait_ge(sV.h, vi(v.tensor_copy(colsb[:, :], psP4[:, :])))
                            vi(v.tensor_sub(nscol[:, :], colsb[:, 64:128], colsb[:, 0:64]))
                            R["cols"] = sV.v

                        @blk.tensor
                        def _(t, c=c):
                            t.wait_ge(sS.h, R["bcc"])
                            for sb in range(2):
                                mm = t.transpose(psBt[:, sb * 64:(sb + 1) * 64],
                                                 bcc[0:64, sb * 128:(sb + 1) * 128], ident_b[0:64, 0:64])
                            pi(mm)
                            R["btokp"] = sP.v

                        @blk.vector
                        def _(v, c=c):
                            v.wait_ge(sP.h, R["btokp"])
                            vi(v.tensor_copy(btokb[:, :], psBt[:, :]))
                            R["btok"] = sV.v
                            g_tree_ops(v, 2)

                        @blk.tensor
                        def _(t, c=c):
                            wt(t, R["cset"])
                            for sb in range(2):
                                mm = t.matmul(psC[:, sb * 256:(sb + 1) * 256],
                                              bcc[0:64, sb * 128:(sb + 1) * 128],
                                              cset[:, :], start=True, stop=True)
                            pi(mm)
                            R["cbtp"] = sP.v

                        @blk.scalar
                        def _(sc, c=c):
                            sc.wait_ge(sP.h, R["cbtp"])
                            si(sc.copy(cbts[:, :], psC[:, :]))
                            R["cbte"] = sS.v

                        @blk.vector
                        def _(v, c=c):
                            v.wait_ge(sS.h, R["cbte"])
                            for sb in range(2):
                                vi(v.copy_predicated(cbts[:, sb * 256:(sb + 1) * 256],
                                                     umask[:, sb * 256:(sb + 1) * 256], zero_b[:, :]))
                            v.wait_ge(sV.h, sV.v)
                            R["cbt"] = sV.v
                            g_tree_ops(v, 2)

                        SR = []
                        for su in range(8):
                            Rs = {}
                            par = su % 2

                            if c == 0:
                                @blk.gpsimd
                                def _(g, su=su, par=par):
                                    if su >= 2:
                                        g.wait_ge(sV.h, SR[su - 2]["free"])
                                    gi(g.memset(xbufs[:, par * 1040:par * 1040 + 1040]
                                                .rearrange("p (g w) -> p g w", g=4)[:, :, 0:3], 0.0))
                                    Rs["xz"] = sG.v

                            @blk.sync
                            def _(s, c=c, su=su, par=par):
                                if su >= 2:
                                    s.wait_ge(sV.h, SR[su - 2]["free"])
                                    s.wait_ge(sS.h, SR[su - 2]["a64"])
                                if c == 0:
                                    s.wait_ge(sG.h, Rs["xz"])
                                    Rs["xload"] = DS["xl"].dma(
                                        s, xbufs[:, par * 1040:par * 1040 + 1040]
                                        .rearrange("p (g w) -> p g w", g=4)[:, :, 3:259],
                                        AP(zxb_s, (DIN + su * 4 * 128) * T,
                                           [[T, 128], [128 * T, 4], [1, 256]]))
                                else:
                                    Rs["xload"] = DS["xl"].dma(
                                        s, xbufs[:, par * 1040:par * 1040 + 1040]
                                        .rearrange("p (g w) -> p g w", g=4)[:, :, 0:259],
                                        AP(zxb_s, (DIN + su * 4 * 128) * T + c * CHUNK - 3,
                                           [[T, 128], [128 * T, 4], [1, 259]]))
                                Rs["zload"] = DS["zl"].dma(
                                    s, zsb[:, par * 1024:(par + 1) * 1024]
                                    .rearrange("p (g w) -> p g w", g=4),
                                    AP(zxb_s, (su * 4 * 128) * T + c * CHUNK,
                                       [[T, 128], [128 * T, 4], [1, 256]]))
                                wt(s, R["acr"])
                                Rs["rload"] = DS["rs"].dma(
                                    s, rsubb[:, par * 1024:(par + 1) * 1024],
                                    AP(acr_s, su * 1024, [[0, 128], [1, 1024]]))

                            @blk.vector
                            def _(v, c=c, su=su, par=par):
                                wt(v, Rs["xload"])
                                if su >= 2:
                                    v.wait_ge(sP.h, SR[su - 2]["xtokp"])
                                for gl in range(4):
                                    gg = su * 4 + gl
                                    xb = xbufs[:, par * 1040 + gl * 260:par * 1040 + gl * 260 + 260]
                                    xo = xcs[:, par * 1024 + gl * 256:par * 1024 + (gl + 1) * 256]
                                    v.wait_ge(sV.h, vi(v.tensor_scalar_mul(xo, xb[:, 0:256],
                                                                           cwcol[:, gg * 4:gg * 4 + 1])))
                                    for j in range(1, 4):
                                        v.wait_ge(sV.h, vi(v.scalar_tensor_tensor(
                                            xo, xb[:, j:j + 256], cwcol[:, gg * 4 + j:gg * 4 + j + 1],
                                            xo, ALU.mult, ALU.add)))
                                Rs["xconv"] = sV.v
                                g_tree_ops(v, 3)

                            @blk.scalar
                            def _(sc, su=su, par=par):
                                sc.wait_ge(sV.h, Rs["xconv"])
                                for gl in range(4):
                                    gg = su * 4 + gl
                                    xo = xcs[:, par * 1024 + gl * 256:par * 1024 + (gl + 1) * 256]
                                    si(sc.activation(xo, xo, AF.Silu, bias=cbcol[:, gg:gg + 1]))
                                Rs["xsilu"] = sS.v
                                wt(sc, Rs["zload"])
                                if su >= 2:
                                    sc.wait_ge(sV.h, SR[su - 2]["gate"])
                                si(sc.activation(zsb[:, par * 1024:(par + 1) * 1024],
                                                 zsb[:, par * 1024:(par + 1) * 1024], AF.Silu))
                                Rs["zsilu"] = sS.v

                            for sb in range(2):
                                dpar = (su % 2) * 2 + sb

                                @blk.vector
                                def _(v, c=c, su=su, sb=sb, par=par, dpar=dpar):
                                    wt(v, Rs["rload"])
                                    v.wait_ge(sV.h, R["cols"])
                                    if su >= 2:
                                        v.wait_ge(sS.h, SR[su - 2]["E%d" % sb])
                                    for hl in range(4):
                                        h = su * 4 + hl
                                        vi(v.tensor_scalar(
                                            dmbb[:, dpar * 1024 + hl * 256:dpar * 1024 + (hl + 1) * 256],
                                            rsubb[:, par * 1024 + hl * 256:par * 1024 + (hl + 1) * 256],
                                            nscol[:, sb * 32 + h:sb * 32 + h + 1], 20.0,
                                            ALU.add, ALU.min))
                                    v.wait_ge(sV.h, sV.v)
                                    Rs["dm%d" % sb] = sV.v

                            @blk.scalar
                            def _(sc, su=su, par=par):
                                for sb in range(2):
                                    dpar = (su % 2) * 2 + sb
                                    sc.wait_ge(sV.h, Rs["dm%d" % sb])
                                    if su >= 2:
                                        sc.wait_ge(sV.h, SR[su - 2]["SL%d" % sb])
                                    si(sc.activation(ebuf[:, dpar * 1024:(dpar + 1) * 1024],
                                                     dmbb[:, dpar * 1024:(dpar + 1) * 1024], AF.Exp))
                                    Rs["E%d" % sb] = sS.v
                                wt(sc, Rs["rload"])
                                if su >= 1:
                                    sc.wait_ge(sV.h, SR[su - 1]["csc"])
                                si(sc.activation(e2b[:, :],
                                                 rsubb[0:64, par * 1024:(par + 1) * 1024], AF.Exp))
                                Rs["E2"] = sS.v
                                si(sc.activation(a64b[:, su * 4:(su + 1) * 4],
                                                 AP(rsubb, par * 1024 + 255, [[2048, 64], [256, 4]]),
                                                 AF.Exp))
                                Rs["a64"] = sS.v

                            for sb in range(2):
                                dpar = (su % 2) * 2 + sb

                                @blk.vector
                                def _(v, su=su, sb=sb, par=par, dpar=dpar):
                                    v.wait_ge(sS.h, Rs["E%d" % sb])
                                    v.wait_ge(sV.h, R["cbt"])
                                    if su >= 2:
                                        v.wait_ge(sP.h, SR[su - 2]["ymm"])
                                    vi(v.tensor_tensor(
                                        slb[:, (sb * 2 + par) * 1024:(sb * 2 + par + 1) * 1024],
                                        ebuf[:, dpar * 1024:(dpar + 1) * 1024],
                                        AP(cbts.handle if hasattr(cbts, "handle") else cbts,
                                           sb * 256, [[512, 128], [0, 4], [1, 256]]),
                                        ALU.mult))
                                    v.wait_ge(sV.h, sV.v)
                                    Rs["SL%d" % sb] = sV.v
                                    g_tree_ops(v, 2)

                            @blk.tensor
                            def _(t, su=su, par=par):
                                t.wait_ge(sS.h, Rs["xsilu"])
                                if su >= 2:
                                    t.wait_ge(sS.h, SR[su - 2]["xtoke"])
                                for gl in range(4):
                                    for sb in range(2):
                                        mm = t.transpose(
                                            psX[:, par * 1024 + (gl * 2 + sb) * 128:par * 1024 + (gl * 2 + sb + 1) * 128],
                                            xcs[:, par * 1024 + gl * 256 + sb * 128:par * 1024 + gl * 256 + (sb + 1) * 128],
                                            ident_b[:, :])
                                pi(mm)
                                Rs["xtokp"] = sP.v

                            @blk.scalar
                            def _(sc, su=su, par=par):
                                sc.wait_ge(sP.h, Rs["xtokp"])
                                si(sc.copy(xtokb[:, par * 1024:(par + 1) * 1024],
                                           psX[:, par * 1024:(par + 1) * 1024]))
                                Rs["xtoke"] = sS.v

                            @blk.vector
                            def _(v, su=su, par=par):
                                v.wait_ge(sS.h, Rs["E2"])
                                wt(v, R["cset"])
                                if su >= 1:
                                    v.wait_ge(sP.h, SR[su - 1]["ymm"])
                                vi(v.tensor_tensor(
                                    cscb[:, :],
                                    e2b[:, :],
                                    AP(cset, 0, [[256, 64], [0, 4], [1, 256]]),
                                    ALU.mult))
                                v.wait_ge(sV.h, sV.v)
                                Rs["csc"] = sV.v
                                g_tree_ops(v, 3)

                            for hl in range(4):
                                h = su * 4 + hl

                                @blk.vector
                                def _(v, su=su, hl=hl, h=h, par=par):
                                    v.wait_ge(sV.h, R["btok"])
                                    if su >= 1 and hl == 0:
                                        v.wait_ge(sV.h, SR[su - 1]["upd"])
                                    if hl >= 2:
                                        v.wait_ge(sP.h, Rs["stmm%d" % (hl - 2)])
                                    for sb in range(2):
                                        vi(v.tensor_scalar_mul(
                                            bdb[:, (hl % 2) * 128 + sb * 64:(hl % 2) * 128 + (sb + 1) * 64],
                                            btokb[:, sb * 64:(sb + 1) * 64],
                                            colsb[:, 128 + sb * 32 + h:128 + sb * 32 + h + 1]))
                                    v.wait_ge(sV.h, sV.v)
                                    Rs["bd%d" % hl] = sV.v

                                @blk.tensor
                                def _(t, su=su, hl=hl, h=h, par=par):
                                    t.wait_ge(sV.h, Rs["bd%d" % hl])
                                    t.wait_ge(sS.h, Rs["xtoke"])
                                    if su >= 1 and hl == 0:
                                        t.wait_ge(sV.h, SR[su - 1]["upd"])
                                    for sb in range(2):
                                        mm = t.matmul(psSt[:, hl * 128:(hl + 1) * 128],
                                                      bdb[:, (hl % 2) * 128 + sb * 64:(hl % 2) * 128 + (sb + 1) * 64],
                                                      xtokb[:, par * 1024 + (hl * 2 + sb) * 128:par * 1024 + (hl * 2 + sb + 1) * 128],
                                                      start=(sb == 0), stop=(sb == 1))
                                    pi(mm)
                                    Rs["stmm%d" % hl] = sP.v

                            for hl in range(4):
                                h = su * 4 + hl

                                @blk.vector
                                def _(v, su=su, hl=hl, h=h):
                                    v.wait_ge(sP.h, Rs["stmm3"])
                                    v.wait_ge(sS.h, Rs["a64"])
                                    vi(v.scalar_tensor_tensor(
                                        prevf[:, h * 128:(h + 1) * 128],
                                        prevf[:, h * 128:(h + 1) * 128],
                                        a64b[:, h:h + 1],
                                        psSt[:, hl * 128:(hl + 1) * 128], ALU.mult, ALU.add))
                                    v.wait_ge(sV.h, sV.v)
                                    if hl == 3:
                                        Rs["upd"] = sV.v
                                    if hl % 2 == 1:
                                        g_tree_ops(v, 1)

                            for pr in range(2):
                                for hl2 in range(2):
                                    hl = pr * 2 + hl2
                                    h = su * 4 + hl

                                    @blk.tensor
                                    def _(t, su=su, pr=pr, hl=hl, h=h, par=par, c=c):
                                        t.wait_ge(sV.h, Rs["SL0"])
                                        t.wait_ge(sV.h, Rs["SL1"])
                                        t.wait_ge(sV.h, Rs["csc"])
                                        t.wait_ge(sV.h, prev["prevcp"])
                                        if pr == 1 and "yev0" in Rs:
                                            pass
                                        if hl == 0 or hl == 2:
                                            if pr == 1:
                                                t.wait_ge(sS.h, Rs["yev0"])
                                            elif su >= 1:
                                                t.wait_ge(sS.h, SR[su - 1]["yev1"])
                                        mm = t.matmul(psY[:, hl2 * 256:(hl2 + 1) * 256],
                                                      xtokb[:, par * 1024 + (hl * 2) * 128:par * 1024 + (hl * 2 + 1) * 128],
                                                      slb[:, (0 * 2 + par) * 1024 + hl * 256:(0 * 2 + par) * 1024 + (hl + 1) * 256],
                                                      start=True, stop=False)
                                        mm = t.matmul(psY[:, hl2 * 256:(hl2 + 1) * 256],
                                                      xtokb[:, par * 1024 + (hl * 2 + 1) * 128:par * 1024 + (hl * 2 + 2) * 128],
                                                      slb[:, (1 * 2 + par) * 1024 + hl * 256:(1 * 2 + par) * 1024 + (hl + 1) * 256],
                                                      start=False, stop=False)
                                        mm = t.matmul(psY[:, hl2 * 256:(hl2 + 1) * 256],
                                                      prevbb[:, h * 128:(h + 1) * 128],
                                                      cscb[:, hl * 256:(hl + 1) * 256],
                                                      start=False, stop=True)
                                        pi(mm)
                                        Rs["ymm%d" % hl] = sP.v

                                @blk.scalar
                                def _(sc, su=su, pr=pr):
                                    sc.wait_ge(sP.h, Rs["ymm%d" % (pr * 2 + 1)])
                                    si(sc.copy(ybuf[:, (su * 4 + pr * 2) * 256:(su * 4 + pr * 2 + 2) * 256],
                                               psY[:, :]))
                                    Rs["yev%d" % pr] = sS.v

                                for hl2 in range(2):
                                    hl = pr * 2 + hl2
                                    h = su * 4 + hl

                                    @blk.vector
                                    def _(v, su=su, hl=hl, h=h, par=par, pr=pr):
                                        v.wait_ge(sS.h, Rs["yev%d" % pr])
                                        vi(v.scalar_tensor_tensor(
                                            ybuf[:, h * 256:(h + 1) * 256],
                                            xcs[:, par * 1024 + hl * 256:par * 1024 + (hl + 1) * 256],
                                            dcol128[:, h:h + 1],
                                            ybuf[:, h * 256:(h + 1) * 256], ALU.mult, ALU.add))
                                        v.wait_ge(sV.h, sV.v)
                                        Rs["dsk%d" % hl] = sV.v

                            Rs["ymm"] = Rs["ymm3"]

                            @blk.vector
                            def _(v, su=su, par=par):
                                v.wait_ge(sS.h, Rs["zsilu"])
                                for hl in range(4):
                                    v.wait_ge(sV.h, Rs["dsk%d" % hl])
                                vi(v.tensor_mul(ybuf[:, su * 1024:(su + 1) * 1024],
                                                ybuf[:, su * 1024:(su + 1) * 1024],
                                                zsb[:, par * 1024:(par + 1) * 1024]))
                                v.wait_ge(sV.h, sV.v)
                                Rs["gate"] = sV.v
                                Rs["free"] = sV.v
                                if su >= 2:
                                    v.wait_ge(sP.h, SR[su - 2]["vmm"])
                                vi(v.tensor_mul(sqbb[:, (su % 2) * 1024:(su % 2 + 1) * 1024],
                                                ybuf[:, su * 1024:(su + 1) * 1024],
                                                ybuf[:, su * 1024:(su + 1) * 1024]))
                                v.wait_ge(sV.h, sV.v)
                                Rs["sq"] = sV.v
                                g_tree_ops(v, 3)

                            @blk.tensor
                            def _(t, su=su, par=par, c=c):
                                t.wait_ge(sV.h, Rs["sq"])
                                for hl in range(4):
                                    mm = t.matmul(psV[:, :], onecol[:, :],
                                                  sqbb[:, (su % 2) * 1024 + hl * 256:
                                                       (su % 2) * 1024 + (hl + 1) * 256],
                                                  start=(su == 0 and hl == 0),
                                                  stop=(su == 7 and hl == 3))
                                pi(mm)
                                Rs["vmm"] = sP.v

                            SR.append(Rs)

                            @blk.gpsimd
                            def _(g):
                                g_load(g, 1)

                            @blk.tensor
                            def _(t):
                                g_tp(t, 1)

                            @blk.scalar
                            def _(sc):
                                g_store(sc, 1)

                            @blk.vector
                            def _(v):
                                g_evac(v, 1)

                        @blk.scalar
                        def _(sc, c=c):
                            sc.wait_ge(sP.h, SR[7]["vmm"])
                            sc.wait_ge(sS.h, si(sc.activation(rstd1[:, :], psV[:, :], AF.Ln,
                                                              bias=ceps[0:1, :], scale=cidin[0:1, :])))
                            si(sc.activation(rstd1[:, :], rstd1[:, :], AF.Exp, scale=chalf[0:1, :]))
                            R["rstd"] = sS.v

                        @blk.gpsimd
                        def _(g, c=c):
                            g.wait_ge(sS.h, R["rstd"])
                            R["rbounce"] = DS["rb2"].dma(g, row_s[0:1, 0:256], rstd1[:, :])

                        @blk.sync
                        def _(s, c=c):
                            wt(s, R["rbounce"])
                            R["rbc"] = DS["rbc"].dma(s, rbcb[:, :], AP(row_s, 0, [[0, 128], [1, 256]]))

                        @blk.vector
                        def _(v, c=c):
                            wt(v, R["rbc"])
                            vi(v.tensor_tensor(ybuf[:, :], ybuf[:, :],
                                               AP(rbcb, 0, [[256, 128], [0, 32], [1, 256]]),
                                               ALU.mult))
                            v.wait_ge(sV.h, sV.v)
                            for gg in range(NKI):
                                vi(v.tensor_scalar_mul(ybuf[:, gg * 256:(gg + 1) * 256],
                                                       ybuf[:, gg * 256:(gg + 1) * 256],
                                                       nwcol[:, gg:gg + 1]))
                            v.wait_ge(sV.h, sV.v)
                            R["ynorm"] = sV.v

                        @blk.vector
                        def _(v, c=c):
                            v.wait_ge(sP.h, SR[7]["ymm3"])
                            for q in range(4):
                                vi(v.tensor_copy(prevbb[:, q * 1024:(q + 1) * 1024],
                                                 prevf[:, q * 1024:(q + 1) * 1024]))
                            v.wait_ge(sV.h, sV.v)
                            prev["prevcp"] = sV.v

                        @blk.sync
                        def _(s, c=c):
                            s.wait_ge(sV.h, R["ynorm"])
                            for q in range(4):
                                DS["yst"].dma(
                                    s, AP(ynorm_s, (q * 8 * 128) * T + c * CHUNK,
                                          [[T, 128], [128 * T, 8], [1, 256]]),
                                    ybuf[:, q * 2048:(q + 1) * 2048]
                                    .rearrange("p (g w) -> p g w", g=8))

                        prev["endV"] = sV.v
                        prev["endS"] = sS.v
                        prev["endP"] = sP.v

                    g_drain(blk)
                    flush_stores(blk, ["yst", "rb2", "acr", "cse",
                                       "bcx", "xl", "zl", "rs", "rbc", "gst", "gix"])

        stDT.close()
        stX = ExitStack()
        x1T = stX.enter_context(nc.sbuf_tensor("x1T", [128, NKD * T], BF16))

        if stage in ("b3", "full"):
            with ExitStack() as st3:
                ynsl = st3.enter_context(nc.sbuf_tensor("ynsl", [128, NKI * 512], BF16))
                rsl = st3.enter_context(nc.sbuf_tensor("rsl", [128, NKD * 512], F32))
                wobm = st3.enter_context(nc.sbuf_tensor("wobm", [128, 4 * 4096], BF16))
                ubuf = st3.enter_context(nc.sbuf_tensor("ubuf", [128, 2 * 512], F32))
                rbsl = st3.enter_context(nc.sbuf_tensor("rbsl", [128, 2 * 512], BF16))
                sqsl = st3.enter_context(nc.sbuf_tensor("sqsl", [128, 2 * 512], BF16))
                rowsb = st3.enter_context(nc.sbuf_tensor("rowsb", [1, 4 * 512], F32))
                rnb = st3.enter_context(nc.sbuf_tensor("rnb", [128, 512], F32))
                rsb = st3.enter_context(nc.sbuf_tensor("rsb", [128, 512], F32))
                tmpn = st3.enter_context(nc.sbuf_tensor("tmpn", [128, 512], F32))
                x1stg = st3.enter_context(nc.sbuf_tensor("x1stg", [128, 2 * 512], F32))
                psO = st3.enter_context(nc.psum_tensor("psO", [128, 4 * 512], F32))
                psS3 = st3.enter_context(nc.psum_tensor("psS3", [1, 1024], F32))

                with nc.Block() as blk:
                    @blk.sync
                    def _(s):
                        DS["yst"].flush(s)
                        DS["ust"].flush(s)
                        DS["wos"].flush(s)

                    x1st_v = []
                    for tp in range(4):
                        RTP = RT_prev if tp else None
                        RT = {}

                        @blk.sync
                        def _(s, tp=tp):
                            if tp >= 1:
                                s.wait_ge(sP.h, RT_prev["mm%d" % (NKD - 1)])
                            toks = []
                            for q in range(4):
                                toks.append(DS["ynl"].dma(
                                    s, ynsl[:, q * 4096:(q + 1) * 4096]
                                    .rearrange("p (g w) -> p g w", g=8),
                                    AP(ynorm_s, (q * 8 * 128) * T + tp * 512,
                                       [[T, 128], [128 * T, 8], [1, 512]])))
                            RT["yn"] = toks

                        mev = []
                        for m in range(NKD):
                            mpar = m % 2

                            @blk.sync
                            def _(s, tp=tp, m=m, mpar=mpar):
                                if m >= 4:
                                    s.wait_ge(sP.h, mev[m - 4][0])
                                elif tp >= 1:
                                    s.wait_ge(sP.h, RTP["mm%d" % (NKD - 4 + m)])
                                if m >= 2:
                                    s.wait_ge(sV.h, mev[m - 2][1])
                                elif tp >= 1:
                                    s.wait_ge(sV.h, RTP["rb%d" % (NKD - 2 + m)])
                                RT["w%d" % m] = DS["wom"].dma(
                                    s, wobm[:, (m % 4) * 4096:(m % 4 + 1) * 4096],
                                    AP(wob_s, m * 128 * DIN, [[DIN, 128], [1, 4096]]))
                                utoks = []
                                for ut in range(4):
                                    utoks.append(DS["ub"].dma(
                                        s, ubuf[:, mpar * 512 + ut * 128:mpar * 512 + (ut + 1) * 128],
                                        AP(uT_s, (4 * tp + ut) * 128 * DM + m * 128,
                                           [[DM, 128], [1, 128]])))
                                RT["u%d" % m] = utoks

                            @blk.tensor
                            def _(t, tp=tp, m=m, mpar=mpar):
                                wt(t, RT["w%d" % m])
                                for tok in RT["yn"]:
                                    wt(t, tok)
                                if m >= 4:
                                    t.wait_ge(sV.h, mev[m - 4][1])
                                elif tp >= 1:
                                    t.wait_ge(sV.h, RTP["rb%d" % (NKD - 4 + m)])
                                for k in range(NKI):
                                    mm = t.matmul(psO[:, (m % 4) * 512:(m % 4 + 1) * 512],
                                                  wobm[:, (m % 4) * 4096 + k * 128:(m % 4) * 4096 + (k + 1) * 128],
                                                  ynsl[:, k * 512:(k + 1) * 512],
                                                  start=(k == 0), stop=(k == NKI - 1))
                                pi(mm)
                                RT["mm%d" % m] = sP.v

                            @blk.vector
                            def _(v, tp=tp, m=m, mpar=mpar):
                                v.wait_ge(sP.h, RT["mm%d" % m])
                                for tok in RT["u%d" % m]:
                                    wt(v, tok)
                                if m >= 2:
                                    v.wait_ge(sP.h, RT["st%d" % (m - 2)])
                                elif tp >= 1:
                                    v.wait_ge(sP.h, RTP["st%d" % (NKD - 2 + m)])
                                v.wait_ge(sV.h, vi(v.tensor_add(rsl[:, m * 512:(m + 1) * 512],
                                                                psO[:, (m % 4) * 512:(m % 4 + 1) * 512],
                                                                ubuf[:, mpar * 512:(mpar + 1) * 512])))
                                vi(v.tensor_copy(rbsl[:, mpar * 512:(mpar + 1) * 512],
                                                 rsl[:, m * 512:(m + 1) * 512]))
                                v.wait_ge(sV.h, sV.v)
                                RT["rb%d" % m] = sV.v

                            @blk.scalar
                            def _(sc, tp=tp, m=m, mpar=mpar):
                                sc.wait_ge(sV.h, RT["rb%d" % m])
                                si(sc.activation(sqsl[:, mpar * 512:(mpar + 1) * 512],
                                                 rbsl[:, mpar * 512:(mpar + 1) * 512], AF.Square))
                                RT["sq%d" % m] = sS.v

                            @blk.tensor
                            def _(t, tp=tp, m=m, mpar=mpar):
                                t.wait_ge(sS.h, RT["sq%d" % m])
                                if m == 0 and tp >= 1:
                                    t.wait_ge(sV.h, RT_prev["var"])
                                mm = t.matmul(psS3[:, 0:512], onecol[:, :],
                                              rbsl[:, mpar * 512:(mpar + 1) * 512],
                                              start=(m == 0), stop=(m == NKD - 1))
                                mm = t.matmul(psS3[:, 512:1024], onecol[:, :],
                                              sqsl[:, mpar * 512:(mpar + 1) * 512],
                                              start=(m == 0), stop=(m == NKD - 1))
                                pi(mm)
                                mev.append((sP.v, RT["rb%d" % m]))
                                RT["st%d" % m] = sP.v

                            RT_prev = RT

                        @blk.vector
                        def _(v, tp=tp):
                            v.wait_ge(sP.h, RT["st%d" % (NKD - 1)])
                            v.wait_ge(sV.h, vi(v.tensor_scalar_mul(rowsb[:, 0:512], psS3[:, 0:512], -1.0 / DM)))
                            v.wait_ge(sV.h, vi(v.tensor_scalar_mul(rowsb[:, 512:1024], psS3[:, 512:1024], 1.0 / DM)))
                            v.wait_ge(sV.h, vi(v.tensor_mul(rowsb[:, 1024:1536], rowsb[:, 0:512], rowsb[:, 0:512])))
                            v.wait_ge(sV.h, vi(v.tensor_sub(rowsb[:, 1024:1536], rowsb[:, 512:1024], rowsb[:, 1024:1536])))
                            RT["var"] = sV.v

                        @blk.scalar
                        def _(sc, tp=tp):
                            sc.wait_ge(sV.h, RT["var"])
                            sc.wait_ge(sS.h, si(sc.activation(rowsb[:, 1536:2048], rowsb[:, 1024:1536],
                                                              AF.Ln, bias=ceps[0:1, :])))
                            si(sc.activation(rowsb[:, 1536:2048], rowsb[:, 1536:2048], AF.Exp, scale=chalf[0:1, :]))
                            RT["rstd"] = sS.v

                        @blk.gpsimd
                        def _(g, tp=tp):
                            g.wait_ge(sS.h, RT["rstd"])
                            t1 = DS["r3b"].dma(g, row_s[1:2, 0:512], rowsb[:, 0:512])
                            t2 = DS["r3b"].dma(g, row_s[2:3, 0:512], rowsb[:, 1536:2048])
                            RT["rowb"] = [t1, t2]

                        @blk.sync
                        def _(s, tp=tp):
                            for tok in RT["rowb"]:
                                wt(s, tok)
                            t1 = DS["bc3"].dma(s, rnb[:, :], AP(row_s, 2048, [[0, 128], [1, 512]]))
                            t2 = DS["bc3"].dma(s, rsb[:, :], AP(row_s, 4096, [[0, 128], [1, 512]]))
                            RT["bc"] = [t1, t2]

                        for m in range(NKD):
                            mpar = m % 2

                            @blk.vector
                            def _(v, tp=tp, m=m, mpar=mpar):
                                for tok in RT["bc"]:
                                    wt(v, tok)
                                v.wait_ge(sV.h, vi(v.tensor_add(tmpn[:, :], rsl[:, m * 512:(m + 1) * 512], rnb[:, :])))
                                gidx_st = tp * NKD + m - 2
                                if gidx_st >= 0:
                                    wt(v, x1st_v[gidx_st])
                                v.wait_ge(sV.h, vi(v.tensor_mul(x1stg[:, mpar * 512:(mpar + 1) * 512],
                                                                tmpn[:, :], rsb[:, :])))
                                vi(v.tensor_copy(x1T[:, m * T + tp * 512:m * T + (tp + 1) * 512],
                                                 x1stg[:, mpar * 512:(mpar + 1) * 512]))
                                v.wait_ge(sV.h, sV.v)
                                RT["x1%d" % m] = sV.v

                            @blk.sync
                            def _(s, tp=tp, m=m, mpar=mpar):
                                s.wait_ge(sV.h, RT["x1%d" % m])
                                x1st_v.append(DS["x1s"].dma(
                                    s, AP(x1f_s, m * 128 * T + tp * 512, [[T, 128], [1, 512]]),
                                    x1stg[:, mpar * 512:(mpar + 1) * 512]))

                        RT["x1done"] = RT["x1%d" % (NKD - 1)]
                        RT_prev = RT
                    flush_stores(blk, ["x1s", "r3b", "ynl", "wom", "ub", "bc3"])

        if stage == "full":
            with ExitStack() as st4:
                wgb = st4.enter_context(nc.sbuf_tensor("wgb", [128, 2 * 2048], BF16))
                wub = st4.enter_context(nc.sbuf_tensor("wub", [128, 2 * 2048], BF16))
                sgb = st4.enter_context(nc.sbuf_tensor("sgb", [128, 2 * 512], BF16))
                hstg = st4.enter_context(nc.sbuf_tensor("hstg", [128, 2 * 512], BF16))
                psG = st4.enter_context(nc.psum_tensor("psG", [128, 2 * 512], F32))
                psU = st4.enter_context(nc.psum_tensor("psU", [128, 2 * 512], F32))
                galloc(st4)

                with nc.Block() as blk:
                    wload = []
                    hmul_v = {}
                    hst_v = {}
                    for m in range(NMO):
                        @blk.sync
                        def _(s, m=m):
                            if m >= 2:
                                s.wait_ge(sP.h, mm_last[m - 2])
                            wt(s, wg_store_v[m])
                            wt(s, wu_store_v[m])
                            t1 = DS["wg"].dma(s, wgb[:, (m % 2) * 2048:(m % 2 + 1) * 2048],
                                              AP(wgT_s, m * 128 * DM, [[DM, 128], [1, 2048]]))
                            t2 = DS["wu"].dma(s, wub[:, (m % 2) * 2048:(m % 2 + 1) * 2048],
                                              AP(wuT_s, m * 128 * DM, [[DM, 128], [1, 2048]]))
                            wload.append([t1, t2])

                        mm_last = mm_last if m else {}
                        for tt in range(4):
                            tpar = tt % 2

                            @blk.tensor
                            def _(t, m=m, tt=tt, tpar=tpar):
                                for tok in wload[m]:
                                    wt(t, tok)
                                if (m, tt) != (0, 0):
                                    pm, ptt = (m, tt - 2) if tt >= 2 else (m - 1, tt + 2)
                                    if ptt >= 0 and (pm, ptt) in hmul_v:
                                        t.wait_ge(sV.h, hmul_v[(pm, ptt)])
                                for k in range(NKD):
                                    mm = t.matmul(psG[:, tpar * 512:(tpar + 1) * 512],
                                                  wgb[:, (m % 2) * 2048 + k * 128:(m % 2) * 2048 + (k + 1) * 128],
                                                  x1T[:, k * T + tt * 512:k * T + (tt + 1) * 512],
                                                  start=(k == 0), stop=(k == NKD - 1))
                                for k in range(NKD):
                                    mm = t.matmul(psU[:, tpar * 512:(tpar + 1) * 512],
                                                  wub[:, (m % 2) * 2048 + k * 128:(m % 2) * 2048 + (k + 1) * 128],
                                                  x1T[:, k * T + tt * 512:k * T + (tt + 1) * 512],
                                                  start=(k == 0), stop=(k == NKD - 1))
                                pi(mm)
                                mm_last[m] = sP.v
                                hmul_v[(m, tt, "mm")] = sP.v

                            @blk.scalar
                            def _(sc, m=m, tt=tt, tpar=tpar):
                                sc.wait_ge(sP.h, hmul_v[(m, tt, "mm")])
                                pm, ptt = (m, tt - 2) if tt >= 2 else (m - 1, tt + 2)
                                if (pm, ptt) in hmul_v:
                                    sc.wait_ge(sV.h, hmul_v[(pm, ptt)])
                                si(sc.activation(sgb[:, tpar * 512:(tpar + 1) * 512],
                                                 psG[:, tpar * 512:(tpar + 1) * 512], AF.Silu))
                                hmul_v[(m, tt, "sg")] = sS.v

                            @blk.vector
                            def _(v, m=m, tt=tt, tpar=tpar):
                                v.wait_ge(sS.h, hmul_v[(m, tt, "sg")])
                                pm, ptt = (m, tt - 2) if tt >= 2 else (m - 1, tt + 2)
                                if (pm, ptt) in hst_v:
                                    wt(v, hst_v[(pm, ptt)])
                                vi(v.tensor_mul(hstg[:, tpar * 512:(tpar + 1) * 512],
                                                sgb[:, tpar * 512:(tpar + 1) * 512],
                                                psU[:, tpar * 512:(tpar + 1) * 512]))
                                v.wait_ge(sV.h, sV.v)
                                hmul_v[(m, tt)] = sV.v
                                g_tree_ops(v, 6)

                            @blk.sync
                            def _(s, m=m, tt=tt, tpar=tpar):
                                s.wait_ge(sV.h, hmul_v[(m, tt)])
                                hst_v[(m, tt)] = DS["hst"].dma(
                                    s, AP(h_s, m * 128 * T + tt * 512, [[T, 128], [1, 512]]),
                                    hstg[:, tpar * 512:(tpar + 1) * 512])

                        @blk.gpsimd
                        def _(g):
                            g_load(g, 2)

                        @blk.tensor
                        def _(t):
                            g_tp(t, 2)

                        @blk.scalar
                        def _(sc):
                            g_store(sc, 2)

                        @blk.vector
                        def _(v):
                            g_evac(v, 2)
                    g_drain(blk, all_tiles=True)
                    flush_stores(blk, ["hst", "gst", "gix", "wg", "wu"])

            stX.close()
            stR = ExitStack()
            r2x = stR.enter_context(nc.sbuf_tensor("r2x", [128, NKD * T], F32))
            with ExitStack() as st5:
                with nc.Block() as blk:
                    @blk.sync
                    def _(s):
                        DS["hst"].flush(s)
                        DS["x1s"].flush(s)
                        DS["gst"].flush(s)

                    with (
                        nc.sbuf_tensor("wdb", [128, 4 * INTER], BF16) as wdb,
                        nc.sbuf_tensor("hbb", [128, 8 * 512], BF16) as hbb,
                        nc.sbuf_tensor("x1fb", [128, 2 * 512], F32) as x1fb,
                        nc.psum_tensor("psD", [128, 8 * 512], F32) as psD,
                    ):
                        hld = {}
                        dn_ev = {}
                        for qd in range(4):
                            RQ = {}

                            @blk.sync
                            def _(s, qd=qd):
                                if qd >= 1:
                                    for rdl in range(4):
                                        s.wait_ge(sP.h, RQ_prev["mm_%d_3" % rdl])
                                toks = []
                                for rdl in range(4):
                                    toks.append(DS["wd"].dma(
                                        s, wdb[:, rdl * INTER:(rdl + 1) * INTER],
                                        AP(wdT_s, (qd * 4 + rdl) * 128 * INTER,
                                           [[INTER, 128], [1, INTER]])))
                                RQ["wd"] = toks

                            for tt in range(4):
                                tpar = tt % 2

                                for k in range(NMO):
                                    @blk.sync
                                    def _(s, qd=qd, tt=tt, k=k):
                                        if k >= 8:
                                            s.wait_ge(sP.h, RQ["kmm_%d_%d" % (tt, k - 8)])
                                        elif tt >= 1:
                                            s.wait_ge(sP.h, RQ["kmm_%d_%d" % (tt - 1, NMO - 8 + k)])
                                        elif qd >= 1:
                                            s.wait_ge(sP.h, RQ_prev["kmm_%d_%d" % (3, NMO - 8 + k)])
                                        RQ["hld_%d_%d" % (tt, k)] = DS["hl"].dma(
                                            s, hbb[:, (k % 8) * 512:(k % 8 + 1) * 512],
                                            AP(h_s, k * 128 * T + tt * 512,
                                               [[T, 128], [1, 512]]))

                                    @blk.tensor
                                    def _(t, qd=qd, tt=tt, k=k, tpar=tpar):
                                        wt(t, RQ["hld_%d_%d" % (tt, k)])
                                        if k == 0:
                                            for tok in RQ["wd"]:
                                                wt(t, tok)
                                            if (qd, tt) != (0, 0):
                                                for rdl in range(4):
                                                    key = (qd, tt - 2, rdl) if tt >= 2 else (qd - 1, tt + 2, rdl)
                                                    if key in dn_ev:
                                                        t.wait_ge(sV.h, dn_ev[key])
                                        for rdl in range(4):
                                            mm = t.matmul(
                                                psD[:, (tpar * 4 + rdl) * 512:(tpar * 4 + rdl + 1) * 512],
                                                wdb[:, rdl * INTER + k * 128:rdl * INTER + (k + 1) * 128],
                                                hbb[:, (k % 8) * 512:(k % 8 + 1) * 512],
                                                start=(k == 0), stop=(k == NMO - 1))
                                        pi(mm)
                                        RQ["kmm_%d_%d" % (tt, k)] = sP.v
                                        if k == NMO - 1:
                                            for rdl in range(4):
                                                RQ["mm_%d_%d" % (rdl, tt)] = sP.v

                                for rdl in range(4):
                                    rd = qd * 4 + rdl

                                    @blk.sync
                                    def _(s, qd=qd, tt=tt, rdl=rdl, rd=rd, tpar=tpar):
                                        if rdl >= 2:
                                            key = (qd, tt, rdl - 2)
                                        elif tt >= 1:
                                            key = (qd, tt - 1, rdl + 2)
                                        elif qd >= 1:
                                            key = (qd - 1, 3, rdl + 2)
                                        else:
                                            key = None
                                        if key in dn_ev:
                                            s.wait_ge(sV.h, dn_ev[key])
                                        RQ["x1f_%d_%d" % (rdl, tt)] = DS["xfl"].dma(
                                            s, x1fb[:, (rdl % 2) * 512:(rdl % 2 + 1) * 512],
                                            AP(x1f_s, rd * 128 * T + tt * 512,
                                               [[T, 128], [1, 512]]))

                                    @blk.vector
                                    def _(v, qd=qd, tt=tt, rdl=rdl, rd=rd, tpar=tpar):
                                        v.wait_ge(sP.h, RQ["mm_%d_%d" % (rdl, tt)])
                                        wt(v, RQ["x1f_%d_%d" % (rdl, tt)])
                                        vi(v.tensor_add(r2x[:, rd * T + tt * 512:rd * T + (tt + 1) * 512],
                                                        psD[:, (tpar * 4 + rdl) * 512:(tpar * 4 + rdl + 1) * 512],
                                                        x1fb[:, (rdl % 2) * 512:(rdl % 2 + 1) * 512]))
                                        v.wait_ge(sV.h, sV.v)
                                        dn_ev[(qd, tt, rdl)] = sV.v

                            RQ_prev = RQ

                    with (
                        nc.psum_tensor("psS4a", [1, 2048], F32) as psS4a,
                        nc.psum_tensor("psS4b", [1, 2048], F32) as psS4b,
                        nc.sbuf_tensor("tb4", [128, 2 * 2048], BF16) as tb4,
                        nc.sbuf_tensor("rows4", [1, 4 * 2048], F32) as rows4,
                    ):
                        st4v = {}
                        for m in range(NKD):
                            @blk.vector
                            def _(v, m=m):
                                if m >= 1:
                                    v.wait_ge(sS.h, st4v[("sq", m - 1)])
                                    v.wait_ge(sP.h, st4v[("mm", m - 1)])
                                v.wait_ge(sV.h, vi(v.tensor_copy(tb4[:, 0:2048],
                                                                 r2x[:, m * T:(m + 1) * T])))
                                st4v[("rb", m)] = sV.v

                            @blk.scalar
                            def _(sc, m=m):
                                sc.wait_ge(sV.h, st4v[("rb", m)])
                                si(sc.activation(tb4[:, 2048:4096], tb4[:, 0:2048], AF.Square))
                                st4v[("sq", m)] = sS.v

                            @blk.tensor
                            def _(t, m=m):
                                t.wait_ge(sS.h, st4v[("sq", m)])
                                for w in range(4):
                                    mm = t.matmul(psS4a[:, w * 512:(w + 1) * 512], onecol[:, :],
                                                  tb4[:, w * 512:(w + 1) * 512],
                                                  start=(m == 0), stop=(m == NKD - 1))
                                    mm = t.matmul(psS4b[:, w * 512:(w + 1) * 512], onecol[:, :],
                                                  tb4[:, 2048 + w * 512:2048 + (w + 1) * 512],
                                                  start=(m == 0), stop=(m == NKD - 1))
                                pi(mm)
                                st4v[("mm", m)] = sP.v

                        @blk.vector
                        def _(v):
                            v.wait_ge(sP.h, st4v[("mm", NKD - 1)])
                            v.wait_ge(sV.h, vi(v.tensor_scalar_mul(rows4[:, 0:2048], psS4a[:, :], -1.0 / DM)))
                            v.wait_ge(sV.h, vi(v.tensor_scalar_mul(rows4[:, 2048:4096], psS4b[:, :], 1.0 / DM)))
                            v.wait_ge(sV.h, vi(v.tensor_mul(rows4[:, 4096:6144], rows4[:, 0:2048], rows4[:, 0:2048])))
                            v.wait_ge(sV.h, vi(v.tensor_sub(rows4[:, 4096:6144], rows4[:, 2048:4096], rows4[:, 4096:6144])))
                        v_var4 = sV.v

                        @blk.scalar
                        def _(sc):
                            sc.wait_ge(sV.h, v_var4)
                            sc.wait_ge(sS.h, si(sc.activation(rows4[:, 6144:8192], rows4[:, 4096:6144],
                                                              AF.Ln, bias=ceps[0:1, :])))
                            si(sc.activation(rows4[:, 6144:8192], rows4[:, 6144:8192], AF.Exp, scale=chalf[0:1, :]))
                        v_rstd4 = sS.v

                        rows4_toks = []

                        @blk.gpsimd
                        def _(g):
                            g.wait_ge(sS.h, v_rstd4)
                            rows4_toks.append(DS["r4b"].dma(g, row_s[1:2, :], rows4[:, 0:2048]))
                            rows4_toks.append(DS["r4b"].dma(g, row_s[2:3, :], rows4[:, 6144:8192]))


            with ExitStack() as st6:
                ostg = st6.enter_context(nc.sbuf_tensor("ostg", [128, 2 * 2048], F32))
                rsc = st6.enter_context(nc.sbuf_tensor("rsc", [128, 16], F32))
                rnc = st6.enter_context(nc.sbuf_tensor("rnc", [128, 16], F32))
                rnrs = st6.enter_context(nc.sbuf_tensor("rnrs", [128, 16], F32))
                psFa = st6.enter_context(nc.psum_tensor("psFa", [128, 2048], F32))
                psFb = st6.enter_context(nc.psum_tensor("psFb", [128, 2048], F32))

                with nc.Block() as blk:
                    colt = []

                    @blk.sync
                    def _(s):
                        for tok in rows4_toks:
                            wt(s, tok)
                        with nc.allow_non_contiguous_dma(reason="tiny per-token scale cols"):
                            colt.append(DS["bc4"].dma(s, rnc[:, :],
                                                      AP(row_s, 2048, [[1, 128], [128, 16]])))
                            colt.append(DS["bc4"].dma(s, rsc[:, :],
                                                      AP(row_s, 4096, [[1, 128], [128, 16]])))

                    @blk.vector
                    def _(v):
                        for tok in colt:
                            wt(v, tok)
                        vi(v.tensor_mul(rnrs[:, :], rnc[:, :], rsc[:, :]))
                        v.wait_ge(sV.h, sV.v)
                    v_rnrs = sV.v

                    oev = []
                    ost = []
                    ostq = []
                    for ti in range(NKD):
                        ps = psFa if ti % 2 == 0 else psFb

                        @blk.tensor
                        def _(t, ti=ti, ps=ps):
                            if ti >= 2:
                                t.wait_ge(sS.h, oev[ti - 2])
                            for j in range(NKD):
                                mm = t.transpose(ps[:, j * 128:(j + 1) * 128],
                                                 r2x[:, j * T + ti * 128:j * T + (ti + 1) * 128],
                                                 ident_f[:, :])
                            pi(mm)
                            oev.append(None)
                            ost.append(sP.v)

                        @blk.scalar
                        def _(sc, ti=ti, ps=ps):
                            sc.wait_ge(sP.h, ost[ti])
                            sc.wait_ge(sV.h, v_rnrs)
                            if ti >= 2:
                                wt(sc, ostq[ti - 2])
                            si(sc.activation(ostg[:, (ti % 2) * 2048:(ti % 2 + 1) * 2048],
                                             ps[:, :], AF.Identity,
                                             bias=rnrs[:, ti:ti + 1], scale=rsc[:, ti:ti + 1]))
                            oev[ti] = sS.v

                        @blk.sync
                        def _(s, ti=ti):
                            s.wait_ge(sS.h, oev[ti])
                            ostq.append(DS["out"].dma(
                                s, out[ti * 128:(ti + 1) * 128, :],
                                ostg[:, (ti % 2) * 2048:(ti % 2 + 1) * 2048]))

            stR.close()

        with nc.Block() as blk:
            @blk.sync
            def _(s):
                for name, dst in dump_outs.items():
                    if name in dump_specs:
                        DS["dmp"].dma(s, dst.ap(), dump_specs[name].ap())
                for st_ in DS.values():
                    st_.flush(s)

    return nc


def _in_map(inputs, b):
    return {
        "u": np.ascontiguousarray(inputs["L0_attn_u"][b]),
        "w_in": inputs["L0_attn_W_in"],
        "conv_w": inputs["L0_attn_conv_w"],
        "conv_b": inputs["L0_attn_conv_b"].reshape(CONVD, 1),
        "dt_bias": inputs["L0_attn_dt_bias"].reshape(NH, 1),
        "a_log": inputs["L0_attn_A_log"].reshape(NH, 1),
        "d_vec": inputs["L0_attn_D"].reshape(1, NH),
        "norm_w": inputs["L0_attn_norm_w"].reshape(DIN, 1),
        "w_out": inputs["L0_attn_W_out"],
        "lut": inputs["L0_ffn_lut"].reshape(1, 16),
        "gate_idx": inputs["L0_ffn_gate_idx"],
        "up_idx": inputs["L0_ffn_up_idx"],
        "down_idx": inputs["L0_ffn_down_idx"],
    }


def kernel(**inputs):
    if "nc" not in _CACHE:
        _CACHE["nc"] = build("full")
    nc = _CACHE["nc"]
    in_maps = [_in_map(inputs, b) for b in range(8)]
    res = run_bass_kernel_spmd(nc, in_maps, core_ids=list(range(8)))
    return np.stack([res.results[b]["out"] for b in range(8)], axis=0)


# version 35
# speedup vs baseline: 1.0018x; 1.0018x over previous
import sys
sys.path.insert(0, "/opt/trn_rl_repo")
from contextlib import ExitStack

import numpy as np
import concourse.bass as bass
import concourse.mybir as mybir
from concourse.bass_utils import run_bass_kernel_spmd

F32 = mybir.dt.float32
U8 = mybir.dt.uint8
BF16 = mybir.dt.bfloat16
I32 = mybir.dt.int32
AF = mybir.ActivationFunctionType
ALU = mybir.AluOpType
AP = bass.AP

T = 2048
DM = 2048
DIN = 4096
NH = 32
CHUNK = 256
NCH = 8
CONVD = 4224
DIP = 8352
INTER = 5504
EPS = 1e-5
NKD = 16
NKI = 32
NMO = 43
NMP = 66

_CACHE = {}


def build(stage="full", dump=()):
    nc = bass.Bass("TRN2", target_bir_lowering=False)

    u_in = nc.dram_tensor("u", [T, DM], F32, kind="ExternalInput")
    w_in = nc.dram_tensor("w_in", [DM, DIP], F32, kind="ExternalInput")
    conv_w = nc.dram_tensor("conv_w", [CONVD, 4], F32, kind="ExternalInput")
    conv_b = nc.dram_tensor("conv_b", [CONVD, 1], F32, kind="ExternalInput")
    dt_bias = nc.dram_tensor("dt_bias", [NH, 1], F32, kind="ExternalInput")
    a_log = nc.dram_tensor("a_log", [NH, 1], F32, kind="ExternalInput")
    d_in = nc.dram_tensor("d_vec", [1, NH], F32, kind="ExternalInput")
    norm_w = nc.dram_tensor("norm_w", [DIN, 1], F32, kind="ExternalInput")
    w_out = nc.dram_tensor("w_out", [DIN, DM], F32, kind="ExternalInput")
    lut_in = nc.dram_tensor("lut", [1, 16], F32, kind="ExternalInput")
    gidx = nc.dram_tensor("gate_idx", [INTER, DM], I32, kind="ExternalInput")
    uidx = nc.dram_tensor("up_idx", [INTER, DM], I32, kind="ExternalInput")
    didx = nc.dram_tensor("down_idx", [DM, INTER], I32, kind="ExternalInput")
    out = nc.dram_tensor("out", [T, DM], F32, kind="ExternalOutput")

    uT_s = nc.dram_tensor("uT_s", [NKD, 128, DM], F32)
    zxb_s = nc.dram_tensor("zxb_s", [DIN + CONVD, T], BF16)
    ynorm_s = nc.dram_tensor("ynorm_s", [DIN, T], BF16)
    x1f_s = nc.dram_tensor("x1f_s", [NKD, 128, T], F32)
    x1b_s = nc.dram_tensor("x1b_s", [NKD, 128, T], BF16)
    h_s = nc.dram_tensor("h_s", [NMO, 128, T], BF16)
    wgT_s = nc.dram_tensor("wgT_s", [NMO, 128, DM], BF16)
    wuT_s = nc.dram_tensor("wuT_s", [NMO, 128, DM], BF16)
    wdT_s = nc.dram_tensor("wdT_s", [NKD, 128, INTER], BF16)
    wob_s = nc.dram_tensor("wob_s", [NKD, 128, DIN], BF16)
    acr_s = nc.dram_tensor("acr_s", [NCH, NH * CHUNK], F32)
    row_s = nc.dram_tensor("row_s", [4, 2048], F32)

    dump_specs = {"zxb": zxb_s, "ynorm": ynorm_s, "uT": uT_s, "x1f": x1f_s,
                  "h": h_s, "wgT": wgT_s, "wuT": wuT_s, "wdT": wdT_s, "wob": wob_s}
    dump_outs = {}
    row_srcs = {}

    es = ExitStack()
    with es:
        ident_f = es.enter_context(nc.sbuf_tensor("ident_f", [128, 128], F32))
        ident_b = es.enter_context(nc.sbuf_tensor("ident_b", [128, 128], BF16))
        lutc = es.enter_context(nc.sbuf_tensor("lutc", [128, 16], F32))
        treec = es.enter_context(nc.sbuf_tensor("treec", [128, 16], F32))
        umask = es.enter_context(nc.sbuf_tensor("umask", [128, 2 * CHUNK], U8))
        zero_b = es.enter_context(nc.sbuf_tensor("zero_b", [128, CHUNK], BF16))
        onecol = es.enter_context(nc.sbuf_tensor("onecol", [128, 1], BF16))
        ceps = es.enter_context(nc.sbuf_tensor("ceps", [128, 1], F32))
        chalf = es.enter_context(nc.sbuf_tensor("chalf", [128, 1], F32))
        cidin = es.enter_context(nc.sbuf_tensor("cidin", [128, 1], F32))
        cwcol = es.enter_context(nc.sbuf_tensor("cwcol", [128, 33 * 4], F32))
        cbcol = es.enter_context(nc.sbuf_tensor("cbcol", [128, 33], F32))
        dtbc = es.enter_context(nc.sbuf_tensor("dtbc", [NH, 1], F32))
        nega = es.enter_context(nc.sbuf_tensor("nega", [NH, 1], F32))
        dcol128 = es.enter_context(nc.sbuf_tensor("dcol128", [128, NH], F32))
        nwcol = es.enter_context(nc.sbuf_tensor("nwcol", [128, NKI], F32))

        class Sem:
            def __init__(self, name):
                self.h = es.enter_context(nc.semaphore(name))
                self.v = 0

        sQ, aQ, gQ = Sem("sQ"), Sem("aQ"), Sem("gQ")
        sV, sS, sP, sG = Sem("sV"), Sem("sS"), Sem("sP"), Sem("sG")

        class DStream:

            def __init__(self, name, n=2):
                self.sems = [es.enter_context(nc.semaphore(f"{name}{i}")) for i in range(n)]
                self.v = [0] * n
                self.n = n
                self.i = 0

            def dma(self, q, out, in_, **kw):
                k = self.i % self.n
                q.wait_ge(self.sems[k], self.v[k])
                q.dma_start(out, in_, **kw).then_inc(self.sems[k], 16)
                self.v[k] += 16
                self.i += 1
                return (self.sems[k], self.v[k])

            def last(self):
                k = (self.i - 1) % self.n
                return (self.sems[k], self.v[k])

            def flush(self, eng):
                for k in range(self.n):
                    eng.wait_ge(self.sems[k], self.v[k])

        def wt(eng, tok):
            eng.wait_ge(tok[0], tok[1])

        DS = {}
        for nm, nn in [("gix", 2), ("gst", 2), ("prm", 1), ("prm2", 1), ("ul", 2),
                       ("ust", 2), ("win", 2), ("zst", 2), ("dmp", 1), ("wol", 2),
                       ("wos", 2), ("acr", 1), ("bcx", 1), ("cse", 1), ("xl", 2),
                       ("zl", 2), ("rs", 2), ("rb2", 1), ("rbc", 1), ("yst", 4),
                       ("ynl", 4), ("wom", 4), ("ub", 4), ("x1s", 2), ("r3b", 2),
                       ("bc3", 2), ("wg", 2), ("wu", 2), ("hst", 2), ("wd", 4),
                       ("hl", 4), ("xfl", 2), ("r4b", 2), ("bc4", 2), ("out", 2), ("x1b", 2), ("x1l", 2)]:
            DS[nm] = DStream(nm, nn)

        def vi(inst):
            inst.then_inc(sV.h, 1); sV.v += 1; return sV.v

        def si(inst):
            inst.then_inc(sS.h, 1); sS.v += 1; return sS.v

        def pi(inst):
            inst.then_inc(sP.h, 1); sP.v += 1; return sP.v

        def gi(inst):
            inst.then_inc(sG.h, 1); sG.v += 1; return sG.v

        def qi(inst, sem):
            inst.then_inc(sem.h, 16); sem.v += 16; return sem.v

        gtiles = []
        for r in range(NMO):
            gtiles.append((gidx, wgT_s, r, 0, 2048))
        for r in range(NMO):
            gtiles.append((uidx, wuT_s, r, 0, 2048))
        for rd in range(NKD):
            for cb, F in ((0, 2048), (2048, 2048), (4096, 1408)):
                gtiles.append((didx, wdT_s, rd, cb, F))
        n_gt = len(gtiles)
        gst = dict(load=0, tree=0, tp=0, evac=0, store=0, tree_ready=0, tree_readyQ=None,
                   tg=None)
        g_load_v, g_tree_v, g_tp_v, g_evac_v, g_store_v = [], [], [], [], []
        wu_store_v = {}
        wg_store_v = {}
        gb = {}

        gcnt = [0]

        def galloc(stk, nsl=5, pslots=2):
            n = gcnt[0]
            gcnt[0] += 1
            gb["nsl"] = nsl
            gb["pslots"] = pslots
            gb["idx"] = stk.enter_context(nc.sbuf_tensor("g%d_idx" % n, [128, 4 * 2048], BF16))
            gb["b3"] = stk.enter_context(nc.sbuf_tensor("g%d_b3" % n, [128, 2048], U8))
            gb["b2"] = stk.enter_context(nc.sbuf_tensor("g%d_b2" % n, [128, 2048], U8))
            gb["b1"] = stk.enter_context(nc.sbuf_tensor("g%d_b1" % n, [128, 2048], U8))
            gb["s"] = stk.enter_context(nc.sbuf_tensor("g%d_s" % n, [128, 2048], BF16))
            gb["v"] = stk.enter_context(nc.sbuf_tensor("g%d_v" % n, [128, 6 * 2048], BF16))
            gb["stg"] = stk.enter_context(nc.sbuf_tensor("g%d_stg" % n, [128, nsl * 2048], BF16))
            gb["ps"] = stk.enter_context(nc.psum_tensor("g%d_ps" % n, [128, pslots * 2048], BF16))

        def g_load(g, n):
            for _ in range(n):
                i = gst["load"]
                if i >= n_gt or i >= len(g_tree_v) + 4:
                    return
                src, dst, r, cb, F = gtiles[i]
                par = i % 4
                if i >= 4:
                    g.wait_ge(sV.h, g_tree_v[i - 4])
                g_load_v.append(DS["gix"].dma(
                    g, gb["idx"][:, par * 2048:par * 2048 + F],
                    AP(src, (r * 128) * src.shape[1] + cb,
                       [[src.shape[1], 128], [1, F]])))
                gst["load"] += 1

        def _tree_prog(i):
            src, dst, r, cb, F = gtiles[i]
            par = i % 4
            res = i % 2
            P = []

            def first(v):
                wt(v, g_load_v[i])
                v.wait_ge(sV.h, gst["tree_ready"])
                if gst["tree_readyQ"]:
                    wt(v, gst["tree_readyQ"])
                if i >= 2:
                    v.wait_ge(sP.h, g_tp_v[i - 2])
                ix = gb["idx"][:, par * 2048:par * 2048 + F]
                v.wait_ge(sV.h, vi(v.tensor_single_scalar(gb["b3"][:, :F], ix, 8.0, ALU.is_ge)))
            P.append(first)
            P.append(lambda v: v.wait_ge(sV.h, vi(v.scalar_tensor_tensor(
                gb["s"][:, :F], gb["b3"][:, :F], -8.0,
                gb["idx"][:, par * 2048:par * 2048 + F], ALU.mult, ALU.add))))
            P.append(lambda v: v.wait_ge(sV.h, vi(v.tensor_single_scalar(
                gb["b2"][:, :F], gb["s"][:, :F], 4.0, ALU.is_ge))))
            P.append(lambda v: v.wait_ge(sV.h, vi(v.scalar_tensor_tensor(
                gb["s"][:, :F], gb["b2"][:, :F], -4.0, gb["s"][:, :F], ALU.mult, ALU.add))))
            P.append(lambda v: v.wait_ge(sV.h, vi(v.tensor_single_scalar(
                gb["b1"][:, :F], gb["s"][:, :F], 2.0, ALU.is_ge))))
            P.append(lambda v: v.wait_ge(sV.h, vi(v.scalar_tensor_tensor(
                gb["s"][:, :F], gb["b1"][:, :F], -2.0, gb["s"][:, :F], ALU.mult, ALU.add))))
            for g8, sl4 in ((4, (2, 3, 4, 5)), (0, (res, 3, 4, 5))):
                for jj in range(4):
                    P.append(lambda v, jj=jj, g8=g8, sl4=sl4, last=(jj == 3): (
                        vi(v.tensor_scalar(gb["v"][:, sl4[jj] * 2048:sl4[jj] * 2048 + F],
                                           gb["s"][:, :F],
                                           treec[:, g8 + jj:g8 + jj + 1],
                                           treec[:, 8 + g8 + jj:9 + g8 + jj],
                                           ALU.mult, ALU.add)),
                        v.wait_ge(sV.h, sV.v) if last else None))
                P.append(lambda v, sl4=sl4: vi(v.copy_predicated(
                    gb["v"][:, sl4[0] * 2048:sl4[0] * 2048 + F], gb["b1"][:, :F],
                    gb["v"][:, sl4[1] * 2048:sl4[1] * 2048 + F])))
                P.append(lambda v, sl4=sl4: (
                    vi(v.copy_predicated(
                        gb["v"][:, sl4[2] * 2048:sl4[2] * 2048 + F], gb["b1"][:, :F],
                        gb["v"][:, sl4[3] * 2048:sl4[3] * 2048 + F])),
                    v.wait_ge(sV.h, sV.v)))
                P.append(lambda v, sl4=sl4: (
                    vi(v.copy_predicated(
                        gb["v"][:, sl4[0] * 2048:sl4[0] * 2048 + F], gb["b2"][:, :F],
                        gb["v"][:, sl4[2] * 2048:sl4[2] * 2048 + F])),
                    v.wait_ge(sV.h, sV.v)))

            def final(v):
                vi(v.copy_predicated(gb["v"][:, res * 2048:res * 2048 + F], gb["b3"][:, :F],
                                     gb["v"][:, 2 * 2048:2 * 2048 + F]))
                v.wait_ge(sV.h, sV.v)
            P.append(final)
            return P

        def g_tree_ops(v, k):
            while k > 0:
                if gst["tg"] is None:
                    i = gst["tree"]
                    if i >= n_gt or i >= gst["load"] or i >= len(g_tp_v) + 2:
                        return
                    gst["tg"] = (_tree_prog(i), 0)
                prog, pos = gst["tg"]
                while k > 0 and pos < len(prog):
                    prog[pos](v)
                    pos += 1
                    k -= 1
                if pos == len(prog):
                    gst["tg"] = None
                    g_tree_v.append(sV.v)
                    gst["tree"] += 1
                else:
                    gst["tg"] = (prog, pos)

        def g_tree(v, n):
            g_tree_ops(v, 21 * n)

        def g_tp(t, n):
            psl = gb["pslots"]
            for _ in range(n):
                i = gst["tp"]
                if (i >= n_gt or gst["tree"] < i + 1
                        or (gst["tree"] < min(i + 2, n_gt)
                            and gst["tree"] < gst["load"])
                        or i >= gst["evac"] + psl):
                    return
                src, dst, r, cb, F = gtiles[i]
                par = i % psl
                res = i % 2
                t.wait_ge(sV.h, g_tree_v[i])
                if i >= psl:
                    t.wait_ge(sV.h, g_evac_v[i - psl])
                for j in range(F // 128):
                    mm = t.transpose(gb["ps"][:, par * 2048 + j * 128:par * 2048 + (j + 1) * 128],
                                     gb["v"][:, res * 2048 + j * 128:res * 2048 + (j + 1) * 128],
                                     ident_b[:, :])
                pi(mm)
                g_tp_v.append(sP.v)
                gst["tp"] += 1

        def g_evac(v, n):
            for _ in range(n):
                i = gst["evac"]
                if i >= n_gt or i >= gst["tp"]:
                    return
                src, dst, r, cb, F = gtiles[i]
                nsl = gb["nsl"]
                sl = i % nsl
                par = i % gb["pslots"]
                v.wait_ge(sP.h, g_tp_v[i])
                if i >= nsl:
                    wt(v, g_store_v[i - nsl])
                vi(v.tensor_copy(gb["stg"][:, sl * 2048:sl * 2048 + F],
                                 gb["ps"][:, par * 2048:par * 2048 + F]))
                v.wait_ge(sV.h, sV.v)
                g_evac_v.append(sV.v)
                gst["evac"] += 1

        def g_store(sc, n, lag=4):
            lag = min(lag, gb["nsl"] - 1)
            for _ in range(n):
                i = gst["store"]
                if i >= n_gt or i >= gst["evac"] - lag:
                    return
                src, dst, r, cb, F = gtiles[i]
                sl = i % gb["nsl"]
                sc.wait_ge(sV.h, g_evac_v[i])
                tok = DS["gst"].dma(sc, AP(dst, r * 128 * dst.shape[2] + cb,
                                           [[dst.shape[2], 128], [1, F]]),
                                    gb["stg"][:, sl * 2048:sl * 2048 + F])
                g_store_v.append(tok)
                if dst is wgT_s:
                    wg_store_v[r] = tok
                if dst is wuT_s:
                    wu_store_v[r] = tok
                gst["store"] += 1

        def g_step(blk, n):
            for _ in range(n):
                @blk.gpsimd
                def _(g):
                    g_load(g, 1 + (1 if gst["load"] < 3 else 0))

                @blk.tensor
                def _(t):
                    g_tp(t, 1)

                @blk.scalar
                def _(sc):
                    g_store(sc, 1)

                @blk.vector
                def _(v):
                    g_tree(v, 1)
                    g_evac(v, 1)

        def flush_stores(blk, names):
            @blk.sync
            def _(s):
                for nm in names:
                    DS[nm].flush(s)

        def g_drain(blk, all_tiles=False):
            guard = 0
            target = n_gt if all_tiles else None
            while (gst["store"] < (target if all_tiles else gst["load"])
                   and guard < 250):
                if all_tiles:
                    @blk.gpsimd
                    def _(g):
                        g_load(g, 2)

                @blk.tensor
                def _(t):
                    g_tp(t, 1)

                @blk.scalar
                def _(sc):
                    g_store(sc, 2, lag=0)

                @blk.vector
                def _(v):
                    g_tree(v, 1)
                    g_evac(v, 1)
                guard += 1
            assert gst["store"] == (target if all_tiles else gst["load"]), gst

        stDT = ExitStack()
        dtv = stDT.enter_context(nc.sbuf_tensor("dtv", [NH, T], F32, side="right"))
        lndt = stDT.enter_context(nc.sbuf_tensor("lndt", [NH, T], F32, side="right"))
        acum = stDT.enter_context(nc.sbuf_tensor("acum", [NH, T], F32, side="right"))
        dtdec = stDT.enter_context(nc.sbuf_tensor("dtdec", [NH, T], F32, side="right"))
        dtraw = sp1 = sp2 = None
        row_srcs = {"dtv": dtv, "acum": acum, "dtdec": dtdec, "lndt": lndt}
        for name in dump:
            if name in dump_specs:
                src = dump_specs[name]
                dump_outs[name] = nc.dram_tensor("dump_" + name, list(src.shape), src.dtype,
                                                 kind="ExternalOutput")
            elif name in row_srcs:
                dump_outs[name] = nc.dram_tensor("dump_" + name, [NH, T], F32,
                                                 kind="ExternalOutput")

        stU = ExitStack()
        uTb = stU.enter_context(nc.sbuf_tensor("uTb", [128, NKD * T], BF16))
        wobb = stU.enter_context(nc.sbuf_tensor("wobb", [128, 2 * 1024], BF16))
        wob_items = [(mb, kh) for mb in range(NKD) for kh in range(4)]
        wob_ld_v, wob_st_v = [], []
        wst = dict(ld=0, st=0)

        def wob_load(g, n):
            for _ in range(n):
                i = wst["ld"]
                if i >= len(wob_items) or i >= wst["st"] + 2:
                    return
                mb, kh = wob_items[i]
                if i >= 2:
                    wt(g, wob_st_v[i - 2])
                wob_ld_v.append(DS["wol"].dma(
                    g, wobb[:, (i % 2) * 1024:(i % 2 + 1) * 1024]
                    .rearrange("p (k c) -> p k c", k=8),
                    AP(w_out, (kh * 8 * 128) * DM + mb * 128,
                       [[DM, 128], [DM * 128, 8], [1, 128]])))
                wst["ld"] += 1

        def wob_store(sc, n):
            for _ in range(n):
                i = wst["st"]
                if i >= len(wob_items) or i >= wst["ld"]:
                    return
                mb, kh = wob_items[i]
                wt(sc, wob_ld_v[i])
                wob_st_v.append(DS["wos"].dma(
                    sc, AP(wob_s, mb * 128 * DIN + kh * 1024, [[DIN, 128], [1, 1024]]),
                    wobb[:, (i % 2) * 1024:(i % 2 + 1) * 1024]))
                wst["st"] += 1
        with ExitStack() as st0:
            utile = st0.enter_context(nc.sbuf_tensor("utile", [128, 2 * DM], F32))
            ustg = st0.enter_context(nc.sbuf_tensor("ustg", [128, 2 * 1024], F32))
            iscr = st0.enter_context(nc.sbuf_tensor("iscr", [128, 256], I32))
            psTa = st0.enter_context(nc.psum_tensor("psTa", [128, 1024], F32))
            psTb = st0.enter_context(nc.psum_tensor("psTb", [128, 1024], F32))
            galloc(st0)

            with nc.Block() as blk:
                @blk.gpsimd
                def _(g):
                    gi(g.iota(iscr[:, :128], [[1, 128]], base=0, channel_multiplier=-1))
                v_iota1 = sG.v

                @blk.vector
                def _(v):
                    v.wait_ge(sG.h, v_iota1)
                    v.wait_ge(sV.h, vi(v.tensor_single_scalar(ident_f[:, :], iscr[:, :128], 0.0, ALU.is_equal)))
                    vi(v.tensor_copy(ident_b[:, :], ident_f[:, :]))
                v_ident = sV.v

                @blk.gpsimd
                def _(g):
                    g.wait_ge(sV.h, v_ident)
                    gi(g.iota(iscr[:, :256], [[-1, 256]], base=0, channel_multiplier=1))
                    gi(g.memset(zero_b[:, :], 0.0))
                    gi(g.memset(onecol[:, :], 1.0))
                    gi(g.memset(ceps[:, :], EPS))
                    gi(g.memset(chalf[:, :], -0.5))
                    gi(g.memset(cidin[:, :], 1.0 / DIN))
                v_iota2 = sG.v

                @blk.vector
                def _(v):
                    v.wait_ge(sG.h, v_iota2)
                    vi(v.tensor_single_scalar(umask[:, 0:CHUNK], iscr[:, :256], 0.0, ALU.is_gt))
                    vi(v.tensor_single_scalar(umask[:, CHUNK:2 * CHUNK], iscr[:, :256], -128.0, ALU.is_gt))

                @blk.sync
                def _(s):
                    DS["prm"].dma(s, cwcol[:, :].rearrange("p (g j) -> p g j", g=33),
                                  AP(conv_w, 0, [[4, 128], [512, 33], [1, 4]]))
                    DS["prm"].dma(s, dtbc[:, :], dt_bias[:, :])
                    DS["prm"].dma(s, nega[:, :], a_log[:, :])
                    DS["prm"].dma(s, dcol128[:, :], AP(d_in, 0, [[0, 128], [1, NH]]))
                    DS["prm"].dma(s, lutc[:, :], AP(lut_in, 0, [[0, 128], [1, 16]]))

                @blk.gpsimd
                def _(g):
                    with nc.allow_non_contiguous_dma(reason="tiny one-time param loads"):
                        DS["prm2"].dma(g, cbcol[:, :], AP(conv_b, 0, [[1, 128], [128, 33]]))
                        DS["prm2"].dma(g, nwcol[:, :], AP(norm_w, 0, [[1, 128], [128, NKI]]))
                v_paramsG = DS["prm2"].last()
                v_params = DS["prm"].last()

                @blk.vector
                def _(v):
                    wt(v, v_params)
                    vi(v.tensor_sub(treec[:, 0:8], lutc[:, 1:16:2], lutc[:, 0:16:2]))
                    vi(v.tensor_copy(treec[:, 8:16], lutc[:, 0:16:2]))
                gst["tree_ready"] = sV.v
                gst["tree_readyQ"] = v_params

                u_load_v, half_tp_v, half_st_v, half_ev_v = [], [], [], []

                @blk.sync
                def _(s):
                    for ti in range(2):
                        u_load_v.append(DS["ul"].dma(
                            s, utile[:, (ti % 2) * DM:(ti % 2 + 1) * DM],
                            u_in[ti * 128:(ti + 1) * 128, :]))

                for ti in range(NKD):
                    for half in range(2):
                        hidx = 2 * ti + half
                        ps = psTa if hidx % 2 == 0 else psTb

                        @blk.tensor
                        def _(t, ti=ti, half=half, hidx=hidx, ps=ps):
                            t.wait_ge(sV.h, v_ident)
                            wt(t, u_load_v[ti])
                            if hidx >= 2:
                                t.wait_ge(sV.h, half_ev_v[hidx - 2][0])
                                t.wait_ge(sS.h, half_ev_v[hidx - 2][1])
                            for jl in range(8):
                                j = half * 8 + jl
                                mm = t.transpose(ps[:, jl * 128:(jl + 1) * 128],
                                                 utile[:, (ti % 2) * DM + j * 128:(ti % 2) * DM + (j + 1) * 128],
                                                 ident_f[:, :])
                            pi(mm)
                            half_tp_v.append(sP.v)

                        @blk.vector
                        def _(v, ti=ti, half=half, hidx=hidx, ps=ps):
                            v.wait_ge(sP.h, half_tp_v[hidx])
                            vi(v.tensor_copy(
                                uTb[:, :].rearrange("p (j t) -> p j t", j=NKD)[
                                    :, half * 8:half * 8 + 8, ti * 128:(ti + 1) * 128],
                                ps[:, :].rearrange("p (j q) -> p j q", j=8)))
                            half_ev_v.append([sV.v, None])

                        @blk.scalar
                        def _(sc, ti=ti, half=half, hidx=hidx, ps=ps):
                            sc.wait_ge(sP.h, half_tp_v[hidx])
                            if hidx >= 2:
                                wt(sc, half_st_v[hidx - 2])
                            si(sc.copy(ustg[:, (hidx % 2) * 1024:(hidx % 2 + 1) * 1024], ps[:, :]))
                            half_ev_v[hidx][1] = sS.v

                        @blk.sync
                        def _(s, ti=ti, half=half, hidx=hidx):
                            s.wait_ge(sS.h, half_ev_v[hidx][1])
                            half_st_v.append(DS["ust"].dma(
                                s, AP(uT_s, ti * 128 * DM + half * 1024,
                                      [[DM, 128], [1, 1024]]),
                                ustg[:, (hidx % 2) * 1024:(hidx % 2 + 1) * 1024]))
                            if half == 1 and ti + 2 < NKD:
                                s.wait_ge(sP.h, half_tp_v[2 * ti + 1])
                                u_load_v.append(DS["ul"].dma(
                                    s, utile[:, (ti % 2) * DM:(ti % 2 + 1) * DM],
                                    u_in[(ti + 2) * 128:(ti + 3) * 128, :]))

                    if ti % 3 == 2:
                        g_step(blk, 1)
                g_drain(blk)
                flush_stores(blk, ["gst", "ust", "gix"])

        if stage != "b0":
            with ExitStack() as st1:
                wcolb = st1.enter_context(nc.sbuf_tensor("wcolb", [128, 2 * 2048], BF16))
                zstg = st1.enter_context(nc.sbuf_tensor("zstg", [128, 2 * 1024], BF16))
                dtraw = st1.enter_context(nc.sbuf_tensor("dtraw", [NH, T], F32))
                sp1 = st1.enter_context(nc.sbuf_tensor("sp1", [NH, T], F32))
                sp2 = st1.enter_context(nc.sbuf_tensor("sp2", [NH, T], F32))
                psWa = st1.enter_context(nc.psum_tensor("psWa", [128, 1024], F32))
                psWb = st1.enter_context(nc.psum_tensor("psWb", [128, 1024], F32))
                galloc(st1, nsl=4)

                with nc.Block() as blk:
                    w_load_v, mm_v, ev_v, st_v = [], [], [], []

                    @blk.gpsimd
                    def _(g):
                        for m in range(2):
                            M = 128 if m < NMP - 1 else 32
                            w_load_v.append(DS["win"].dma(
                                g, wcolb[:, (m % 2) * 2048:(m % 2) * 2048 + 16 * M]
                                .rearrange("p (k c) -> p k c", k=16),
                                AP(w_in, m * 128, [[DIP, 128], [DIP * 128, 16], [1, M]])))

                    for m in range(NMP):
                        M = 128 if m < NMP - 1 else 32
                        for half in range(2):
                            hidx = 2 * m + half
                            ps = psWa if hidx % 2 == 0 else psWb

                            @blk.tensor
                            def _(t, m=m, M=M, half=half, hidx=hidx, ps=ps):
                                wt(t, w_load_v[m])
                                if hidx >= 2:
                                    t.wait_ge(sS.h, ev_v[hidx - 2])
                                for k in range(16):
                                    for tt in range(2):
                                        mm = t.matmul(
                                            ps[0:M, tt * 512:(tt + 1) * 512],
                                            wcolb[:, (m % 2) * 2048 + k * M:(m % 2) * 2048 + (k + 1) * M],
                                            uTb[:, k * T + half * 1024 + tt * 512:
                                                k * T + half * 1024 + (tt + 1) * 512],
                                            start=(k == 0), stop=(k == 15))
                                pi(mm)
                                mm_v.append(sP.v)

                            @blk.scalar
                            def _(sc, m=m, M=M, half=half, hidx=hidx, ps=ps):
                                sc.wait_ge(sP.h, mm_v[hidx])
                                if hidx >= 2 and st_v[hidx - 2] is not None:
                                    wt(sc, st_v[hidx - 2])
                                if m < 32:
                                    si(sc.copy(zstg[0:M, (hidx % 2) * 1024:(hidx % 2 + 1) * 1024],
                                               ps[0:M, :]))
                                elif m < NMP - 1:
                                    si(sc.copy(zstg[0:M, (hidx % 2) * 1024:(hidx % 2 + 1) * 1024],
                                               ps[0:M, :]))
                                else:
                                    si(sc.copy(dtraw[:, half * 1024:(half + 1) * 1024], ps[0:NH, :]))
                                ev_v.append(sS.v)

                            @blk.sync
                            def _(s, m=m, M=M, half=half, hidx=hidx):
                                if m < NMP - 1:
                                    s.wait_ge(sS.h, ev_v[hidx])
                                    st_v.append(DS["zst"].dma(
                                        s, AP(zxb_s, m * 128 * T + half * 1024,
                                              [[T, M], [1, 1024]]),
                                        zstg[0:M, (hidx % 2) * 1024:(hidx % 2 + 1) * 1024]))
                                else:
                                    st_v.append(None)

                        @blk.gpsimd
                        def _(g, m=m):
                            wob_load(g, 1)

                        @blk.scalar
                        def _(sc, m=m):
                            wob_store(sc, 1)

                        @blk.gpsimd
                        def _(g, m=m):
                            wob_load(g, 1)

                        @blk.scalar
                        def _(sc, m=m):
                            wob_store(sc, 1)

                        @blk.gpsimd
                        def _(g, m=m):
                            if m + 2 < NMP:
                                mn = m + 2
                                Mn = 128 if mn < NMP - 1 else 32
                                g.wait_ge(sP.h, mm_v[2 * m + 1])
                                w_load_v.append(DS["win"].dma(
                                    g, wcolb[:, (mn % 2) * 2048:(mn % 2) * 2048 + 16 * Mn]
                                    .rearrange("p (k c) -> p k c", k=16),
                                    AP(w_in, mn * 128, [[DIP, 128], [DIP * 128, 16], [1, Mn]])))

                        if m % 2 == 1:
                            g_step(blk, 1)

                    g_drain(blk)

                    v_dtraw = sS.v

                    @blk.vector
                    def _(v):
                        v.wait_ge(sS.h, v_dtraw)
                        wt(v, v_params)
                        vi(v.tensor_scalar(sp1[:, :], dtraw[:, :], dtbc[:, 0:1], 0.0, ALU.add, ALU.max))
                    v_sp1 = sV.v

                    @blk.scalar
                    def _(sc):
                        sc.wait_ge(sS.h, v_dtraw)
                        wt(sc, v_params)
                        sc.wait_ge(sS.h, si(sc.activation(sp2[:, :], dtraw[:, :], AF.Abs, bias=dtbc[:, 0:1])))
                        sc.wait_ge(sS.h, si(sc.activation(sp2[:, :], sp2[:, :], AF.Exp, scale=-1.0)))
                        si(sc.activation(sp2[:, :], sp2[:, :], AF.Ln, bias=1.0))
                    v_sp2 = sS.v

                    @blk.vector
                    def _(v):
                        v.wait_ge(sV.h, v_sp1)
                        v.wait_ge(sS.h, v_sp2)
                        vi(v.tensor_add(dtv[:, :], sp1[:, :], sp2[:, :]))
                    v_dtv = sV.v

                    @blk.scalar
                    def _(sc):
                        sc.wait_ge(sV.h, v_dtv)
                        si(sc.activation(lndt[:, :], dtv[:, :], AF.Ln))
                        sc.wait_ge(sS.h, si(sc.activation(nega[:, :], nega[:, :], AF.Exp)))
                    v_lndt = sS.v

                    @blk.vector
                    def _(v):
                        v.wait_ge(sS.h, v_lndt)
                        v.wait_ge(sV.h, vi(v.tensor_scalar_mul(nega[:, :], nega[:, :], -1.0)))
                        v.wait_ge(sV.h, vi(v.tensor_scalar_mul(sp1[:, :], dtv[:, :], nega[:, 0:1])))
                        for c in range(NCH):
                            sl = slice(c * CHUNK, (c + 1) * CHUNK)
                            vi(v.tensor_tensor_scan(acum[:, sl], sp1[:, sl], sp1[:, sl],
                                                    0.0, ALU.add, ALU.bypass))
                        v.wait_ge(sV.h, sV.v)
                        for c in range(NCH):
                            sl = slice(c * CHUNK, (c + 1) * CHUNK)
                            vi(v.tensor_single_scalar(sp2[:, sl], acum[:, sl],
                                                      acum[:, c * CHUNK + 255:c * CHUNK + 256],
                                                      ALU.subtract))
                    v_acum = sV.v

                    @blk.scalar
                    def _(sc):
                        sc.wait_ge(sV.h, v_acum)
                        si(sc.activation(sp2[:, :], sp2[:, :], AF.Exp, scale=-1.0))
                    v_dec = sS.v

                    @blk.vector
                    def _(v):
                        v.wait_ge(sS.h, v_dec)
                        vi(v.tensor_mul(dtdec[:, :], sp2[:, :], dtv[:, :]))
                    v_ddec = sV.v

                    @blk.sync
                    def _(s):
                        s.wait_ge(sV.h, v_ddec)
                        for name, dst in dump_outs.items():
                            if name in row_srcs:
                                DS["dmp"].dma(s, dst.ap(), row_srcs[name][:, :])

                    assert wst["st"] == len(wob_items), wst
                    flush_stores(blk, ["gst", "zst", "dmp", "gix", "win", "wol", "wos"])

        stU.close()

        if stage in ("b2", "b3", "full"):
            with ExitStack() as st2:
                bcx = st2.enter_context(nc.sbuf_tensor("bcx", [128, 260], BF16))
                bcc = st2.enter_context(nc.sbuf_tensor("bcc", [128, 256], BF16))
                cset = st2.enter_context(nc.sbuf_tensor("cset", [64, 256], BF16))
                colsb = st2.enter_context(nc.sbuf_tensor("colsb", [128, 192], F32))
                nscol = st2.enter_context(nc.sbuf_tensor("nscol", [128, 64], F32))
                btokb = st2.enter_context(nc.sbuf_tensor("btokb", [128, 128], BF16))
                bdb = st2.enter_context(nc.sbuf_tensor("bdb", [128, 256], BF16))
                cbts = st2.enter_context(nc.sbuf_tensor("cbts", [128, 2 * 256], BF16))
                a64b = st2.enter_context(nc.sbuf_tensor("a64b", [64, NH], F32))
                rsubb = st2.enter_context(nc.sbuf_tensor("rsubb", [128, 2 * 1024], F32))
                dmbb = st2.enter_context(nc.sbuf_tensor("dmbb", [128, 4 * 1024], F32))
                ebuf = st2.enter_context(nc.sbuf_tensor("ebuf", [128, 4 * 1024], BF16))
                slb = st2.enter_context(nc.sbuf_tensor("slb", [128, 4 * 1024], BF16))
                e2b = st2.enter_context(nc.sbuf_tensor("e2b", [64, 1024], BF16))
                cscb = st2.enter_context(nc.sbuf_tensor("cscb", [64, 1024], BF16))
                xbufs = st2.enter_context(nc.sbuf_tensor("xbufs", [128, 2 * 1040], BF16))
                xcs = st2.enter_context(nc.sbuf_tensor("xcs", [128, 2 * 1024], BF16))
                xtokb = st2.enter_context(nc.sbuf_tensor("xtokb", [128, 2 * 1024], BF16))
                zsb = st2.enter_context(nc.sbuf_tensor("zsb", [128, 2 * 1024], BF16))
                sqbb = st2.enter_context(nc.sbuf_tensor("sqbb", [128, 2 * 1024], BF16))
                ybuf = st2.enter_context(nc.sbuf_tensor("ybuf", [128, NH * CHUNK], BF16))
                prevf = st2.enter_context(nc.sbuf_tensor("prevf", [64, NH * 128], F32))
                prevbb = st2.enter_context(nc.sbuf_tensor("prevbb", [64, NH * 128], BF16))
                rstd1 = st2.enter_context(nc.sbuf_tensor("rstd1", [1, 256], F32))
                rbcb = st2.enter_context(nc.sbuf_tensor("rbcb", [128, 256], F32))
                psC = st2.enter_context(nc.psum_tensor("psC", [128, 512], F32))
                psY = psC
                psSt = st2.enter_context(nc.psum_tensor("psSt", [64, 512], F32))
                psX = st2.enter_context(nc.psum_tensor("psX", [128, 2 * 1024], BF16))
                psP4V = st2.enter_context(nc.psum_tensor("psP4V", [128, 448], F32))
                psP4 = psP4V[:, 0:192]
                psBt = st2.enter_context(nc.psum_tensor("psBt", [128, 128], BF16))
                galloc(st2, nsl=3, pslots=1)
                psV = psP4V[0:1, 192:448]

                with nc.Block() as blk:
                    @blk.vector
                    def _(v):
                        vi(v.memset(prevf[:, :], 0.0))
                        vi(v.memset(prevbb[:, :], 0.0))
                    prevcp = sV.v

                    acrl = []

                    @blk.gpsimd
                    def _(g):
                        for cc in range(NCH):
                            acrl.append(DS["acr"].dma(
                                g,
                                acr_s.ap()[cc:cc + 1, :].rearrange("x (p n) -> p (x n)", p=NH),
                                acum[:, cc * CHUNK:(cc + 1) * CHUNK]))

                    prev = {"endV": sV.v, "endS": sS.v, "endP": sP.v, "prevcp": prevcp}
                    for c in range(NCH):
                        R = {}

                        @blk.tensor
                        def _(t):
                            t.wait_ge(sV.h, prev["endV"])
                            t.wait_ge(sS.h, prev["endS"])
                            DS["cse"].flush(t)

                        @blk.vector
                        def _(v):
                            v.wait_ge(sS.h, prev["endS"])
                            v.wait_ge(sP.h, prev["endP"])
                            DS["cse"].flush(v)
                            DS["rbc"].flush(v)
                            if c == 0:
                                wt(v, v_paramsG)
                                wt(v, v_params)

                        @blk.scalar
                        def _(sc):
                            sc.wait_ge(sV.h, prev["endV"])
                            sc.wait_ge(sP.h, prev["endP"])
                            DS["yst"].flush(sc)
                            if c == 0:
                                wt(sc, v_paramsG)
                                wt(sc, v_params)

                        @blk.gpsimd
                        def _(g, c=c):
                            g.wait_ge(sV.h, prev["endV"])
                            DS["rs"].flush(g)
                            R["acr"] = acrl[c]

                        if c == 0:
                            @blk.gpsimd
                            def _(g):
                                gi(g.memset(bcx[:, 0:3], 0.0))
                                R["bcx0"] = sG.v

                        @blk.sync
                        def _(s, c=c):
                            s.wait_ge(sV.h, prev["endV"])
                            s.wait_ge(sP.h, prev["endP"])
                            s.wait_ge(sS.h, prev["endS"])
                            if c == 0:
                                DS["zst"].flush(s)
                                s.wait_ge(sG.h, R["bcx0"])
                                R["bcx"] = DS["bcx"].dma(
                                    s, bcx[:, 3:259],
                                    AP(zxb_s, (DIN + 4096) * T, [[T, 128], [1, 256]]))
                            else:
                                R["bcx"] = DS["bcx"].dma(
                                    s, bcx[:, 0:259],
                                    AP(zxb_s, (DIN + 4096) * T + c * CHUNK - 3,
                                       [[T, 128], [1, 259]]))

                        @blk.vector
                        def _(v, c=c):
                            wt(v, R["bcx"])
                            if c == 0:
                                v.wait_ge(sG.h, R["bcx0"])
                            gg = 32
                            v.wait_ge(sV.h, vi(v.tensor_scalar_mul(bcc[:, :], bcx[:, 0:256],
                                                                   cwcol[:, gg * 4:gg * 4 + 1])))
                            for j in range(1, 4):
                                v.wait_ge(sV.h, vi(v.scalar_tensor_tensor(
                                    bcc[:, :], bcx[:, j:j + 256], cwcol[:, gg * 4 + j:gg * 4 + j + 1],
                                    bcc[:, :], ALU.mult, ALU.add)))
                            R["bccv"] = sV.v

                        @blk.scalar
                        def _(sc, c=c):
                            sc.wait_ge(sV.h, R["bccv"])
                            si(sc.activation(bcc[:, :], bcc[:, :], AF.Silu, bias=cbcol[:, 32:33]))
                            R["bcc"] = sS.v

                        @blk.gpsimd
                        def _(g, c=c):
                            g.wait_ge(sS.h, R["bcc"])
                            R["cset"] = DS["cse"].dma(g, cset[:, :], bcc[64:128, :])

                        @blk.tensor
                        def _(t, c=c):
                            t.wait_ge(sV.h, prev["endV"])
                            for sb in range(2):
                                sl = slice(c * CHUNK + sb * 128, c * CHUNK + (sb + 1) * 128)
                                mm = t.transpose(psP4[:, sb * 32:(sb + 1) * 32], acum[:, sl], ident_f[0:32, 0:32])
                                mm = t.transpose(psP4[:, 64 + sb * 32:64 + (sb + 1) * 32], lndt[:, sl], ident_f[0:32, 0:32])
                                mm = t.transpose(psP4[:, 128 + sb * 32:128 + (sb + 1) * 32], dtdec[:, sl], ident_f[0:32, 0:32])
                            pi(mm)
                            R["p4"] = sP.v

                        @blk.vector
                        def _(v, c=c):
                            v.wait_ge(sP.h, R["p4"])
                            v.wait_ge(sV.h, vi(v.tensor_copy(colsb[:, :], psP4[:, :])))
                            vi(v.tensor_sub(nscol[:, :], colsb[:, 64:128], colsb[:, 0:64]))
                            R["cols"] = sV.v

                        @blk.tensor
                        def _(t, c=c):
                            t.wait_ge(sS.h, R["bcc"])
                            for sb in range(2):
                                mm = t.transpose(psBt[:, sb * 64:(sb + 1) * 64],
                                                 bcc[0:64, sb * 128:(sb + 1) * 128], ident_b[0:64, 0:64])
                            pi(mm)
                            R["btokp"] = sP.v

                        @blk.vector
                        def _(v, c=c):
                            v.wait_ge(sP.h, R["btokp"])
                            vi(v.tensor_copy(btokb[:, :], psBt[:, :]))
                            R["btok"] = sV.v
                            g_tree_ops(v, 2)

                        @blk.tensor
                        def _(t, c=c):
                            wt(t, R["cset"])
                            for sb in range(2):
                                mm = t.matmul(psC[:, sb * 256:(sb + 1) * 256],
                                              bcc[0:64, sb * 128:(sb + 1) * 128],
                                              cset[:, :], start=True, stop=True)
                            pi(mm)
                            R["cbtp"] = sP.v

                        @blk.scalar
                        def _(sc, c=c):
                            sc.wait_ge(sP.h, R["cbtp"])
                            si(sc.copy(cbts[:, :], psC[:, :]))
                            R["cbte"] = sS.v

                        @blk.vector
                        def _(v, c=c):
                            v.wait_ge(sS.h, R["cbte"])
                            for sb in range(2):
                                vi(v.copy_predicated(cbts[:, sb * 256:(sb + 1) * 256],
                                                     umask[:, sb * 256:(sb + 1) * 256], zero_b[:, :]))
                            v.wait_ge(sV.h, sV.v)
                            R["cbt"] = sV.v
                            g_tree_ops(v, 2)

                        SR = []
                        for su in range(8):
                            Rs = {}
                            par = su % 2

                            if c == 0:
                                @blk.gpsimd
                                def _(g, su=su, par=par):
                                    if su >= 2:
                                        g.wait_ge(sV.h, SR[su - 2]["free"])
                                    gi(g.memset(xbufs[:, par * 1040:par * 1040 + 1040]
                                                .rearrange("p (g w) -> p g w", g=4)[:, :, 0:3], 0.0))
                                    Rs["xz"] = sG.v

                            @blk.sync
                            def _(s, c=c, su=su, par=par):
                                if su >= 2:
                                    s.wait_ge(sV.h, SR[su - 2]["free"])
                                    s.wait_ge(sS.h, SR[su - 2]["a64"])
                                if c == 0:
                                    s.wait_ge(sG.h, Rs["xz"])
                                    Rs["xload"] = DS["xl"].dma(
                                        s, xbufs[:, par * 1040:par * 1040 + 1040]
                                        .rearrange("p (g w) -> p g w", g=4)[:, :, 3:259],
                                        AP(zxb_s, (DIN + su * 4 * 128) * T,
                                           [[T, 128], [128 * T, 4], [1, 256]]))
                                else:
                                    Rs["xload"] = DS["xl"].dma(
                                        s, xbufs[:, par * 1040:par * 1040 + 1040]
                                        .rearrange("p (g w) -> p g w", g=4)[:, :, 0:259],
                                        AP(zxb_s, (DIN + su * 4 * 128) * T + c * CHUNK - 3,
                                           [[T, 128], [128 * T, 4], [1, 259]]))
                                Rs["zload"] = DS["zl"].dma(
                                    s, zsb[:, par * 1024:(par + 1) * 1024]
                                    .rearrange("p (g w) -> p g w", g=4),
                                    AP(zxb_s, (su * 4 * 128) * T + c * CHUNK,
                                       [[T, 128], [128 * T, 4], [1, 256]]))
                                wt(s, R["acr"])
                                Rs["rload"] = DS["rs"].dma(
                                    s, rsubb[:, par * 1024:(par + 1) * 1024],
                                    AP(acr_s, c * NH * CHUNK + su * 1024, [[0, 128], [1, 1024]]))

                            @blk.vector
                            def _(v, c=c, su=su, par=par):
                                wt(v, Rs["xload"])
                                if su >= 2:
                                    v.wait_ge(sP.h, SR[su - 2]["xtokp"])
                                for gl in range(4):
                                    gg = su * 4 + gl
                                    xb = xbufs[:, par * 1040 + gl * 260:par * 1040 + gl * 260 + 260]
                                    xo = xcs[:, par * 1024 + gl * 256:par * 1024 + (gl + 1) * 256]
                                    v.wait_ge(sV.h, vi(v.tensor_scalar_mul(xo, xb[:, 0:256],
                                                                           cwcol[:, gg * 4:gg * 4 + 1])))
                                    for j in range(1, 4):
                                        v.wait_ge(sV.h, vi(v.scalar_tensor_tensor(
                                            xo, xb[:, j:j + 256], cwcol[:, gg * 4 + j:gg * 4 + j + 1],
                                            xo, ALU.mult, ALU.add)))
                                Rs["xconv"] = sV.v
                                g_tree_ops(v, 2)

                            @blk.scalar
                            def _(sc, su=su, par=par):
                                sc.wait_ge(sV.h, Rs["xconv"])
                                for gl in range(4):
                                    gg = su * 4 + gl
                                    xo = xcs[:, par * 1024 + gl * 256:par * 1024 + (gl + 1) * 256]
                                    si(sc.activation(xo, xo, AF.Silu, bias=cbcol[:, gg:gg + 1]))
                                Rs["xsilu"] = sS.v
                                wt(sc, Rs["zload"])
                                if su >= 2:
                                    sc.wait_ge(sV.h, SR[su - 2]["gate"])
                                si(sc.activation(zsb[:, par * 1024:(par + 1) * 1024],
                                                 zsb[:, par * 1024:(par + 1) * 1024], AF.Silu))
                                Rs["zsilu"] = sS.v

                            for sb in range(2):
                                dpar = (su % 2) * 2 + sb

                                @blk.vector
                                def _(v, c=c, su=su, sb=sb, par=par, dpar=dpar):
                                    wt(v, Rs["rload"])
                                    v.wait_ge(sV.h, R["cols"])
                                    if su >= 2:
                                        v.wait_ge(sS.h, SR[su - 2]["E%d" % sb])
                                    for hl in range(4):
                                        h = su * 4 + hl
                                        vi(v.tensor_scalar(
                                            dmbb[:, dpar * 1024 + hl * 256:dpar * 1024 + (hl + 1) * 256],
                                            rsubb[:, par * 1024 + hl * 256:par * 1024 + (hl + 1) * 256],
                                            nscol[:, sb * 32 + h:sb * 32 + h + 1], 20.0,
                                            ALU.add, ALU.min))
                                    v.wait_ge(sV.h, sV.v)
                                    Rs["dm%d" % sb] = sV.v

                            @blk.scalar
                            def _(sc, su=su, par=par):
                                for sb in range(2):
                                    dpar = (su % 2) * 2 + sb
                                    sc.wait_ge(sV.h, Rs["dm%d" % sb])
                                    if su >= 2:
                                        sc.wait_ge(sV.h, SR[su - 2]["SL%d" % sb])
                                    si(sc.activation(ebuf[:, dpar * 1024:(dpar + 1) * 1024],
                                                     dmbb[:, dpar * 1024:(dpar + 1) * 1024], AF.Exp))
                                    Rs["E%d" % sb] = sS.v
                                wt(sc, Rs["rload"])
                                if su >= 1:
                                    sc.wait_ge(sV.h, SR[su - 1]["csc"])
                                si(sc.activation(e2b[:, :],
                                                 rsubb[0:64, par * 1024:(par + 1) * 1024], AF.Exp))
                                Rs["E2"] = sS.v
                                si(sc.activation(a64b[:, su * 4:(su + 1) * 4],
                                                 AP(rsubb, par * 1024 + 255, [[2048, 64], [256, 4]]),
                                                 AF.Exp))
                                Rs["a64"] = sS.v

                            for sb in range(2):
                                dpar = (su % 2) * 2 + sb

                                @blk.vector
                                def _(v, su=su, sb=sb, par=par, dpar=dpar):
                                    v.wait_ge(sS.h, Rs["E%d" % sb])
                                    v.wait_ge(sV.h, R["cbt"])
                                    if su >= 2:
                                        v.wait_ge(sP.h, SR[su - 2]["ymm"])
                                    vi(v.tensor_tensor(
                                        slb[:, (sb * 2 + par) * 1024:(sb * 2 + par + 1) * 1024],
                                        ebuf[:, dpar * 1024:(dpar + 1) * 1024],
                                        AP(cbts.handle if hasattr(cbts, "handle") else cbts,
                                           sb * 256, [[512, 128], [0, 4], [1, 256]]),
                                        ALU.mult))
                                    v.wait_ge(sV.h, sV.v)
                                    Rs["SL%d" % sb] = sV.v
                                    g_tree_ops(v, 1)

                            @blk.tensor
                            def _(t, su=su, par=par):
                                t.wait_ge(sS.h, Rs["xsilu"])
                                if su >= 2:
                                    t.wait_ge(sS.h, SR[su - 2]["xtoke"])
                                for gl in range(4):
                                    for sb in range(2):
                                        mm = t.transpose(
                                            psX[:, par * 1024 + (gl * 2 + sb) * 128:par * 1024 + (gl * 2 + sb + 1) * 128],
                                            xcs[:, par * 1024 + gl * 256 + sb * 128:par * 1024 + gl * 256 + (sb + 1) * 128],
                                            ident_b[:, :])
                                pi(mm)
                                Rs["xtokp"] = sP.v

                            @blk.scalar
                            def _(sc, su=su, par=par):
                                sc.wait_ge(sP.h, Rs["xtokp"])
                                si(sc.copy(xtokb[:, par * 1024:(par + 1) * 1024],
                                           psX[:, par * 1024:(par + 1) * 1024]))
                                Rs["xtoke"] = sS.v

                            @blk.vector
                            def _(v, su=su, par=par):
                                v.wait_ge(sS.h, Rs["E2"])
                                wt(v, R["cset"])
                                if su >= 1:
                                    v.wait_ge(sP.h, SR[su - 1]["ymm"])
                                vi(v.tensor_tensor(
                                    cscb[:, :],
                                    e2b[:, :],
                                    AP(cset, 0, [[256, 64], [0, 4], [1, 256]]),
                                    ALU.mult))
                                v.wait_ge(sV.h, sV.v)
                                Rs["csc"] = sV.v
                                g_tree_ops(v, 2)

                            for hl in range(4):
                                h = su * 4 + hl

                                @blk.vector
                                def _(v, su=su, hl=hl, h=h, par=par):
                                    v.wait_ge(sV.h, R["btok"])
                                    if su >= 1 and hl == 0:
                                        v.wait_ge(sV.h, SR[su - 1]["upd"])
                                    if hl >= 2:
                                        v.wait_ge(sP.h, Rs["stmm%d" % (hl - 2)])
                                    for sb in range(2):
                                        vi(v.tensor_scalar_mul(
                                            bdb[:, (hl % 2) * 128 + sb * 64:(hl % 2) * 128 + (sb + 1) * 64],
                                            btokb[:, sb * 64:(sb + 1) * 64],
                                            colsb[:, 128 + sb * 32 + h:128 + sb * 32 + h + 1]))
                                    v.wait_ge(sV.h, sV.v)
                                    Rs["bd%d" % hl] = sV.v

                                @blk.tensor
                                def _(t, su=su, hl=hl, h=h, par=par):
                                    t.wait_ge(sV.h, Rs["bd%d" % hl])
                                    t.wait_ge(sS.h, Rs["xtoke"])
                                    if su >= 1 and hl == 0:
                                        t.wait_ge(sV.h, SR[su - 1]["upd"])
                                    for sb in range(2):
                                        mm = t.matmul(psSt[:, hl * 128:(hl + 1) * 128],
                                                      bdb[:, (hl % 2) * 128 + sb * 64:(hl % 2) * 128 + (sb + 1) * 64],
                                                      xtokb[:, par * 1024 + (hl * 2 + sb) * 128:par * 1024 + (hl * 2 + sb + 1) * 128],
                                                      start=(sb == 0), stop=(sb == 1))
                                    pi(mm)
                                    Rs["stmm%d" % hl] = sP.v

                            for hl in range(4):
                                h = su * 4 + hl

                                @blk.vector
                                def _(v, su=su, hl=hl, h=h):
                                    v.wait_ge(sP.h, Rs["stmm3"])
                                    v.wait_ge(sS.h, Rs["a64"])
                                    vi(v.scalar_tensor_tensor(
                                        prevf[:, h * 128:(h + 1) * 128],
                                        prevf[:, h * 128:(h + 1) * 128],
                                        a64b[:, h:h + 1],
                                        psSt[:, hl * 128:(hl + 1) * 128], ALU.mult, ALU.add))
                                    v.wait_ge(sV.h, sV.v)
                                    if hl == 3:
                                        Rs["upd"] = sV.v
                                    if hl % 2 == 1:
                                        g_tree_ops(v, 1)

                            for pr in range(2):
                                for hl2 in range(2):
                                    hl = pr * 2 + hl2
                                    h = su * 4 + hl

                                    @blk.tensor
                                    def _(t, su=su, pr=pr, hl=hl, h=h, par=par, c=c):
                                        t.wait_ge(sV.h, Rs["SL0"])
                                        t.wait_ge(sV.h, Rs["SL1"])
                                        t.wait_ge(sV.h, Rs["csc"])
                                        t.wait_ge(sV.h, prev["prevcp"])
                                        if pr == 1 and "yev0" in Rs:
                                            pass
                                        if hl == 0 or hl == 2:
                                            if pr == 1:
                                                t.wait_ge(sS.h, Rs["yev0"])
                                            elif su >= 1:
                                                t.wait_ge(sS.h, SR[su - 1]["yev1"])
                                        mm = t.matmul(psY[:, hl2 * 256:(hl2 + 1) * 256],
                                                      xtokb[:, par * 1024 + (hl * 2) * 128:par * 1024 + (hl * 2 + 1) * 128],
                                                      slb[:, (0 * 2 + par) * 1024 + hl * 256:(0 * 2 + par) * 1024 + (hl + 1) * 256],
                                                      start=True, stop=False)
                                        mm = t.matmul(psY[:, hl2 * 256:(hl2 + 1) * 256],
                                                      xtokb[:, par * 1024 + (hl * 2 + 1) * 128:par * 1024 + (hl * 2 + 2) * 128],
                                                      slb[:, (1 * 2 + par) * 1024 + hl * 256:(1 * 2 + par) * 1024 + (hl + 1) * 256],
                                                      start=False, stop=False)
                                        mm = t.matmul(psY[:, hl2 * 256:(hl2 + 1) * 256],
                                                      prevbb[:, h * 128:(h + 1) * 128],
                                                      cscb[:, hl * 256:(hl + 1) * 256],
                                                      start=False, stop=True)
                                        pi(mm)
                                        Rs["ymm%d" % hl] = sP.v

                                @blk.scalar
                                def _(sc, su=su, pr=pr):
                                    sc.wait_ge(sP.h, Rs["ymm%d" % (pr * 2 + 1)])
                                    si(sc.copy(ybuf[:, (su * 4 + pr * 2) * 256:(su * 4 + pr * 2 + 2) * 256],
                                               psY[:, :]))
                                    Rs["yev%d" % pr] = sS.v

                                for hl2 in range(2):
                                    hl = pr * 2 + hl2
                                    h = su * 4 + hl

                                    @blk.vector
                                    def _(v, su=su, hl=hl, h=h, par=par, pr=pr):
                                        v.wait_ge(sS.h, Rs["yev%d" % pr])
                                        vi(v.scalar_tensor_tensor(
                                            ybuf[:, h * 256:(h + 1) * 256],
                                            xcs[:, par * 1024 + hl * 256:par * 1024 + (hl + 1) * 256],
                                            dcol128[:, h:h + 1],
                                            ybuf[:, h * 256:(h + 1) * 256], ALU.mult, ALU.add))
                                        v.wait_ge(sV.h, sV.v)
                                        Rs["dsk%d" % hl] = sV.v

                            Rs["ymm"] = Rs["ymm3"]

                            @blk.vector
                            def _(v, su=su, par=par):
                                v.wait_ge(sS.h, Rs["zsilu"])
                                for hl in range(4):
                                    v.wait_ge(sV.h, Rs["dsk%d" % hl])
                                vi(v.tensor_mul(ybuf[:, su * 1024:(su + 1) * 1024],
                                                ybuf[:, su * 1024:(su + 1) * 1024],
                                                zsb[:, par * 1024:(par + 1) * 1024]))
                                v.wait_ge(sV.h, sV.v)
                                Rs["gate"] = sV.v
                                Rs["free"] = sV.v
                                if su >= 2:
                                    v.wait_ge(sP.h, SR[su - 2]["vmm"])
                                vi(v.tensor_mul(sqbb[:, (su % 2) * 1024:(su % 2 + 1) * 1024],
                                                ybuf[:, su * 1024:(su + 1) * 1024],
                                                ybuf[:, su * 1024:(su + 1) * 1024]))
                                v.wait_ge(sV.h, sV.v)
                                Rs["sq"] = sV.v
                                g_tree_ops(v, 3)

                            @blk.tensor
                            def _(t, su=su, par=par, c=c):
                                t.wait_ge(sV.h, Rs["sq"])
                                for hl in range(4):
                                    mm = t.matmul(psV[:, :], onecol[:, :],
                                                  sqbb[:, (su % 2) * 1024 + hl * 256:
                                                       (su % 2) * 1024 + (hl + 1) * 256],
                                                  start=(su == 0 and hl == 0),
                                                  stop=(su == 7 and hl == 3))
                                pi(mm)
                                Rs["vmm"] = sP.v

                            SR.append(Rs)

                            @blk.gpsimd
                            def _(g):
                                g_load(g, 1)

                            @blk.tensor
                            def _(t):
                                g_tp(t, 1)

                            @blk.scalar
                            def _(sc):
                                g_store(sc, 1)

                            @blk.vector
                            def _(v):
                                g_evac(v, 1)

                        @blk.scalar
                        def _(sc, c=c):
                            sc.wait_ge(sP.h, SR[7]["vmm"])
                            sc.wait_ge(sS.h, si(sc.activation(rstd1[:, :], psV[:, :], AF.Ln,
                                                              bias=ceps[0:1, :], scale=cidin[0:1, :])))
                            si(sc.activation(rstd1[:, :], rstd1[:, :], AF.Exp, scale=chalf[0:1, :]))
                            R["rstd"] = sS.v

                        @blk.gpsimd
                        def _(g, c=c):
                            g.wait_ge(sS.h, R["rstd"])
                            R["rbounce"] = DS["rb2"].dma(g, row_s[0:1, 0:256], rstd1[:, :])

                        @blk.sync
                        def _(s, c=c):
                            wt(s, R["rbounce"])
                            R["rbc"] = DS["rbc"].dma(s, rbcb[:, :], AP(row_s, 0, [[0, 128], [1, 256]]))

                        @blk.vector
                        def _(v, c=c):
                            wt(v, R["rbc"])
                            vi(v.tensor_tensor(ybuf[:, :], ybuf[:, :],
                                               AP(rbcb, 0, [[256, 128], [0, 32], [1, 256]]),
                                               ALU.mult))
                            v.wait_ge(sV.h, sV.v)
                            for gg in range(NKI):
                                vi(v.tensor_scalar_mul(ybuf[:, gg * 256:(gg + 1) * 256],
                                                       ybuf[:, gg * 256:(gg + 1) * 256],
                                                       nwcol[:, gg:gg + 1]))
                            v.wait_ge(sV.h, sV.v)
                            R["ynorm"] = sV.v

                        @blk.vector
                        def _(v, c=c):
                            v.wait_ge(sP.h, SR[7]["ymm3"])
                            for q in range(4):
                                vi(v.tensor_copy(prevbb[:, q * 1024:(q + 1) * 1024],
                                                 prevf[:, q * 1024:(q + 1) * 1024]))
                            v.wait_ge(sV.h, sV.v)
                            prev["prevcp"] = sV.v

                        @blk.sync
                        def _(s, c=c):
                            s.wait_ge(sV.h, R["ynorm"])
                            for q in range(4):
                                DS["yst"].dma(
                                    s, AP(ynorm_s, (q * 8 * 128) * T + c * CHUNK,
                                          [[T, 128], [128 * T, 8], [1, 256]]),
                                    ybuf[:, q * 2048:(q + 1) * 2048]
                                    .rearrange("p (g w) -> p g w", g=8))

                        prev["endV"] = sV.v
                        prev["endS"] = sS.v
                        prev["endP"] = sP.v

                    g_drain(blk)
                    flush_stores(blk, ["yst", "rb2", "acr", "cse",
                                       "bcx", "xl", "zl", "rs", "rbc", "gst", "gix"])

        stDT.close()
        stX = ExitStack()
        x1T = stX.enter_context(nc.sbuf_tensor("x1T", [128, NKD * T], BF16))

        if stage in ("b3", "full"):
            with ExitStack() as st3:
                ynsl = st3.enter_context(nc.sbuf_tensor("ynsl", [128, NKI * 512], BF16))
                rsl = st3.enter_context(nc.sbuf_tensor("rsl", [128, NKD * 512], F32))
                wobm = st3.enter_context(nc.sbuf_tensor("wobm", [128, 4 * 4096], BF16))
                ubuf = st3.enter_context(nc.sbuf_tensor("ubuf", [128, 2 * 512], F32))
                rbsl = st3.enter_context(nc.sbuf_tensor("rbsl", [128, 2 * 512], BF16))
                sqsl = st3.enter_context(nc.sbuf_tensor("sqsl", [128, 2 * 512], BF16))
                rowsb = st3.enter_context(nc.sbuf_tensor("rowsb", [1, 4 * 512], F32))
                rnb = st3.enter_context(nc.sbuf_tensor("rnb", [128, 512], F32))
                rsb = st3.enter_context(nc.sbuf_tensor("rsb", [128, 512], F32))
                tmpn = st3.enter_context(nc.sbuf_tensor("tmpn", [128, 512], F32))
                x1stg = st3.enter_context(nc.sbuf_tensor("x1stg", [128, 2 * 512], F32))
                psO = st3.enter_context(nc.psum_tensor("psO", [128, 4 * 512], F32))
                psS3 = st3.enter_context(nc.psum_tensor("psS3", [1, 1024], F32))

                with nc.Block() as blk:
                    @blk.sync
                    def _(s):
                        DS["yst"].flush(s)
                        DS["ust"].flush(s)
                        DS["wos"].flush(s)

                    x1st_v = []
                    for tp in range(4):
                        RTP = RT_prev if tp else None
                        RT = {}

                        @blk.sync
                        def _(s, tp=tp):
                            if tp >= 1:
                                s.wait_ge(sP.h, RT_prev["mm%d" % (NKD - 1)])
                            toks = []
                            for q in range(4):
                                toks.append(DS["ynl"].dma(
                                    s, ynsl[:, q * 4096:(q + 1) * 4096]
                                    .rearrange("p (g w) -> p g w", g=8),
                                    AP(ynorm_s, (q * 8 * 128) * T + tp * 512,
                                       [[T, 128], [128 * T, 8], [1, 512]])))
                            RT["yn"] = toks

                        mev = []
                        for m in range(NKD):
                            mpar = m % 2

                            @blk.sync
                            def _(s, tp=tp, m=m, mpar=mpar):
                                if m >= 4:
                                    s.wait_ge(sP.h, mev[m - 4][0])
                                elif tp >= 1:
                                    s.wait_ge(sP.h, RTP["mm%d" % (NKD - 4 + m)])
                                if m >= 2:
                                    s.wait_ge(sV.h, mev[m - 2][1])
                                elif tp >= 1:
                                    s.wait_ge(sV.h, RTP["rb%d" % (NKD - 2 + m)])
                                RT["w%d" % m] = DS["wom"].dma(
                                    s, wobm[:, (m % 4) * 4096:(m % 4 + 1) * 4096],
                                    AP(wob_s, m * 128 * DIN, [[DIN, 128], [1, 4096]]))
                                utoks = []
                                for ut in range(4):
                                    utoks.append(DS["ub"].dma(
                                        s, ubuf[:, mpar * 512 + ut * 128:mpar * 512 + (ut + 1) * 128],
                                        AP(uT_s, (4 * tp + ut) * 128 * DM + m * 128,
                                           [[DM, 128], [1, 128]])))
                                RT["u%d" % m] = utoks

                            @blk.tensor
                            def _(t, tp=tp, m=m, mpar=mpar):
                                wt(t, RT["w%d" % m])
                                for tok in RT["yn"]:
                                    wt(t, tok)
                                if m >= 4:
                                    t.wait_ge(sV.h, mev[m - 4][1])
                                elif tp >= 1:
                                    t.wait_ge(sV.h, RTP["rb%d" % (NKD - 4 + m)])
                                for k in range(NKI):
                                    mm = t.matmul(psO[:, (m % 4) * 512:(m % 4 + 1) * 512],
                                                  wobm[:, (m % 4) * 4096 + k * 128:(m % 4) * 4096 + (k + 1) * 128],
                                                  ynsl[:, k * 512:(k + 1) * 512],
                                                  start=(k == 0), stop=(k == NKI - 1))
                                pi(mm)
                                RT["mm%d" % m] = sP.v

                            @blk.vector
                            def _(v, tp=tp, m=m, mpar=mpar):
                                v.wait_ge(sP.h, RT["mm%d" % m])
                                for tok in RT["u%d" % m]:
                                    wt(v, tok)
                                if m >= 2:
                                    v.wait_ge(sP.h, RT["st%d" % (m - 2)])
                                elif tp >= 1:
                                    v.wait_ge(sP.h, RTP["st%d" % (NKD - 2 + m)])
                                v.wait_ge(sV.h, vi(v.tensor_add(rsl[:, m * 512:(m + 1) * 512],
                                                                psO[:, (m % 4) * 512:(m % 4 + 1) * 512],
                                                                ubuf[:, mpar * 512:(mpar + 1) * 512])))
                                vi(v.tensor_copy(rbsl[:, mpar * 512:(mpar + 1) * 512],
                                                 rsl[:, m * 512:(m + 1) * 512]))
                                v.wait_ge(sV.h, sV.v)
                                RT["rb%d" % m] = sV.v

                            @blk.scalar
                            def _(sc, tp=tp, m=m, mpar=mpar):
                                sc.wait_ge(sV.h, RT["rb%d" % m])
                                si(sc.activation(sqsl[:, mpar * 512:(mpar + 1) * 512],
                                                 rbsl[:, mpar * 512:(mpar + 1) * 512], AF.Square))
                                RT["sq%d" % m] = sS.v

                            @blk.tensor
                            def _(t, tp=tp, m=m, mpar=mpar):
                                t.wait_ge(sS.h, RT["sq%d" % m])
                                if m == 0 and tp >= 1:
                                    t.wait_ge(sV.h, RT_prev["var"])
                                mm = t.matmul(psS3[:, 0:512], onecol[:, :],
                                              rbsl[:, mpar * 512:(mpar + 1) * 512],
                                              start=(m == 0), stop=(m == NKD - 1))
                                mm = t.matmul(psS3[:, 512:1024], onecol[:, :],
                                              sqsl[:, mpar * 512:(mpar + 1) * 512],
                                              start=(m == 0), stop=(m == NKD - 1))
                                pi(mm)
                                mev.append((sP.v, RT["rb%d" % m]))
                                RT["st%d" % m] = sP.v

                            RT_prev = RT

                        @blk.vector
                        def _(v, tp=tp):
                            v.wait_ge(sP.h, RT["st%d" % (NKD - 1)])
                            v.wait_ge(sV.h, vi(v.tensor_scalar_mul(rowsb[:, 0:512], psS3[:, 0:512], -1.0 / DM)))
                            v.wait_ge(sV.h, vi(v.tensor_scalar_mul(rowsb[:, 512:1024], psS3[:, 512:1024], 1.0 / DM)))
                            v.wait_ge(sV.h, vi(v.tensor_mul(rowsb[:, 1024:1536], rowsb[:, 0:512], rowsb[:, 0:512])))
                            v.wait_ge(sV.h, vi(v.tensor_sub(rowsb[:, 1024:1536], rowsb[:, 512:1024], rowsb[:, 1024:1536])))
                            RT["var"] = sV.v

                        @blk.scalar
                        def _(sc, tp=tp):
                            sc.wait_ge(sV.h, RT["var"])
                            sc.wait_ge(sS.h, si(sc.activation(rowsb[:, 1536:2048], rowsb[:, 1024:1536],
                                                              AF.Ln, bias=ceps[0:1, :])))
                            si(sc.activation(rowsb[:, 1536:2048], rowsb[:, 1536:2048], AF.Exp, scale=chalf[0:1, :]))
                            RT["rstd"] = sS.v

                        @blk.gpsimd
                        def _(g, tp=tp):
                            g.wait_ge(sS.h, RT["rstd"])
                            t1 = DS["r3b"].dma(g, row_s[1:2, 0:512], rowsb[:, 0:512])
                            t2 = DS["r3b"].dma(g, row_s[2:3, 0:512], rowsb[:, 1536:2048])
                            RT["rowb"] = [t1, t2]

                        @blk.sync
                        def _(s, tp=tp):
                            for tok in RT["rowb"]:
                                wt(s, tok)
                            t1 = DS["bc3"].dma(s, rnb[:, :], AP(row_s, 2048, [[0, 128], [1, 512]]))
                            t2 = DS["bc3"].dma(s, rsb[:, :], AP(row_s, 4096, [[0, 128], [1, 512]]))
                            RT["bc"] = [t1, t2]

                        for m in range(NKD):
                            mpar = m % 2

                            @blk.vector
                            def _(v, tp=tp, m=m, mpar=mpar):
                                for tok in RT["bc"]:
                                    wt(v, tok)
                                v.wait_ge(sV.h, vi(v.tensor_add(tmpn[:, :], rsl[:, m * 512:(m + 1) * 512], rnb[:, :])))
                                gidx_st = tp * NKD + m - 2
                                if gidx_st >= 0:
                                    wt(v, x1st_v[gidx_st])
                                v.wait_ge(sV.h, vi(v.tensor_mul(x1stg[:, mpar * 512:(mpar + 1) * 512],
                                                                tmpn[:, :], rsb[:, :])))
                                vi(v.tensor_copy(x1T[:, m * T + tp * 512:m * T + (tp + 1) * 512],
                                                 x1stg[:, mpar * 512:(mpar + 1) * 512]))
                                v.wait_ge(sV.h, sV.v)
                                RT["x1%d" % m] = sV.v

                            @blk.sync
                            def _(s, tp=tp, m=m, mpar=mpar):
                                s.wait_ge(sV.h, RT["x1%d" % m])
                                x1st_v.append(DS["x1s"].dma(
                                    s, AP(x1f_s, m * 128 * T + tp * 512, [[T, 128], [1, 512]]),
                                    x1stg[:, mpar * 512:(mpar + 1) * 512]))

                        RT["x1done"] = RT["x1%d" % (NKD - 1)]
                        RT_prev = RT
                    flush_stores(blk, ["x1s", "r3b", "ynl", "wom", "ub", "bc3"])

        if stage == "full":
            with ExitStack() as st4:
                wgb = st4.enter_context(nc.sbuf_tensor("wgb", [128, 2 * 2048], BF16))
                wub = st4.enter_context(nc.sbuf_tensor("wub", [128, 2 * 2048], BF16))
                sgb = st4.enter_context(nc.sbuf_tensor("sgb", [128, 2 * 512], BF16))
                hstg = st4.enter_context(nc.sbuf_tensor("hstg", [128, 2 * 512], BF16))
                psG = st4.enter_context(nc.psum_tensor("psG", [128, 2 * 512], F32))
                psU = st4.enter_context(nc.psum_tensor("psU", [128, 2 * 512], F32))
                galloc(st4)

                with nc.Block() as blk:
                    wload = []
                    hmul_v = {}
                    hst_v = {}
                    for m in range(NMO):
                        @blk.sync
                        def _(s, m=m):
                            if m >= 2:
                                s.wait_ge(sP.h, mm_last[m - 2])
                            wt(s, wg_store_v[m])
                            wt(s, wu_store_v[m])
                            t1 = DS["wg"].dma(s, wgb[:, (m % 2) * 2048:(m % 2 + 1) * 2048],
                                              AP(wgT_s, m * 128 * DM, [[DM, 128], [1, 2048]]))
                            t2 = DS["wu"].dma(s, wub[:, (m % 2) * 2048:(m % 2 + 1) * 2048],
                                              AP(wuT_s, m * 128 * DM, [[DM, 128], [1, 2048]]))
                            wload.append([t1, t2])

                        mm_last = mm_last if m else {}
                        for tt in range(4):
                            tpar = tt % 2

                            @blk.tensor
                            def _(t, m=m, tt=tt, tpar=tpar):
                                for tok in wload[m]:
                                    wt(t, tok)
                                if (m, tt) != (0, 0):
                                    pm, ptt = (m, tt - 2) if tt >= 2 else (m - 1, tt + 2)
                                    if ptt >= 0 and (pm, ptt) in hmul_v:
                                        t.wait_ge(sV.h, hmul_v[(pm, ptt)])
                                for k in range(NKD):
                                    mm = t.matmul(psG[:, tpar * 512:(tpar + 1) * 512],
                                                  wgb[:, (m % 2) * 2048 + k * 128:(m % 2) * 2048 + (k + 1) * 128],
                                                  x1T[:, k * T + tt * 512:k * T + (tt + 1) * 512],
                                                  start=(k == 0), stop=(k == NKD - 1))
                                for k in range(NKD):
                                    mm = t.matmul(psU[:, tpar * 512:(tpar + 1) * 512],
                                                  wub[:, (m % 2) * 2048 + k * 128:(m % 2) * 2048 + (k + 1) * 128],
                                                  x1T[:, k * T + tt * 512:k * T + (tt + 1) * 512],
                                                  start=(k == 0), stop=(k == NKD - 1))
                                pi(mm)
                                mm_last[m] = sP.v
                                hmul_v[(m, tt, "mm")] = sP.v

                            @blk.scalar
                            def _(sc, m=m, tt=tt, tpar=tpar):
                                sc.wait_ge(sP.h, hmul_v[(m, tt, "mm")])
                                pm, ptt = (m, tt - 2) if tt >= 2 else (m - 1, tt + 2)
                                if (pm, ptt) in hmul_v:
                                    sc.wait_ge(sV.h, hmul_v[(pm, ptt)])
                                si(sc.activation(sgb[:, tpar * 512:(tpar + 1) * 512],
                                                 psG[:, tpar * 512:(tpar + 1) * 512], AF.Silu))
                                hmul_v[(m, tt, "sg")] = sS.v

                            @blk.vector
                            def _(v, m=m, tt=tt, tpar=tpar):
                                v.wait_ge(sS.h, hmul_v[(m, tt, "sg")])
                                pm, ptt = (m, tt - 2) if tt >= 2 else (m - 1, tt + 2)
                                if (pm, ptt) in hst_v:
                                    wt(v, hst_v[(pm, ptt)])
                                vi(v.tensor_mul(hstg[:, tpar * 512:(tpar + 1) * 512],
                                                sgb[:, tpar * 512:(tpar + 1) * 512],
                                                psU[:, tpar * 512:(tpar + 1) * 512]))
                                v.wait_ge(sV.h, sV.v)
                                hmul_v[(m, tt)] = sV.v
                                g_tree_ops(v, 6)

                            @blk.sync
                            def _(s, m=m, tt=tt, tpar=tpar):
                                s.wait_ge(sV.h, hmul_v[(m, tt)])
                                hst_v[(m, tt)] = DS["hst"].dma(
                                    s, AP(h_s, m * 128 * T + tt * 512, [[T, 128], [1, 512]]),
                                    hstg[:, tpar * 512:(tpar + 1) * 512])

                        @blk.gpsimd
                        def _(g):
                            g_load(g, 2)

                        @blk.tensor
                        def _(t):
                            g_tp(t, 2)

                        @blk.scalar
                        def _(sc):
                            g_store(sc, 2)

                        @blk.vector
                        def _(v):
                            g_evac(v, 2)
                    g_drain(blk, all_tiles=True)
                    flush_stores(blk, ["hst", "gst", "gix", "wg", "wu"])

            stX.close()
            stR = ExitStack()
            r2x = stR.enter_context(nc.sbuf_tensor("r2x", [128, NKD * T], F32))
            with ExitStack() as st5:
                with nc.Block() as blk:
                    @blk.sync
                    def _(s):
                        DS["hst"].flush(s)
                        DS["x1s"].flush(s)
                        DS["gst"].flush(s)

                    with (
                        nc.sbuf_tensor("wdb", [128, 4 * INTER], BF16) as wdb,
                        nc.sbuf_tensor("hbb", [128, 8 * 512], BF16) as hbb,
                        nc.sbuf_tensor("x1fb", [128, 2 * 512], F32) as x1fb,
                        nc.psum_tensor("psD", [128, 8 * 512], F32) as psD,
                    ):
                        hld = {}
                        dn_ev = {}
                        for qd in range(4):
                            RQ = {}

                            @blk.sync
                            def _(s, qd=qd):
                                if qd >= 1:
                                    for rdl in range(4):
                                        s.wait_ge(sP.h, RQ_prev["mm_%d_3" % rdl])
                                toks = []
                                for rdl in range(4):
                                    toks.append(DS["wd"].dma(
                                        s, wdb[:, rdl * INTER:(rdl + 1) * INTER],
                                        AP(wdT_s, (qd * 4 + rdl) * 128 * INTER,
                                           [[INTER, 128], [1, INTER]])))
                                RQ["wd"] = toks

                            for tt in range(4):
                                tpar = tt % 2

                                for k in range(NMO):
                                    @blk.sync
                                    def _(s, qd=qd, tt=tt, k=k):
                                        if k >= 8:
                                            s.wait_ge(sP.h, RQ["kmm_%d_%d" % (tt, k - 8)])
                                        elif tt >= 1:
                                            s.wait_ge(sP.h, RQ["kmm_%d_%d" % (tt - 1, NMO - 8 + k)])
                                        elif qd >= 1:
                                            s.wait_ge(sP.h, RQ_prev["kmm_%d_%d" % (3, NMO - 8 + k)])
                                        RQ["hld_%d_%d" % (tt, k)] = DS["hl"].dma(
                                            s, hbb[:, (k % 8) * 512:(k % 8 + 1) * 512],
                                            AP(h_s, k * 128 * T + tt * 512,
                                               [[T, 128], [1, 512]]))

                                    @blk.tensor
                                    def _(t, qd=qd, tt=tt, k=k, tpar=tpar):
                                        wt(t, RQ["hld_%d_%d" % (tt, k)])
                                        if k == 0:
                                            for tok in RQ["wd"]:
                                                wt(t, tok)
                                            if (qd, tt) != (0, 0):
                                                for rdl in range(4):
                                                    key = (qd, tt - 2, rdl) if tt >= 2 else (qd - 1, tt + 2, rdl)
                                                    if key in dn_ev:
                                                        t.wait_ge(sV.h, dn_ev[key])
                                        for rdl in range(4):
                                            mm = t.matmul(
                                                psD[:, (tpar * 4 + rdl) * 512:(tpar * 4 + rdl + 1) * 512],
                                                wdb[:, rdl * INTER + k * 128:rdl * INTER + (k + 1) * 128],
                                                hbb[:, (k % 8) * 512:(k % 8 + 1) * 512],
                                                start=(k == 0), stop=(k == NMO - 1))
                                        pi(mm)
                                        RQ["kmm_%d_%d" % (tt, k)] = sP.v
                                        if k == NMO - 1:
                                            for rdl in range(4):
                                                RQ["mm_%d_%d" % (rdl, tt)] = sP.v

                                for rdl in range(4):
                                    rd = qd * 4 + rdl

                                    @blk.sync
                                    def _(s, qd=qd, tt=tt, rdl=rdl, rd=rd, tpar=tpar):
                                        if rdl >= 2:
                                            key = (qd, tt, rdl - 2)
                                        elif tt >= 1:
                                            key = (qd, tt - 1, rdl + 2)
                                        elif qd >= 1:
                                            key = (qd - 1, 3, rdl + 2)
                                        else:
                                            key = None
                                        if key in dn_ev:
                                            s.wait_ge(sV.h, dn_ev[key])
                                        RQ["x1f_%d_%d" % (rdl, tt)] = DS["xfl"].dma(
                                            s, x1fb[:, (rdl % 2) * 512:(rdl % 2 + 1) * 512],
                                            AP(x1f_s, rd * 128 * T + tt * 512,
                                               [[T, 128], [1, 512]]))

                                    @blk.vector
                                    def _(v, qd=qd, tt=tt, rdl=rdl, rd=rd, tpar=tpar):
                                        v.wait_ge(sP.h, RQ["mm_%d_%d" % (rdl, tt)])
                                        wt(v, RQ["x1f_%d_%d" % (rdl, tt)])
                                        vi(v.tensor_add(r2x[:, rd * T + tt * 512:rd * T + (tt + 1) * 512],
                                                        psD[:, (tpar * 4 + rdl) * 512:(tpar * 4 + rdl + 1) * 512],
                                                        x1fb[:, (rdl % 2) * 512:(rdl % 2 + 1) * 512]))
                                        v.wait_ge(sV.h, sV.v)
                                        dn_ev[(qd, tt, rdl)] = sV.v

                            RQ_prev = RQ

                    with (
                        nc.psum_tensor("psS4a", [1, 2048], F32) as psS4a,
                        nc.psum_tensor("psS4b", [1, 2048], F32) as psS4b,
                        nc.sbuf_tensor("tb4", [128, 2 * 2048], BF16) as tb4,
                        nc.sbuf_tensor("rows4", [1, 4 * 2048], F32) as rows4,
                    ):
                        st4v = {}
                        for m in range(NKD):
                            @blk.vector
                            def _(v, m=m):
                                if m >= 1:
                                    v.wait_ge(sS.h, st4v[("sq", m - 1)])
                                    v.wait_ge(sP.h, st4v[("mm", m - 1)])
                                v.wait_ge(sV.h, vi(v.tensor_copy(tb4[:, 0:2048],
                                                                 r2x[:, m * T:(m + 1) * T])))
                                st4v[("rb", m)] = sV.v

                            @blk.scalar
                            def _(sc, m=m):
                                sc.wait_ge(sV.h, st4v[("rb", m)])
                                si(sc.activation(tb4[:, 2048:4096], tb4[:, 0:2048], AF.Square))
                                st4v[("sq", m)] = sS.v

                            @blk.tensor
                            def _(t, m=m):
                                t.wait_ge(sS.h, st4v[("sq", m)])
                                for w in range(4):
                                    mm = t.matmul(psS4a[:, w * 512:(w + 1) * 512], onecol[:, :],
                                                  tb4[:, w * 512:(w + 1) * 512],
                                                  start=(m == 0), stop=(m == NKD - 1))
                                    mm = t.matmul(psS4b[:, w * 512:(w + 1) * 512], onecol[:, :],
                                                  tb4[:, 2048 + w * 512:2048 + (w + 1) * 512],
                                                  start=(m == 0), stop=(m == NKD - 1))
                                pi(mm)
                                st4v[("mm", m)] = sP.v

                        @blk.vector
                        def _(v):
                            v.wait_ge(sP.h, st4v[("mm", NKD - 1)])
                            v.wait_ge(sV.h, vi(v.tensor_scalar_mul(rows4[:, 0:2048], psS4a[:, :], -1.0 / DM)))
                            v.wait_ge(sV.h, vi(v.tensor_scalar_mul(rows4[:, 2048:4096], psS4b[:, :], 1.0 / DM)))
                            v.wait_ge(sV.h, vi(v.tensor_mul(rows4[:, 4096:6144], rows4[:, 0:2048], rows4[:, 0:2048])))
                            v.wait_ge(sV.h, vi(v.tensor_sub(rows4[:, 4096:6144], rows4[:, 2048:4096], rows4[:, 4096:6144])))
                        v_var4 = sV.v

                        @blk.scalar
                        def _(sc):
                            sc.wait_ge(sV.h, v_var4)
                            sc.wait_ge(sS.h, si(sc.activation(rows4[:, 6144:8192], rows4[:, 4096:6144],
                                                              AF.Ln, bias=ceps[0:1, :])))
                            si(sc.activation(rows4[:, 6144:8192], rows4[:, 6144:8192], AF.Exp, scale=chalf[0:1, :]))
                        v_rstd4 = sS.v

                        rows4_toks = []

                        @blk.gpsimd
                        def _(g):
                            g.wait_ge(sS.h, v_rstd4)
                            rows4_toks.append(DS["r4b"].dma(g, row_s[1:2, :], rows4[:, 0:2048]))
                            rows4_toks.append(DS["r4b"].dma(g, row_s[2:3, :], rows4[:, 6144:8192]))


            with ExitStack() as st6:
                ostg = st6.enter_context(nc.sbuf_tensor("ostg", [128, 2 * 2048], F32))
                rsc = st6.enter_context(nc.sbuf_tensor("rsc", [128, 16], F32))
                rnc = st6.enter_context(nc.sbuf_tensor("rnc", [128, 16], F32))
                rnrs = st6.enter_context(nc.sbuf_tensor("rnrs", [128, 16], F32))
                psFa = st6.enter_context(nc.psum_tensor("psFa", [128, 2048], F32))
                psFb = st6.enter_context(nc.psum_tensor("psFb", [128, 2048], F32))

                with nc.Block() as blk:
                    colt = []

                    @blk.sync
                    def _(s):
                        for tok in rows4_toks:
                            wt(s, tok)
                        with nc.allow_non_contiguous_dma(reason="tiny per-token scale cols"):
                            colt.append(DS["bc4"].dma(s, rnc[:, :],
                                                      AP(row_s, 2048, [[1, 128], [128, 16]])))
                            colt.append(DS["bc4"].dma(s, rsc[:, :],
                                                      AP(row_s, 4096, [[1, 128], [128, 16]])))

                    @blk.vector
                    def _(v):
                        for tok in colt:
                            wt(v, tok)
                        vi(v.tensor_mul(rnrs[:, :], rnc[:, :], rsc[:, :]))
                        v.wait_ge(sV.h, sV.v)
                    v_rnrs = sV.v

                    oev = []
                    ost = []
                    ostq = []
                    for ti in range(NKD):
                        ps = psFa if ti % 2 == 0 else psFb

                        @blk.tensor
                        def _(t, ti=ti, ps=ps):
                            if ti >= 2:
                                t.wait_ge(sS.h, oev[ti - 2])
                            for j in range(NKD):
                                mm = t.transpose(ps[:, j * 128:(j + 1) * 128],
                                                 r2x[:, j * T + ti * 128:j * T + (ti + 1) * 128],
                                                 ident_f[:, :])
                            pi(mm)
                            oev.append(None)
                            ost.append(sP.v)

                        @blk.scalar
                        def _(sc, ti=ti, ps=ps):
                            sc.wait_ge(sP.h, ost[ti])
                            sc.wait_ge(sV.h, v_rnrs)
                            if ti >= 2:
                                wt(sc, ostq[ti - 2])
                            si(sc.activation(ostg[:, (ti % 2) * 2048:(ti % 2 + 1) * 2048],
                                             ps[:, :], AF.Identity,
                                             bias=rnrs[:, ti:ti + 1], scale=rsc[:, ti:ti + 1]))
                            oev[ti] = sS.v

                        @blk.sync
                        def _(s, ti=ti):
                            s.wait_ge(sS.h, oev[ti])
                            ostq.append(DS["out"].dma(
                                s, out[ti * 128:(ti + 1) * 128, :],
                                ostg[:, (ti % 2) * 2048:(ti % 2 + 1) * 2048]))

            stR.close()

        with nc.Block() as blk:
            @blk.sync
            def _(s):
                for name, dst in dump_outs.items():
                    if name in dump_specs:
                        DS["dmp"].dma(s, dst.ap(), dump_specs[name].ap())
                for st_ in DS.values():
                    st_.flush(s)

    return nc


def _in_map(inputs, b):
    return {
        "u": np.ascontiguousarray(inputs["L0_attn_u"][b]),
        "w_in": inputs["L0_attn_W_in"],
        "conv_w": inputs["L0_attn_conv_w"],
        "conv_b": inputs["L0_attn_conv_b"].reshape(CONVD, 1),
        "dt_bias": inputs["L0_attn_dt_bias"].reshape(NH, 1),
        "a_log": inputs["L0_attn_A_log"].reshape(NH, 1),
        "d_vec": inputs["L0_attn_D"].reshape(1, NH),
        "norm_w": inputs["L0_attn_norm_w"].reshape(DIN, 1),
        "w_out": inputs["L0_attn_W_out"],
        "lut": inputs["L0_ffn_lut"].reshape(1, 16),
        "gate_idx": inputs["L0_ffn_gate_idx"],
        "up_idx": inputs["L0_ffn_up_idx"],
        "down_idx": inputs["L0_ffn_down_idx"],
    }


def kernel(**inputs):
    if "nc" not in _CACHE:
        _CACHE["nc"] = build("full")
    nc = _CACHE["nc"]
    in_maps = [_in_map(inputs, b) for b in range(8)]
    res = run_bass_kernel_spmd(nc, in_maps, core_ids=list(range(8)))
    return np.stack([res.results[b]["out"] for b in range(8)], axis=0)


# version 36
# speedup vs baseline: 1.0086x; 1.0068x over previous
import sys
sys.path.insert(0, "/opt/trn_rl_repo")
from contextlib import ExitStack

import numpy as np
import concourse.bass as bass
import concourse.mybir as mybir
from concourse.bass_utils import run_bass_kernel_spmd

F32 = mybir.dt.float32
U8 = mybir.dt.uint8
BF16 = mybir.dt.bfloat16
I32 = mybir.dt.int32
AF = mybir.ActivationFunctionType
ALU = mybir.AluOpType
AP = bass.AP

T = 2048
DM = 2048
DIN = 4096
NH = 32
CHUNK = 256
NCH = 8
CONVD = 4224
DIP = 8352
INTER = 5504
EPS = 1e-5
NKD = 16
NKI = 32
NMO = 43
NMP = 66

_CACHE = {}


def build(stage="full", dump=()):
    nc = bass.Bass("TRN2", target_bir_lowering=False)

    u_in = nc.dram_tensor("u", [T, DM], F32, kind="ExternalInput")
    w_in = nc.dram_tensor("w_in", [DM, DIP], F32, kind="ExternalInput")
    conv_w = nc.dram_tensor("conv_w", [CONVD, 4], F32, kind="ExternalInput")
    conv_b = nc.dram_tensor("conv_b", [CONVD, 1], F32, kind="ExternalInput")
    dt_bias = nc.dram_tensor("dt_bias", [NH, 1], F32, kind="ExternalInput")
    a_log = nc.dram_tensor("a_log", [NH, 1], F32, kind="ExternalInput")
    d_in = nc.dram_tensor("d_vec", [1, NH], F32, kind="ExternalInput")
    norm_w = nc.dram_tensor("norm_w", [DIN, 1], F32, kind="ExternalInput")
    w_out = nc.dram_tensor("w_out", [DIN, DM], F32, kind="ExternalInput")
    lut_in = nc.dram_tensor("lut", [1, 16], F32, kind="ExternalInput")
    gidx = nc.dram_tensor("gate_idx", [INTER, DM], I32, kind="ExternalInput")
    uidx = nc.dram_tensor("up_idx", [INTER, DM], I32, kind="ExternalInput")
    didx = nc.dram_tensor("down_idx", [DM, INTER], I32, kind="ExternalInput")
    out = nc.dram_tensor("out", [T, DM], F32, kind="ExternalOutput")

    uT_s = nc.dram_tensor("uT_s", [NKD, 128, DM], F32)
    zxb_s = nc.dram_tensor("zxb_s", [DIN + CONVD, T], BF16)
    ynorm_s = nc.dram_tensor("ynorm_s", [DIN, T], BF16)
    x1f_s = nc.dram_tensor("x1f_s", [NKD, 128, T], F32)
    x1b_s = nc.dram_tensor("x1b_s", [NKD, 128, T], BF16)
    h_s = nc.dram_tensor("h_s", [NMO, 128, T], BF16)
    wgT_s = nc.dram_tensor("wgT_s", [NMO, 128, DM], BF16)
    wuT_s = nc.dram_tensor("wuT_s", [NMO, 128, DM], BF16)
    wdT_s = nc.dram_tensor("wdT_s", [NKD, 128, INTER], BF16)
    wob_s = nc.dram_tensor("wob_s", [NKD, 128, DIN], BF16)
    acr_s = nc.dram_tensor("acr_s", [NCH, NH * CHUNK], F32)
    row_s = nc.dram_tensor("row_s", [4, 2048], F32)

    dump_specs = {"zxb": zxb_s, "ynorm": ynorm_s, "uT": uT_s, "x1f": x1f_s,
                  "h": h_s, "wgT": wgT_s, "wuT": wuT_s, "wdT": wdT_s, "wob": wob_s}
    dump_outs = {}
    row_srcs = {}

    es = ExitStack()
    with es:
        ident_f = es.enter_context(nc.sbuf_tensor("ident_f", [128, 128], F32))
        ident_b = es.enter_context(nc.sbuf_tensor("ident_b", [128, 128], BF16))
        lutc = es.enter_context(nc.sbuf_tensor("lutc", [128, 16], F32))
        treec = es.enter_context(nc.sbuf_tensor("treec", [128, 16], F32))
        umask = es.enter_context(nc.sbuf_tensor("umask", [128, 2 * CHUNK], U8))
        zero_b = es.enter_context(nc.sbuf_tensor("zero_b", [128, CHUNK], BF16))
        onecol = es.enter_context(nc.sbuf_tensor("onecol", [128, 1], BF16))
        ceps = es.enter_context(nc.sbuf_tensor("ceps", [128, 1], F32))
        chalf = es.enter_context(nc.sbuf_tensor("chalf", [128, 1], F32))
        cidin = es.enter_context(nc.sbuf_tensor("cidin", [128, 1], F32))
        cwcol = es.enter_context(nc.sbuf_tensor("cwcol", [128, 33 * 4], F32))
        cbcol = es.enter_context(nc.sbuf_tensor("cbcol", [128, 33], F32))
        dtbc = es.enter_context(nc.sbuf_tensor("dtbc", [NH, 1], F32))
        nega = es.enter_context(nc.sbuf_tensor("nega", [NH, 1], F32))
        dcol128 = es.enter_context(nc.sbuf_tensor("dcol128", [128, NH], F32))
        nwcol = es.enter_context(nc.sbuf_tensor("nwcol", [128, NKI], F32))

        class Sem:
            def __init__(self, name):
                self.h = es.enter_context(nc.semaphore(name))
                self.v = 0

        sQ, aQ, gQ = Sem("sQ"), Sem("aQ"), Sem("gQ")
        sV, sS, sP, sG = Sem("sV"), Sem("sS"), Sem("sP"), Sem("sG")

        class DStream:

            def __init__(self, name, n=2):
                self.sems = [es.enter_context(nc.semaphore(f"{name}{i}")) for i in range(n)]
                self.v = [0] * n
                self.n = n
                self.i = 0

            def dma(self, q, out, in_, **kw):
                k = self.i % self.n
                q.wait_ge(self.sems[k], self.v[k])
                q.dma_start(out, in_, **kw).then_inc(self.sems[k], 16)
                self.v[k] += 16
                self.i += 1
                return (self.sems[k], self.v[k])

            def last(self):
                k = (self.i - 1) % self.n
                return (self.sems[k], self.v[k])

            def flush(self, eng):
                for k in range(self.n):
                    eng.wait_ge(self.sems[k], self.v[k])

        def wt(eng, tok):
            eng.wait_ge(tok[0], tok[1])

        DS = {}
        for nm, nn in [("gix", 2), ("gst", 2), ("prm", 1), ("prm2", 1), ("ul", 2),
                       ("ust", 2), ("win", 2), ("zst", 2), ("dmp", 1), ("wol", 2),
                       ("wos", 2), ("acr", 1), ("bcx", 1), ("cse", 1), ("xl", 2),
                       ("zl", 2), ("rs", 2), ("rb2", 1), ("rbc", 1), ("yst", 4),
                       ("ynl", 4), ("wom", 4), ("ub", 4), ("x1s", 2), ("r3b", 2),
                       ("bc3", 2), ("wg", 2), ("wu", 2), ("hst", 2), ("wd", 4),
                       ("hl", 4), ("xfl", 2), ("r4b", 2), ("bc4", 2), ("out", 2), ("x1b", 2), ("x1l", 2)]:
            DS[nm] = DStream(nm, nn)

        def vi(inst):
            inst.then_inc(sV.h, 1); sV.v += 1; return sV.v

        def si(inst):
            inst.then_inc(sS.h, 1); sS.v += 1; return sS.v

        def pi(inst):
            inst.then_inc(sP.h, 1); sP.v += 1; return sP.v

        def gi(inst):
            inst.then_inc(sG.h, 1); sG.v += 1; return sG.v

        def qi(inst, sem):
            inst.then_inc(sem.h, 16); sem.v += 16; return sem.v

        gtiles = []
        for r in range(NMO):
            gtiles.append((gidx, wgT_s, r, 0, 2048))
        for r in range(NMO):
            gtiles.append((uidx, wuT_s, r, 0, 2048))
        for rd in range(NKD):
            for cb, F in ((0, 2048), (2048, 2048), (4096, 1408)):
                gtiles.append((didx, wdT_s, rd, cb, F))
        n_gt = len(gtiles)
        gst = dict(load=0, tree=0, tp=0, evac=0, store=0, tree_ready=0, tree_readyQ=None,
                   tg=None)
        g_load_v, g_tree_v, g_tp_v, g_evac_v, g_store_v = [], [], [], [], []
        wu_store_v = {}
        wg_store_v = {}
        gb = {}

        gcnt = [0]

        def galloc(stk, nsl=5, pslots=2):
            n = gcnt[0]
            gcnt[0] += 1
            gb["nsl"] = nsl
            gb["pslots"] = pslots
            gb["idx"] = stk.enter_context(nc.sbuf_tensor("g%d_idx" % n, [128, 4 * 2048], BF16))
            gb["b3"] = stk.enter_context(nc.sbuf_tensor("g%d_b3" % n, [128, 2048], U8))
            gb["b2"] = stk.enter_context(nc.sbuf_tensor("g%d_b2" % n, [128, 2048], U8))
            gb["b1"] = stk.enter_context(nc.sbuf_tensor("g%d_b1" % n, [128, 2048], U8))
            gb["s"] = stk.enter_context(nc.sbuf_tensor("g%d_s" % n, [128, 2048], BF16))
            gb["v"] = stk.enter_context(nc.sbuf_tensor("g%d_v" % n, [128, 6 * 2048], BF16))
            gb["stg"] = stk.enter_context(nc.sbuf_tensor("g%d_stg" % n, [128, nsl * 2048], BF16))
            gb["ps"] = stk.enter_context(nc.psum_tensor("g%d_ps" % n, [128, pslots * 2048], BF16))

        def g_load(g, n):
            for _ in range(n):
                i = gst["load"]
                if i >= n_gt or i >= len(g_tree_v) + 4:
                    return
                src, dst, r, cb, F = gtiles[i]
                par = i % 4
                if i >= 4:
                    g.wait_ge(sV.h, g_tree_v[i - 4])
                g_load_v.append(DS["gix"].dma(
                    g, gb["idx"][:, par * 2048:par * 2048 + F],
                    AP(src, (r * 128) * src.shape[1] + cb,
                       [[src.shape[1], 128], [1, F]])))
                gst["load"] += 1

        def _tree_prog(i):
            src, dst, r, cb, F = gtiles[i]
            par = i % 4
            res = i % 2
            P = []

            def first(v):
                wt(v, g_load_v[i])
                v.wait_ge(sV.h, gst["tree_ready"])
                if gst["tree_readyQ"]:
                    wt(v, gst["tree_readyQ"])
                if i >= 2:
                    v.wait_ge(sP.h, g_tp_v[i - 2])
                ix = gb["idx"][:, par * 2048:par * 2048 + F]
                v.wait_ge(sV.h, vi(v.tensor_single_scalar(gb["b3"][:, :F], ix, 8.0, ALU.is_ge)))
            P.append(first)
            P.append(lambda v: v.wait_ge(sV.h, vi(v.scalar_tensor_tensor(
                gb["s"][:, :F], gb["b3"][:, :F], -8.0,
                gb["idx"][:, par * 2048:par * 2048 + F], ALU.mult, ALU.add))))
            P.append(lambda v: v.wait_ge(sV.h, vi(v.tensor_single_scalar(
                gb["b2"][:, :F], gb["s"][:, :F], 4.0, ALU.is_ge))))
            P.append(lambda v: v.wait_ge(sV.h, vi(v.scalar_tensor_tensor(
                gb["s"][:, :F], gb["b2"][:, :F], -4.0, gb["s"][:, :F], ALU.mult, ALU.add))))
            P.append(lambda v: v.wait_ge(sV.h, vi(v.tensor_single_scalar(
                gb["b1"][:, :F], gb["s"][:, :F], 2.0, ALU.is_ge))))
            P.append(lambda v: v.wait_ge(sV.h, vi(v.scalar_tensor_tensor(
                gb["s"][:, :F], gb["b1"][:, :F], -2.0, gb["s"][:, :F], ALU.mult, ALU.add))))
            for g8, sl4 in ((4, (2, 3, 4, 5)), (0, (res, 3, 4, 5))):
                for jj in range(4):
                    P.append(lambda v, jj=jj, g8=g8, sl4=sl4, last=(jj == 3): (
                        vi(v.tensor_scalar(gb["v"][:, sl4[jj] * 2048:sl4[jj] * 2048 + F],
                                           gb["s"][:, :F],
                                           treec[:, g8 + jj:g8 + jj + 1],
                                           treec[:, 8 + g8 + jj:9 + g8 + jj],
                                           ALU.mult, ALU.add)),
                        v.wait_ge(sV.h, sV.v) if last else None))
                P.append(lambda v, sl4=sl4: vi(v.copy_predicated(
                    gb["v"][:, sl4[0] * 2048:sl4[0] * 2048 + F], gb["b1"][:, :F],
                    gb["v"][:, sl4[1] * 2048:sl4[1] * 2048 + F])))
                P.append(lambda v, sl4=sl4: (
                    vi(v.copy_predicated(
                        gb["v"][:, sl4[2] * 2048:sl4[2] * 2048 + F], gb["b1"][:, :F],
                        gb["v"][:, sl4[3] * 2048:sl4[3] * 2048 + F])),
                    v.wait_ge(sV.h, sV.v)))
                P.append(lambda v, sl4=sl4: (
                    vi(v.copy_predicated(
                        gb["v"][:, sl4[0] * 2048:sl4[0] * 2048 + F], gb["b2"][:, :F],
                        gb["v"][:, sl4[2] * 2048:sl4[2] * 2048 + F])),
                    v.wait_ge(sV.h, sV.v)))

            def final(v):
                vi(v.copy_predicated(gb["v"][:, res * 2048:res * 2048 + F], gb["b3"][:, :F],
                                     gb["v"][:, 2 * 2048:2 * 2048 + F]))
                v.wait_ge(sV.h, sV.v)
            P.append(final)
            return P

        def g_tree_ops(v, k):
            while k > 0:
                if gst["tg"] is None:
                    i = gst["tree"]
                    if i >= n_gt or i >= gst["load"] or i >= len(g_tp_v) + 2:
                        return
                    gst["tg"] = (_tree_prog(i), 0)
                prog, pos = gst["tg"]
                while k > 0 and pos < len(prog):
                    prog[pos](v)
                    pos += 1
                    k -= 1
                if pos == len(prog):
                    gst["tg"] = None
                    g_tree_v.append(sV.v)
                    gst["tree"] += 1
                else:
                    gst["tg"] = (prog, pos)

        def g_tree(v, n):
            g_tree_ops(v, 21 * n)

        def g_tp(t, n):
            psl = gb["pslots"]
            for _ in range(n):
                i = gst["tp"]
                if (i >= n_gt or gst["tree"] < i + 1
                        or (gst["tree"] < min(i + 2, n_gt)
                            and gst["tree"] < gst["load"])
                        or i >= gst["evac"] + psl):
                    return
                src, dst, r, cb, F = gtiles[i]
                par = i % psl
                res = i % 2
                t.wait_ge(sV.h, g_tree_v[i])
                if i >= psl:
                    t.wait_ge(sV.h, g_evac_v[i - psl])
                for j in range(F // 128):
                    mm = t.transpose(gb["ps"][:, par * 2048 + j * 128:par * 2048 + (j + 1) * 128],
                                     gb["v"][:, res * 2048 + j * 128:res * 2048 + (j + 1) * 128],
                                     ident_b[:, :])
                pi(mm)
                g_tp_v.append(sP.v)
                gst["tp"] += 1

        def g_evac(v, n):
            for _ in range(n):
                i = gst["evac"]
                if i >= n_gt or i >= gst["tp"]:
                    return
                src, dst, r, cb, F = gtiles[i]
                nsl = gb["nsl"]
                sl = i % nsl
                par = i % gb["pslots"]
                v.wait_ge(sP.h, g_tp_v[i])
                if i >= nsl:
                    wt(v, g_store_v[i - nsl])
                vi(v.tensor_copy(gb["stg"][:, sl * 2048:sl * 2048 + F],
                                 gb["ps"][:, par * 2048:par * 2048 + F]))
                v.wait_ge(sV.h, sV.v)
                g_evac_v.append(sV.v)
                gst["evac"] += 1

        def g_store(sc, n, lag=4):
            lag = min(lag, gb["nsl"] - 1)
            for _ in range(n):
                i = gst["store"]
                if i >= n_gt or i >= gst["evac"] - lag:
                    return
                src, dst, r, cb, F = gtiles[i]
                sl = i % gb["nsl"]
                sc.wait_ge(sV.h, g_evac_v[i])
                tok = DS["gst"].dma(sc, AP(dst, r * 128 * dst.shape[2] + cb,
                                           [[dst.shape[2], 128], [1, F]]),
                                    gb["stg"][:, sl * 2048:sl * 2048 + F])
                g_store_v.append(tok)
                if dst is wgT_s:
                    wg_store_v[r] = tok
                if dst is wuT_s:
                    wu_store_v[r] = tok
                gst["store"] += 1

        def g_step(blk, n):
            for _ in range(n):
                @blk.gpsimd
                def _(g):
                    g_load(g, 1 + (1 if gst["load"] < 3 else 0))

                @blk.tensor
                def _(t):
                    g_tp(t, 1)

                @blk.scalar
                def _(sc):
                    g_store(sc, 1)

                @blk.vector
                def _(v):
                    g_tree(v, 1)
                    g_evac(v, 1)

        def flush_stores(blk, names):
            @blk.sync
            def _(s):
                for nm in names:
                    DS[nm].flush(s)

        def g_drain(blk, all_tiles=False):
            guard = 0
            target = n_gt if all_tiles else None
            while (gst["store"] < (target if all_tiles else gst["load"])
                   and guard < 250):
                if all_tiles:
                    @blk.gpsimd
                    def _(g):
                        g_load(g, 2)

                @blk.tensor
                def _(t):
                    g_tp(t, 1)

                @blk.scalar
                def _(sc):
                    g_store(sc, 2, lag=0)

                @blk.vector
                def _(v):
                    g_tree(v, 1)
                    g_evac(v, 1)
                guard += 1
            assert gst["store"] == (target if all_tiles else gst["load"]), gst

        stDT = ExitStack()
        dtv = stDT.enter_context(nc.sbuf_tensor("dtv", [NH, T], F32, side="right"))
        lndt = stDT.enter_context(nc.sbuf_tensor("lndt", [NH, T], F32, side="right"))
        acum = stDT.enter_context(nc.sbuf_tensor("acum", [NH, T], F32, side="right"))
        dtdec = stDT.enter_context(nc.sbuf_tensor("dtdec", [NH, T], F32, side="right"))
        dtraw = sp1 = sp2 = None
        row_srcs = {"dtv": dtv, "acum": acum, "dtdec": dtdec, "lndt": lndt}
        for name in dump:
            if name in dump_specs:
                src = dump_specs[name]
                dump_outs[name] = nc.dram_tensor("dump_" + name, list(src.shape), src.dtype,
                                                 kind="ExternalOutput")
            elif name in row_srcs:
                dump_outs[name] = nc.dram_tensor("dump_" + name, [NH, T], F32,
                                                 kind="ExternalOutput")

        stU = ExitStack()
        uTb = stU.enter_context(nc.sbuf_tensor("uTb", [128, NKD * T], BF16))
        wobb = stU.enter_context(nc.sbuf_tensor("wobb", [128, 2 * 1024], BF16))
        wob_items = [(mb, kh) for mb in range(NKD) for kh in range(4)]
        wob_ld_v, wob_st_v = [], []
        wst = dict(ld=0, st=0)

        def wob_load(g, n):
            for _ in range(n):
                i = wst["ld"]
                if i >= len(wob_items) or i >= wst["st"] + 2:
                    return
                mb, kh = wob_items[i]
                if i >= 2:
                    wt(g, wob_st_v[i - 2])
                wob_ld_v.append(DS["wol"].dma(
                    g, wobb[:, (i % 2) * 1024:(i % 2 + 1) * 1024]
                    .rearrange("p (k c) -> p k c", k=8),
                    AP(w_out, (kh * 8 * 128) * DM + mb * 128,
                       [[DM, 128], [DM * 128, 8], [1, 128]])))
                wst["ld"] += 1

        def wob_store(sc, n):
            for _ in range(n):
                i = wst["st"]
                if i >= len(wob_items) or i >= wst["ld"]:
                    return
                mb, kh = wob_items[i]
                wt(sc, wob_ld_v[i])
                wob_st_v.append(DS["wos"].dma(
                    sc, AP(wob_s, mb * 128 * DIN + kh * 1024, [[DIN, 128], [1, 1024]]),
                    wobb[:, (i % 2) * 1024:(i % 2 + 1) * 1024]))
                wst["st"] += 1
        with ExitStack() as st0:
            utile = st0.enter_context(nc.sbuf_tensor("utile", [128, 2 * DM], F32))
            ustg = st0.enter_context(nc.sbuf_tensor("ustg", [128, 2 * 1024], F32))
            iscr = st0.enter_context(nc.sbuf_tensor("iscr", [128, 256], I32))
            psTa = st0.enter_context(nc.psum_tensor("psTa", [128, 1024], F32))
            psTb = st0.enter_context(nc.psum_tensor("psTb", [128, 1024], F32))
            galloc(st0)

            with nc.Block() as blk:
                @blk.gpsimd
                def _(g):
                    gi(g.iota(iscr[:, :128], [[1, 128]], base=0, channel_multiplier=-1))
                v_iota1 = sG.v

                @blk.vector
                def _(v):
                    v.wait_ge(sG.h, v_iota1)
                    v.wait_ge(sV.h, vi(v.tensor_single_scalar(ident_f[:, :], iscr[:, :128], 0.0, ALU.is_equal)))
                    vi(v.tensor_copy(ident_b[:, :], ident_f[:, :]))
                v_ident = sV.v

                @blk.gpsimd
                def _(g):
                    g.wait_ge(sV.h, v_ident)
                    gi(g.iota(iscr[:, :256], [[-1, 256]], base=0, channel_multiplier=1))
                    gi(g.memset(zero_b[:, :], 0.0))
                    gi(g.memset(onecol[:, :], 1.0))
                    gi(g.memset(ceps[:, :], EPS))
                    gi(g.memset(chalf[:, :], -0.5))
                    gi(g.memset(cidin[:, :], 1.0 / DIN))
                v_iota2 = sG.v

                @blk.vector
                def _(v):
                    v.wait_ge(sG.h, v_iota2)
                    vi(v.tensor_single_scalar(umask[:, 0:CHUNK], iscr[:, :256], 0.0, ALU.is_gt))
                    vi(v.tensor_single_scalar(umask[:, CHUNK:2 * CHUNK], iscr[:, :256], -128.0, ALU.is_gt))

                @blk.sync
                def _(s):
                    DS["prm"].dma(s, cwcol[:, :].rearrange("p (g j) -> p g j", g=33),
                                  AP(conv_w, 0, [[4, 128], [512, 33], [1, 4]]))
                    DS["prm"].dma(s, dtbc[:, :], dt_bias[:, :])
                    DS["prm"].dma(s, nega[:, :], a_log[:, :])
                    DS["prm"].dma(s, dcol128[:, :], AP(d_in, 0, [[0, 128], [1, NH]]))
                    DS["prm"].dma(s, lutc[:, :], AP(lut_in, 0, [[0, 128], [1, 16]]))

                @blk.gpsimd
                def _(g):
                    with nc.allow_non_contiguous_dma(reason="tiny one-time param loads"):
                        DS["prm2"].dma(g, cbcol[:, :], AP(conv_b, 0, [[1, 128], [128, 33]]))
                        DS["prm2"].dma(g, nwcol[:, :], AP(norm_w, 0, [[1, 128], [128, NKI]]))
                v_paramsG = DS["prm2"].last()
                v_params = DS["prm"].last()

                @blk.vector
                def _(v):
                    wt(v, v_params)
                    vi(v.tensor_sub(treec[:, 0:8], lutc[:, 1:16:2], lutc[:, 0:16:2]))
                    vi(v.tensor_copy(treec[:, 8:16], lutc[:, 0:16:2]))
                gst["tree_ready"] = sV.v
                gst["tree_readyQ"] = v_params

                u_load_v, half_tp_v, half_st_v, half_ev_v = [], [], [], []

                @blk.sync
                def _(s):
                    for ti in range(2):
                        u_load_v.append(DS["ul"].dma(
                            s, utile[:, (ti % 2) * DM:(ti % 2 + 1) * DM],
                            u_in[ti * 128:(ti + 1) * 128, :]))

                for ti in range(NKD):
                    for half in range(2):
                        hidx = 2 * ti + half
                        ps = psTa if hidx % 2 == 0 else psTb

                        @blk.tensor
                        def _(t, ti=ti, half=half, hidx=hidx, ps=ps):
                            t.wait_ge(sV.h, v_ident)
                            wt(t, u_load_v[ti])
                            if hidx >= 2:
                                t.wait_ge(sV.h, half_ev_v[hidx - 2][0])
                                t.wait_ge(sS.h, half_ev_v[hidx - 2][1])
                            for jl in range(8):
                                j = half * 8 + jl
                                mm = t.transpose(ps[:, jl * 128:(jl + 1) * 128],
                                                 utile[:, (ti % 2) * DM + j * 128:(ti % 2) * DM + (j + 1) * 128],
                                                 ident_f[:, :])
                            pi(mm)
                            half_tp_v.append(sP.v)

                        @blk.vector
                        def _(v, ti=ti, half=half, hidx=hidx, ps=ps):
                            v.wait_ge(sP.h, half_tp_v[hidx])
                            vi(v.tensor_copy(
                                uTb[:, :].rearrange("p (j t) -> p j t", j=NKD)[
                                    :, half * 8:half * 8 + 8, ti * 128:(ti + 1) * 128],
                                ps[:, :].rearrange("p (j q) -> p j q", j=8)))
                            half_ev_v.append([sV.v, None])

                        @blk.scalar
                        def _(sc, ti=ti, half=half, hidx=hidx, ps=ps):
                            sc.wait_ge(sP.h, half_tp_v[hidx])
                            if hidx >= 2:
                                wt(sc, half_st_v[hidx - 2])
                            si(sc.copy(ustg[:, (hidx % 2) * 1024:(hidx % 2 + 1) * 1024], ps[:, :]))
                            half_ev_v[hidx][1] = sS.v

                        @blk.sync
                        def _(s, ti=ti, half=half, hidx=hidx):
                            s.wait_ge(sS.h, half_ev_v[hidx][1])
                            half_st_v.append(DS["ust"].dma(
                                s, AP(uT_s, ti * 128 * DM + half * 1024,
                                      [[DM, 128], [1, 1024]]),
                                ustg[:, (hidx % 2) * 1024:(hidx % 2 + 1) * 1024]))
                            if half == 1 and ti + 2 < NKD:
                                s.wait_ge(sP.h, half_tp_v[2 * ti + 1])
                                u_load_v.append(DS["ul"].dma(
                                    s, utile[:, (ti % 2) * DM:(ti % 2 + 1) * DM],
                                    u_in[(ti + 2) * 128:(ti + 3) * 128, :]))

                    if ti % 3 == 2:
                        g_step(blk, 1)
                g_drain(blk)
                flush_stores(blk, ["gst", "ust", "gix"])

        if stage != "b0":
            with ExitStack() as st1:
                wcolb = st1.enter_context(nc.sbuf_tensor("wcolb", [128, 2 * 2048], BF16))
                zstg = st1.enter_context(nc.sbuf_tensor("zstg", [128, 2 * 1024], BF16))
                dtraw = st1.enter_context(nc.sbuf_tensor("dtraw", [NH, T], F32))
                sp1 = st1.enter_context(nc.sbuf_tensor("sp1", [NH, T], F32))
                sp2 = st1.enter_context(nc.sbuf_tensor("sp2", [NH, T], F32))
                psWa = st1.enter_context(nc.psum_tensor("psWa", [128, 1024], F32))
                psWb = st1.enter_context(nc.psum_tensor("psWb", [128, 1024], F32))
                galloc(st1, nsl=4)

                with nc.Block() as blk:
                    w_load_v, mm_v, ev_v, st_v = [], [], [], []

                    @blk.gpsimd
                    def _(g):
                        for m in range(2):
                            M = 128 if m < NMP - 1 else 32
                            w_load_v.append(DS["win"].dma(
                                g, wcolb[:, (m % 2) * 2048:(m % 2) * 2048 + 16 * M]
                                .rearrange("p (k c) -> p k c", k=16),
                                AP(w_in, m * 128, [[DIP, 128], [DIP * 128, 16], [1, M]])))

                    for m in range(NMP):
                        M = 128 if m < NMP - 1 else 32
                        for half in range(2):
                            hidx = 2 * m + half
                            ps = psWa if hidx % 2 == 0 else psWb

                            @blk.tensor
                            def _(t, m=m, M=M, half=half, hidx=hidx, ps=ps):
                                wt(t, w_load_v[m])
                                if hidx >= 2:
                                    t.wait_ge(sS.h, ev_v[hidx - 2])
                                for k in range(16):
                                    for tt in range(2):
                                        mm = t.matmul(
                                            ps[0:M, tt * 512:(tt + 1) * 512],
                                            wcolb[:, (m % 2) * 2048 + k * M:(m % 2) * 2048 + (k + 1) * M],
                                            uTb[:, k * T + half * 1024 + tt * 512:
                                                k * T + half * 1024 + (tt + 1) * 512],
                                            start=(k == 0), stop=(k == 15))
                                pi(mm)
                                mm_v.append(sP.v)

                            @blk.scalar
                            def _(sc, m=m, M=M, half=half, hidx=hidx, ps=ps):
                                sc.wait_ge(sP.h, mm_v[hidx])
                                if hidx >= 2 and st_v[hidx - 2] is not None:
                                    wt(sc, st_v[hidx - 2])
                                if m < 32:
                                    si(sc.copy(zstg[0:M, (hidx % 2) * 1024:(hidx % 2 + 1) * 1024],
                                               ps[0:M, :]))
                                elif m < NMP - 1:
                                    si(sc.copy(zstg[0:M, (hidx % 2) * 1024:(hidx % 2 + 1) * 1024],
                                               ps[0:M, :]))
                                else:
                                    si(sc.copy(dtraw[:, half * 1024:(half + 1) * 1024], ps[0:NH, :]))
                                ev_v.append(sS.v)

                            @blk.sync
                            def _(s, m=m, M=M, half=half, hidx=hidx):
                                if m < NMP - 1:
                                    s.wait_ge(sS.h, ev_v[hidx])
                                    st_v.append(DS["zst"].dma(
                                        s, AP(zxb_s, m * 128 * T + half * 1024,
                                              [[T, M], [1, 1024]]),
                                        zstg[0:M, (hidx % 2) * 1024:(hidx % 2 + 1) * 1024]))
                                else:
                                    st_v.append(None)

                        @blk.gpsimd
                        def _(g, m=m):
                            wob_load(g, 1)

                        @blk.scalar
                        def _(sc, m=m):
                            wob_store(sc, 1)

                        @blk.gpsimd
                        def _(g, m=m):
                            wob_load(g, 1)

                        @blk.scalar
                        def _(sc, m=m):
                            wob_store(sc, 1)

                        @blk.gpsimd
                        def _(g, m=m):
                            if m + 2 < NMP:
                                mn = m + 2
                                Mn = 128 if mn < NMP - 1 else 32
                                g.wait_ge(sP.h, mm_v[2 * m + 1])
                                w_load_v.append(DS["win"].dma(
                                    g, wcolb[:, (mn % 2) * 2048:(mn % 2) * 2048 + 16 * Mn]
                                    .rearrange("p (k c) -> p k c", k=16),
                                    AP(w_in, mn * 128, [[DIP, 128], [DIP * 128, 16], [1, Mn]])))

                        if m % 2 == 1:
                            g_step(blk, 1)

                    g_drain(blk)

                    v_dtraw = sS.v

                    @blk.vector
                    def _(v):
                        v.wait_ge(sS.h, v_dtraw)
                        wt(v, v_params)
                        vi(v.tensor_scalar(sp1[:, :], dtraw[:, :], dtbc[:, 0:1], 0.0, ALU.add, ALU.max))
                    v_sp1 = sV.v

                    @blk.scalar
                    def _(sc):
                        sc.wait_ge(sS.h, v_dtraw)
                        wt(sc, v_params)
                        sc.wait_ge(sS.h, si(sc.activation(sp2[:, :], dtraw[:, :], AF.Abs, bias=dtbc[:, 0:1])))
                        sc.wait_ge(sS.h, si(sc.activation(sp2[:, :], sp2[:, :], AF.Exp, scale=-1.0)))
                        si(sc.activation(sp2[:, :], sp2[:, :], AF.Ln, bias=1.0))
                    v_sp2 = sS.v

                    @blk.vector
                    def _(v):
                        v.wait_ge(sV.h, v_sp1)
                        v.wait_ge(sS.h, v_sp2)
                        vi(v.tensor_add(dtv[:, :], sp1[:, :], sp2[:, :]))
                    v_dtv = sV.v

                    @blk.scalar
                    def _(sc):
                        sc.wait_ge(sV.h, v_dtv)
                        si(sc.activation(lndt[:, :], dtv[:, :], AF.Ln))
                        sc.wait_ge(sS.h, si(sc.activation(nega[:, :], nega[:, :], AF.Exp)))
                    v_lndt = sS.v

                    @blk.vector
                    def _(v):
                        v.wait_ge(sS.h, v_lndt)
                        v.wait_ge(sV.h, vi(v.tensor_scalar_mul(nega[:, :], nega[:, :], -1.0)))
                        v.wait_ge(sV.h, vi(v.tensor_scalar_mul(sp1[:, :], dtv[:, :], nega[:, 0:1])))
                        for c in range(NCH):
                            sl = slice(c * CHUNK, (c + 1) * CHUNK)
                            vi(v.tensor_tensor_scan(acum[:, sl], sp1[:, sl], sp1[:, sl],
                                                    0.0, ALU.add, ALU.bypass))
                        v.wait_ge(sV.h, sV.v)
                        for c in range(NCH):
                            sl = slice(c * CHUNK, (c + 1) * CHUNK)
                            vi(v.tensor_single_scalar(sp2[:, sl], acum[:, sl],
                                                      acum[:, c * CHUNK + 255:c * CHUNK + 256],
                                                      ALU.subtract))
                    v_acum = sV.v

                    @blk.scalar
                    def _(sc):
                        sc.wait_ge(sV.h, v_acum)
                        si(sc.activation(sp2[:, :], sp2[:, :], AF.Exp, scale=-1.0))
                    v_dec = sS.v

                    @blk.vector
                    def _(v):
                        v.wait_ge(sS.h, v_dec)
                        vi(v.tensor_mul(dtdec[:, :], sp2[:, :], dtv[:, :]))
                    v_ddec = sV.v

                    @blk.sync
                    def _(s):
                        s.wait_ge(sV.h, v_ddec)
                        for name, dst in dump_outs.items():
                            if name in row_srcs:
                                DS["dmp"].dma(s, dst.ap(), row_srcs[name][:, :])

                    assert wst["st"] == len(wob_items), wst
                    flush_stores(blk, ["gst", "zst", "dmp", "gix", "win", "wol", "wos"])

        stU.close()

        if stage in ("b2", "b3", "full"):
            with ExitStack() as st2:
                bcx = st2.enter_context(nc.sbuf_tensor("bcx", [128, 2 * 260], BF16))
                bcc = st2.enter_context(nc.sbuf_tensor("bcc", [128, 2 * 256], BF16))
                cset = st2.enter_context(nc.sbuf_tensor("cset", [64, 2 * 256], BF16))
                colsb = st2.enter_context(nc.sbuf_tensor("colsb", [128, 192], F32))
                nscol = st2.enter_context(nc.sbuf_tensor("nscol", [128, 64], F32))
                btokb = st2.enter_context(nc.sbuf_tensor("btokb", [128, 128], BF16))
                bdb = st2.enter_context(nc.sbuf_tensor("bdb", [128, 256], BF16))
                cbts = st2.enter_context(nc.sbuf_tensor("cbts", [128, 2 * 256], BF16))
                a64b = st2.enter_context(nc.sbuf_tensor("a64b", [64, NH], F32))
                rsubb = st2.enter_context(nc.sbuf_tensor("rsubb", [128, 2 * 1024], F32))
                dmbb = st2.enter_context(nc.sbuf_tensor("dmbb", [128, 4 * 1024], F32))
                ebuf = st2.enter_context(nc.sbuf_tensor("ebuf", [128, 4 * 1024], BF16))
                slb = st2.enter_context(nc.sbuf_tensor("slb", [128, 4 * 1024], BF16))
                e2b = st2.enter_context(nc.sbuf_tensor("e2b", [64, 1024], BF16))
                cscb = st2.enter_context(nc.sbuf_tensor("cscb", [64, 1024], BF16))
                xbufs = st2.enter_context(nc.sbuf_tensor("xbufs", [128, 2 * 1040], BF16))
                xcs = st2.enter_context(nc.sbuf_tensor("xcs", [128, 2 * 1024], BF16))
                xtokb = st2.enter_context(nc.sbuf_tensor("xtokb", [128, 2 * 1024], BF16))
                zsb = st2.enter_context(nc.sbuf_tensor("zsb", [128, 2 * 1024], BF16))
                sqbb = st2.enter_context(nc.sbuf_tensor("sqbb", [128, 2 * 1024], BF16))
                ybuf = st2.enter_context(nc.sbuf_tensor("ybuf", [128, NH * CHUNK], BF16))
                prevf = st2.enter_context(nc.sbuf_tensor("prevf", [64, NH * 128], F32))
                prevbb = st2.enter_context(nc.sbuf_tensor("prevbb", [64, NH * 128], BF16))
                rstd1 = st2.enter_context(nc.sbuf_tensor("rstd1", [1, 256], F32))
                rbcb = st2.enter_context(nc.sbuf_tensor("rbcb", [128, 256], F32))
                psC = st2.enter_context(nc.psum_tensor("psC", [128, 512], F32))
                psY = psC
                psSt = st2.enter_context(nc.psum_tensor("psSt", [64, 512], F32))
                psX = st2.enter_context(nc.psum_tensor("psX", [128, 2 * 1024], BF16))
                psP4V = st2.enter_context(nc.psum_tensor("psP4V", [128, 448], F32))
                psP4 = psP4V[:, 0:192]
                psBt = st2.enter_context(nc.psum_tensor("psBt", [128, 128], BF16))
                galloc(st2, nsl=2, pslots=1)
                psV = psP4V[0:1, 192:448]

                with nc.Block() as blk:
                    @blk.vector
                    def _(v):
                        vi(v.memset(prevf[:, :], 0.0))
                        vi(v.memset(prevbb[:, :], 0.0))
                    prevcp = sV.v

                    acrl = []

                    @blk.gpsimd
                    def _(g):
                        for cc in range(NCH):
                            acrl.append(DS["acr"].dma(
                                g,
                                acr_s.ap()[cc:cc + 1, :].rearrange("x (p n) -> p (x n)", p=NH),
                                acum[:, cc * CHUNK:(cc + 1) * CHUNK]))

                    prev = {"endV": sV.v, "endS": sS.v, "endP": sP.v, "prevcp": prevcp}
                    Hd = {}
                    for c in range(NCH):
                        cp = c % 2
                        R = {}

                        @blk.tensor
                        def _(t):
                            t.wait_ge(sV.h, prev["endV"])
                            t.wait_ge(sS.h, prev["endS"])
                            DS["cse"].flush(t)

                        @blk.vector
                        def _(v):
                            v.wait_ge(sS.h, prev["endS"])
                            v.wait_ge(sP.h, prev["endP"])
                            DS["cse"].flush(v)
                            DS["rbc"].flush(v)
                            if c == 0:
                                wt(v, v_paramsG)
                                wt(v, v_params)

                        @blk.scalar
                        def _(sc):
                            sc.wait_ge(sV.h, prev["endV"])
                            sc.wait_ge(sP.h, prev["endP"])
                            DS["yst"].flush(sc)
                            if c == 0:
                                wt(sc, v_paramsG)
                                wt(sc, v_params)

                        @blk.gpsimd
                        def _(g, c=c):
                            g.wait_ge(sV.h, prev["endV"])
                            DS["rs"].flush(g)
                            R["acr"] = acrl[c]

                        if c == 0:
                            @blk.gpsimd
                            def _(g):
                                gi(g.memset(bcx[:, 0:3], 0.0))
                                R["bcx0"] = sG.v

                        @blk.sync
                        def _(s, c=c):
                            s.wait_ge(sV.h, prev["endV"])
                            s.wait_ge(sP.h, prev["endP"])
                            s.wait_ge(sS.h, prev["endS"])
                            if c == 0:
                                DS["zst"].flush(s)
                                s.wait_ge(sG.h, R["bcx0"])
                                R["bcx"] = DS["bcx"].dma(
                                    s, bcx[:, 3:259],
                                    AP(zxb_s, (DIN + 4096) * T, [[T, 128], [1, 256]]))

                        if c == 0:
                            @blk.vector
                            def _(v, c=c):
                                wt(v, R["bcx"])
                                v.wait_ge(sG.h, R["bcx0"])
                                gg = 32
                                v.wait_ge(sV.h, vi(v.tensor_scalar_mul(bcc[:, 0:256], bcx[:, 0:256],
                                                                       cwcol[:, gg * 4:gg * 4 + 1])))
                                for j in range(1, 4):
                                    v.wait_ge(sV.h, vi(v.scalar_tensor_tensor(
                                        bcc[:, 0:256], bcx[:, j:j + 256], cwcol[:, gg * 4 + j:gg * 4 + j + 1],
                                        bcc[:, 0:256], ALU.mult, ALU.add)))
                                R["bccv"] = sV.v

                            @blk.scalar
                            def _(sc, c=c):
                                sc.wait_ge(sV.h, R["bccv"])
                                si(sc.activation(bcc[:, 0:256], bcc[:, 0:256], AF.Silu,
                                                 bias=cbcol[:, 32:33]))
                                R["bcc"] = sS.v

                            @blk.gpsimd
                            def _(g, c=c):
                                g.wait_ge(sS.h, R["bcc"])
                                R["cset"] = DS["cse"].dma(g, cset[:, 0:256], bcc[64:128, 0:256])
                        else:
                            R["bcc"] = Hd[c]["silu"]
                            R["cset"] = Hd[c]["cset"]

                        @blk.tensor
                        def _(t, c=c):
                            t.wait_ge(sV.h, prev["endV"])
                            for sb in range(2):
                                sl = slice(c * CHUNK + sb * 128, c * CHUNK + (sb + 1) * 128)
                                mm = t.transpose(psP4[:, sb * 32:(sb + 1) * 32], acum[:, sl], ident_f[0:32, 0:32])
                                mm = t.transpose(psP4[:, 64 + sb * 32:64 + (sb + 1) * 32], lndt[:, sl], ident_f[0:32, 0:32])
                                mm = t.transpose(psP4[:, 128 + sb * 32:128 + (sb + 1) * 32], dtdec[:, sl], ident_f[0:32, 0:32])
                            pi(mm)
                            R["p4"] = sP.v

                        @blk.vector
                        def _(v, c=c):
                            v.wait_ge(sP.h, R["p4"])
                            v.wait_ge(sV.h, vi(v.tensor_copy(colsb[:, :], psP4[:, :])))
                            vi(v.tensor_sub(nscol[:, :], colsb[:, 64:128], colsb[:, 0:64]))
                            R["cols"] = sV.v

                        @blk.tensor
                        def _(t, c=c, cp=cp):
                            t.wait_ge(sS.h, R["bcc"])
                            for sb in range(2):
                                mm = t.transpose(psBt[:, sb * 64:(sb + 1) * 64],
                                                 bcc[0:64, cp * 256 + sb * 128:cp * 256 + (sb + 1) * 128],
                                                 ident_b[0:64, 0:64])
                            pi(mm)
                            R["btokp"] = sP.v

                        @blk.vector
                        def _(v, c=c):
                            v.wait_ge(sP.h, R["btokp"])
                            vi(v.tensor_copy(btokb[:, :], psBt[:, :]))
                            R["btok"] = sV.v
                            g_tree_ops(v, 2)

                        @blk.tensor
                        def _(t, c=c, cp=cp):
                            wt(t, R["cset"])
                            for sb in range(2):
                                mm = t.matmul(psC[:, sb * 256:(sb + 1) * 256],
                                              bcc[0:64, cp * 256 + sb * 128:cp * 256 + (sb + 1) * 128],
                                              cset[:, cp * 256:(cp + 1) * 256], start=True, stop=True)
                            pi(mm)
                            R["cbtp"] = sP.v

                        @blk.scalar
                        def _(sc, c=c):
                            sc.wait_ge(sP.h, R["cbtp"])
                            si(sc.copy(cbts[:, :], psC[:, :]))
                            R["cbte"] = sS.v

                        @blk.vector
                        def _(v, c=c):
                            v.wait_ge(sS.h, R["cbte"])
                            for sb in range(2):
                                vi(v.copy_predicated(cbts[:, sb * 256:(sb + 1) * 256],
                                                     umask[:, sb * 256:(sb + 1) * 256], zero_b[:, :]))
                            v.wait_ge(sV.h, sV.v)
                            R["cbt"] = sV.v
                            g_tree_ops(v, 2)

                        SR = []
                        for su in range(8):
                            Rs = {}
                            par = su % 2

                            if c == 0:
                                @blk.gpsimd
                                def _(g, su=su, par=par):
                                    if su >= 2:
                                        g.wait_ge(sV.h, SR[su - 2]["free"])
                                    gi(g.memset(xbufs[:, par * 1040:par * 1040 + 1040]
                                                .rearrange("p (g w) -> p g w", g=4)[:, :, 0:3], 0.0))
                                    Rs["xz"] = sG.v

                            @blk.sync
                            def _(s, c=c, su=su, par=par):
                                if su >= 2:
                                    s.wait_ge(sV.h, SR[su - 2]["free"])
                                    s.wait_ge(sS.h, SR[su - 2]["a64"])
                                if c == 0:
                                    s.wait_ge(sG.h, Rs["xz"])
                                    Rs["xload"] = DS["xl"].dma(
                                        s, xbufs[:, par * 1040:par * 1040 + 1040]
                                        .rearrange("p (g w) -> p g w", g=4)[:, :, 3:259],
                                        AP(zxb_s, (DIN + su * 4 * 128) * T,
                                           [[T, 128], [128 * T, 4], [1, 256]]))
                                else:
                                    Rs["xload"] = DS["xl"].dma(
                                        s, xbufs[:, par * 1040:par * 1040 + 1040]
                                        .rearrange("p (g w) -> p g w", g=4)[:, :, 0:259],
                                        AP(zxb_s, (DIN + su * 4 * 128) * T + c * CHUNK - 3,
                                           [[T, 128], [128 * T, 4], [1, 259]]))
                                Rs["zload"] = DS["zl"].dma(
                                    s, zsb[:, par * 1024:(par + 1) * 1024]
                                    .rearrange("p (g w) -> p g w", g=4),
                                    AP(zxb_s, (su * 4 * 128) * T + c * CHUNK,
                                       [[T, 128], [128 * T, 4], [1, 256]]))
                                wt(s, R["acr"])
                                Rs["rload"] = DS["rs"].dma(
                                    s, rsubb[:, par * 1024:(par + 1) * 1024],
                                    AP(acr_s, c * NH * CHUNK + su * 1024, [[0, 128], [1, 1024]]))

                            @blk.vector
                            def _(v, c=c, su=su, par=par):
                                wt(v, Rs["xload"])
                                if su >= 2:
                                    v.wait_ge(sP.h, SR[su - 2]["xtokp"])
                                for gl in range(4):
                                    gg = su * 4 + gl
                                    xb = xbufs[:, par * 1040 + gl * 260:par * 1040 + gl * 260 + 260]
                                    xo = xcs[:, par * 1024 + gl * 256:par * 1024 + (gl + 1) * 256]
                                    v.wait_ge(sV.h, vi(v.tensor_scalar_mul(xo, xb[:, 0:256],
                                                                           cwcol[:, gg * 4:gg * 4 + 1])))
                                    for j in range(1, 4):
                                        v.wait_ge(sV.h, vi(v.scalar_tensor_tensor(
                                            xo, xb[:, j:j + 256], cwcol[:, gg * 4 + j:gg * 4 + j + 1],
                                            xo, ALU.mult, ALU.add)))
                                Rs["xconv"] = sV.v
                                g_tree_ops(v, 2)

                            @blk.scalar
                            def _(sc, su=su, par=par):
                                sc.wait_ge(sV.h, Rs["xconv"])
                                for gl in range(4):
                                    gg = su * 4 + gl
                                    xo = xcs[:, par * 1024 + gl * 256:par * 1024 + (gl + 1) * 256]
                                    si(sc.activation(xo, xo, AF.Silu, bias=cbcol[:, gg:gg + 1]))
                                Rs["xsilu"] = sS.v
                                wt(sc, Rs["zload"])
                                if su >= 2:
                                    sc.wait_ge(sV.h, SR[su - 2]["gate"])
                                si(sc.activation(zsb[:, par * 1024:(par + 1) * 1024],
                                                 zsb[:, par * 1024:(par + 1) * 1024], AF.Silu))
                                Rs["zsilu"] = sS.v

                            for sb in range(2):
                                dpar = (su % 2) * 2 + sb

                                @blk.vector
                                def _(v, c=c, su=su, sb=sb, par=par, dpar=dpar):
                                    wt(v, Rs["rload"])
                                    v.wait_ge(sV.h, R["cols"])
                                    if su >= 2:
                                        v.wait_ge(sS.h, SR[su - 2]["E%d" % sb])
                                    for hl in range(4):
                                        h = su * 4 + hl
                                        vi(v.tensor_scalar(
                                            dmbb[:, dpar * 1024 + hl * 256:dpar * 1024 + (hl + 1) * 256],
                                            rsubb[:, par * 1024 + hl * 256:par * 1024 + (hl + 1) * 256],
                                            nscol[:, sb * 32 + h:sb * 32 + h + 1], 20.0,
                                            ALU.add, ALU.min))
                                    v.wait_ge(sV.h, sV.v)
                                    Rs["dm%d" % sb] = sV.v

                            @blk.scalar
                            def _(sc, su=su, par=par):
                                for sb in range(2):
                                    dpar = (su % 2) * 2 + sb
                                    sc.wait_ge(sV.h, Rs["dm%d" % sb])
                                    if su >= 2:
                                        sc.wait_ge(sV.h, SR[su - 2]["SL%d" % sb])
                                    si(sc.activation(ebuf[:, dpar * 1024:(dpar + 1) * 1024],
                                                     dmbb[:, dpar * 1024:(dpar + 1) * 1024], AF.Exp))
                                    Rs["E%d" % sb] = sS.v
                                wt(sc, Rs["rload"])
                                if su >= 1:
                                    sc.wait_ge(sV.h, SR[su - 1]["csc"])
                                si(sc.activation(e2b[:, :],
                                                 rsubb[0:64, par * 1024:(par + 1) * 1024], AF.Exp))
                                Rs["E2"] = sS.v
                                si(sc.activation(a64b[:, su * 4:(su + 1) * 4],
                                                 AP(rsubb, par * 1024 + 255, [[2048, 64], [256, 4]]),
                                                 AF.Exp))
                                Rs["a64"] = sS.v

                            for sb in range(2):
                                dpar = (su % 2) * 2 + sb

                                @blk.vector
                                def _(v, su=su, sb=sb, par=par, dpar=dpar):
                                    v.wait_ge(sS.h, Rs["E%d" % sb])
                                    v.wait_ge(sV.h, R["cbt"])
                                    if su >= 2:
                                        v.wait_ge(sP.h, SR[su - 2]["ymm"])
                                    vi(v.tensor_tensor(
                                        slb[:, (sb * 2 + par) * 1024:(sb * 2 + par + 1) * 1024],
                                        ebuf[:, dpar * 1024:(dpar + 1) * 1024],
                                        AP(cbts.handle if hasattr(cbts, "handle") else cbts,
                                           sb * 256, [[512, 128], [0, 4], [1, 256]]),
                                        ALU.mult))
                                    v.wait_ge(sV.h, sV.v)
                                    Rs["SL%d" % sb] = sV.v
                                    g_tree_ops(v, 1)

                            @blk.tensor
                            def _(t, su=su, par=par):
                                t.wait_ge(sS.h, Rs["xsilu"])
                                if su >= 2:
                                    t.wait_ge(sS.h, SR[su - 2]["xtoke"])
                                for gl in range(4):
                                    for sb in range(2):
                                        mm = t.transpose(
                                            psX[:, par * 1024 + (gl * 2 + sb) * 128:par * 1024 + (gl * 2 + sb + 1) * 128],
                                            xcs[:, par * 1024 + gl * 256 + sb * 128:par * 1024 + gl * 256 + (sb + 1) * 128],
                                            ident_b[:, :])
                                pi(mm)
                                Rs["xtokp"] = sP.v

                            @blk.scalar
                            def _(sc, su=su, par=par):
                                sc.wait_ge(sP.h, Rs["xtokp"])
                                si(sc.copy(xtokb[:, par * 1024:(par + 1) * 1024],
                                           psX[:, par * 1024:(par + 1) * 1024]))
                                Rs["xtoke"] = sS.v

                            @blk.vector
                            def _(v, su=su, par=par):
                                v.wait_ge(sS.h, Rs["E2"])
                                wt(v, R["cset"])
                                if su >= 1:
                                    v.wait_ge(sP.h, SR[su - 1]["ymm"])
                                vi(v.tensor_tensor(
                                    cscb[:, :],
                                    e2b[:, :],
                                    AP(cset, cp * 256, [[512, 64], [0, 4], [1, 256]]),
                                    ALU.mult))
                                v.wait_ge(sV.h, sV.v)
                                Rs["csc"] = sV.v
                                g_tree_ops(v, 2)

                            for hl in range(4):
                                h = su * 4 + hl

                                @blk.vector
                                def _(v, su=su, hl=hl, h=h, par=par):
                                    v.wait_ge(sV.h, R["btok"])
                                    if su >= 1 and hl == 0:
                                        v.wait_ge(sV.h, SR[su - 1]["upd"])
                                    if hl >= 2:
                                        v.wait_ge(sP.h, Rs["stmm%d" % (hl - 2)])
                                    for sb in range(2):
                                        vi(v.tensor_scalar_mul(
                                            bdb[:, (hl % 2) * 128 + sb * 64:(hl % 2) * 128 + (sb + 1) * 64],
                                            btokb[:, sb * 64:(sb + 1) * 64],
                                            colsb[:, 128 + sb * 32 + h:128 + sb * 32 + h + 1]))
                                    v.wait_ge(sV.h, sV.v)
                                    Rs["bd%d" % hl] = sV.v

                                @blk.tensor
                                def _(t, su=su, hl=hl, h=h, par=par):
                                    t.wait_ge(sV.h, Rs["bd%d" % hl])
                                    t.wait_ge(sS.h, Rs["xtoke"])
                                    if su >= 1 and hl == 0:
                                        t.wait_ge(sV.h, SR[su - 1]["upd"])
                                    for sb in range(2):
                                        mm = t.matmul(psSt[:, hl * 128:(hl + 1) * 128],
                                                      bdb[:, (hl % 2) * 128 + sb * 64:(hl % 2) * 128 + (sb + 1) * 64],
                                                      xtokb[:, par * 1024 + (hl * 2 + sb) * 128:par * 1024 + (hl * 2 + sb + 1) * 128],
                                                      start=(sb == 0), stop=(sb == 1))
                                    pi(mm)
                                    Rs["stmm%d" % hl] = sP.v

                            for hl in range(4):
                                h = su * 4 + hl

                                @blk.vector
                                def _(v, su=su, hl=hl, h=h):
                                    v.wait_ge(sP.h, Rs["stmm3"])
                                    v.wait_ge(sS.h, Rs["a64"])
                                    vi(v.scalar_tensor_tensor(
                                        prevf[:, h * 128:(h + 1) * 128],
                                        prevf[:, h * 128:(h + 1) * 128],
                                        a64b[:, h:h + 1],
                                        psSt[:, hl * 128:(hl + 1) * 128], ALU.mult, ALU.add))
                                    v.wait_ge(sV.h, sV.v)
                                    if hl == 3:
                                        Rs["upd"] = sV.v
                                    if hl % 2 == 1:
                                        g_tree_ops(v, 1)

                            for pr in range(2):
                                for hl2 in range(2):
                                    hl = pr * 2 + hl2
                                    h = su * 4 + hl

                                    @blk.tensor
                                    def _(t, su=su, pr=pr, hl=hl, h=h, par=par, c=c):
                                        t.wait_ge(sV.h, Rs["SL0"])
                                        t.wait_ge(sV.h, Rs["SL1"])
                                        t.wait_ge(sV.h, Rs["csc"])
                                        t.wait_ge(sV.h, prev["prevcp"])
                                        if pr == 1 and "yev0" in Rs:
                                            pass
                                        if hl == 0 or hl == 2:
                                            if pr == 1:
                                                t.wait_ge(sS.h, Rs["yev0"])
                                            elif su >= 1:
                                                t.wait_ge(sS.h, SR[su - 1]["yev1"])
                                        mm = t.matmul(psY[:, hl2 * 256:(hl2 + 1) * 256],
                                                      xtokb[:, par * 1024 + (hl * 2) * 128:par * 1024 + (hl * 2 + 1) * 128],
                                                      slb[:, (0 * 2 + par) * 1024 + hl * 256:(0 * 2 + par) * 1024 + (hl + 1) * 256],
                                                      start=True, stop=False)
                                        mm = t.matmul(psY[:, hl2 * 256:(hl2 + 1) * 256],
                                                      xtokb[:, par * 1024 + (hl * 2 + 1) * 128:par * 1024 + (hl * 2 + 2) * 128],
                                                      slb[:, (1 * 2 + par) * 1024 + hl * 256:(1 * 2 + par) * 1024 + (hl + 1) * 256],
                                                      start=False, stop=False)
                                        mm = t.matmul(psY[:, hl2 * 256:(hl2 + 1) * 256],
                                                      prevbb[:, h * 128:(h + 1) * 128],
                                                      cscb[:, hl * 256:(hl + 1) * 256],
                                                      start=False, stop=True)
                                        pi(mm)
                                        Rs["ymm%d" % hl] = sP.v

                                @blk.scalar
                                def _(sc, su=su, pr=pr):
                                    sc.wait_ge(sP.h, Rs["ymm%d" % (pr * 2 + 1)])
                                    si(sc.copy(ybuf[:, (su * 4 + pr * 2) * 256:(su * 4 + pr * 2 + 2) * 256],
                                               psY[:, :]))
                                    Rs["yev%d" % pr] = sS.v

                                for hl2 in range(2):
                                    hl = pr * 2 + hl2
                                    h = su * 4 + hl

                                    @blk.vector
                                    def _(v, su=su, hl=hl, h=h, par=par, pr=pr):
                                        v.wait_ge(sS.h, Rs["yev%d" % pr])
                                        vi(v.scalar_tensor_tensor(
                                            ybuf[:, h * 256:(h + 1) * 256],
                                            xcs[:, par * 1024 + hl * 256:par * 1024 + (hl + 1) * 256],
                                            dcol128[:, h:h + 1],
                                            ybuf[:, h * 256:(h + 1) * 256], ALU.mult, ALU.add))
                                        v.wait_ge(sV.h, sV.v)
                                        Rs["dsk%d" % hl] = sV.v

                            Rs["ymm"] = Rs["ymm3"]

                            @blk.vector
                            def _(v, su=su, par=par):
                                v.wait_ge(sS.h, Rs["zsilu"])
                                for hl in range(4):
                                    v.wait_ge(sV.h, Rs["dsk%d" % hl])
                                vi(v.tensor_mul(ybuf[:, su * 1024:(su + 1) * 1024],
                                                ybuf[:, su * 1024:(su + 1) * 1024],
                                                zsb[:, par * 1024:(par + 1) * 1024]))
                                v.wait_ge(sV.h, sV.v)
                                Rs["gate"] = sV.v
                                Rs["free"] = sV.v
                                if su >= 2:
                                    v.wait_ge(sP.h, SR[su - 2]["vmm"])
                                vi(v.tensor_mul(sqbb[:, (su % 2) * 1024:(su % 2 + 1) * 1024],
                                                ybuf[:, su * 1024:(su + 1) * 1024],
                                                ybuf[:, su * 1024:(su + 1) * 1024]))
                                v.wait_ge(sV.h, sV.v)
                                Rs["sq"] = sV.v
                                g_tree_ops(v, 3)

                            @blk.tensor
                            def _(t, su=su, par=par, c=c):
                                t.wait_ge(sV.h, Rs["sq"])
                                for hl in range(4):
                                    mm = t.matmul(psV[:, :], onecol[:, :],
                                                  sqbb[:, (su % 2) * 1024 + hl * 256:
                                                       (su % 2) * 1024 + (hl + 1) * 256],
                                                  start=(su == 0 and hl == 0),
                                                  stop=(su == 7 and hl == 3))
                                pi(mm)
                                Rs["vmm"] = sP.v

                            SR.append(Rs)

                            @blk.gpsimd
                            def _(g):
                                g_load(g, 1)

                            @blk.tensor
                            def _(t):
                                g_tp(t, 1)

                            @blk.scalar
                            def _(sc):
                                g_store(sc, 1)

                            @blk.vector
                            def _(v):
                                g_evac(v, 1)

                            if su == 3 and c + 1 < NCH:
                                Hd[c + 1] = {}

                                @blk.sync
                                def _(s, c=c):
                                    cn = c + 1
                                    cpn = cn % 2
                                    Hd[cn]["bcx"] = DS["bcx"].dma(
                                        s, bcx[:, cpn * 260:cpn * 260 + 259],
                                        AP(zxb_s, (DIN + 4096) * T + cn * CHUNK - 3,
                                           [[T, 128], [1, 259]]))

                            if su == 5 and c + 1 < NCH:
                                @blk.vector
                                def _(v, c=c):
                                    cn = c + 1
                                    cpn = cn % 2
                                    wt(v, Hd[cn]["bcx"])
                                    gg = 32
                                    v.wait_ge(sV.h, vi(v.tensor_scalar_mul(
                                        bcc[:, cpn * 256:cpn * 256 + 256],
                                        bcx[:, cpn * 260:cpn * 260 + 256],
                                        cwcol[:, gg * 4:gg * 4 + 1])))
                                    for j in range(1, 4):
                                        v.wait_ge(sV.h, vi(v.scalar_tensor_tensor(
                                            bcc[:, cpn * 256:cpn * 256 + 256],
                                            bcx[:, cpn * 260 + j:cpn * 260 + j + 256],
                                            cwcol[:, gg * 4 + j:gg * 4 + j + 1],
                                            bcc[:, cpn * 256:cpn * 256 + 256],
                                            ALU.mult, ALU.add)))
                                    Hd[cn]["conv"] = sV.v

                                @blk.scalar
                                def _(sc, c=c):
                                    cn = c + 1
                                    cpn = cn % 2
                                    sc.wait_ge(sV.h, Hd[cn]["conv"])
                                    si(sc.activation(bcc[:, cpn * 256:cpn * 256 + 256],
                                                     bcc[:, cpn * 256:cpn * 256 + 256],
                                                     AF.Silu, bias=cbcol[:, 32:33]))
                                    Hd[cn]["silu"] = sS.v

                                @blk.gpsimd
                                def _(g, c=c):
                                    cn = c + 1
                                    cpn = cn % 2
                                    g.wait_ge(sS.h, Hd[cn]["silu"])
                                    Hd[cn]["cset"] = DS["cse"].dma(
                                        g, cset[:, cpn * 256:(cpn + 1) * 256],
                                        bcc[64:128, cpn * 256:cpn * 256 + 256])

                        @blk.scalar
                        def _(sc, c=c):
                            sc.wait_ge(sP.h, SR[7]["vmm"])
                            sc.wait_ge(sS.h, si(sc.activation(rstd1[:, :], psV[:, :], AF.Ln,
                                                              bias=ceps[0:1, :], scale=cidin[0:1, :])))
                            si(sc.activation(rstd1[:, :], rstd1[:, :], AF.Exp, scale=chalf[0:1, :]))
                            R["rstd"] = sS.v

                        @blk.gpsimd
                        def _(g, c=c):
                            g.wait_ge(sS.h, R["rstd"])
                            R["rbounce"] = DS["rb2"].dma(g, row_s[0:1, 0:256], rstd1[:, :])

                        @blk.sync
                        def _(s, c=c):
                            wt(s, R["rbounce"])
                            R["rbc"] = DS["rbc"].dma(s, rbcb[:, :], AP(row_s, 0, [[0, 128], [1, 256]]))

                        @blk.vector
                        def _(v, c=c):
                            wt(v, R["rbc"])
                            vi(v.tensor_tensor(ybuf[:, :], ybuf[:, :],
                                               AP(rbcb, 0, [[256, 128], [0, 32], [1, 256]]),
                                               ALU.mult))
                            v.wait_ge(sV.h, sV.v)
                            for gg in range(NKI):
                                vi(v.tensor_scalar_mul(ybuf[:, gg * 256:(gg + 1) * 256],
                                                       ybuf[:, gg * 256:(gg + 1) * 256],
                                                       nwcol[:, gg:gg + 1]))
                            v.wait_ge(sV.h, sV.v)
                            R["ynorm"] = sV.v

                        @blk.vector
                        def _(v, c=c):
                            v.wait_ge(sP.h, SR[7]["ymm3"])
                            for q in range(4):
                                vi(v.tensor_copy(prevbb[:, q * 1024:(q + 1) * 1024],
                                                 prevf[:, q * 1024:(q + 1) * 1024]))
                            v.wait_ge(sV.h, sV.v)
                            prev["prevcp"] = sV.v

                        @blk.sync
                        def _(s, c=c):
                            s.wait_ge(sV.h, R["ynorm"])
                            for q in range(4):
                                DS["yst"].dma(
                                    s, AP(ynorm_s, (q * 8 * 128) * T + c * CHUNK,
                                          [[T, 128], [128 * T, 8], [1, 256]]),
                                    ybuf[:, q * 2048:(q + 1) * 2048]
                                    .rearrange("p (g w) -> p g w", g=8))

                        prev["endV"] = sV.v
                        prev["endS"] = sS.v
                        prev["endP"] = sP.v

                    g_drain(blk)
                    flush_stores(blk, ["yst", "rb2", "acr", "cse",
                                       "bcx", "xl", "zl", "rs", "rbc", "gst", "gix"])

        stDT.close()
        stX = ExitStack()
        x1T = stX.enter_context(nc.sbuf_tensor("x1T", [128, NKD * T], BF16))

        if stage in ("b3", "full"):
            with ExitStack() as st3:
                ynsl = st3.enter_context(nc.sbuf_tensor("ynsl", [128, NKI * 512], BF16))
                rsl = st3.enter_context(nc.sbuf_tensor("rsl", [128, NKD * 512], F32))
                wobm = st3.enter_context(nc.sbuf_tensor("wobm", [128, 4 * 4096], BF16))
                ubuf = st3.enter_context(nc.sbuf_tensor("ubuf", [128, 2 * 512], F32))
                rbsl = st3.enter_context(nc.sbuf_tensor("rbsl", [128, 2 * 512], BF16))
                sqsl = st3.enter_context(nc.sbuf_tensor("sqsl", [128, 2 * 512], BF16))
                rowsb = st3.enter_context(nc.sbuf_tensor("rowsb", [1, 4 * 512], F32))
                rnb = st3.enter_context(nc.sbuf_tensor("rnb", [128, 512], F32))
                rsb = st3.enter_context(nc.sbuf_tensor("rsb", [128, 512], F32))
                tmpn = st3.enter_context(nc.sbuf_tensor("tmpn", [128, 512], F32))
                x1stg = st3.enter_context(nc.sbuf_tensor("x1stg", [128, 2 * 512], F32))
                psO = st3.enter_context(nc.psum_tensor("psO", [128, 4 * 512], F32))
                psS3 = st3.enter_context(nc.psum_tensor("psS3", [1, 1024], F32))

                with nc.Block() as blk:
                    @blk.sync
                    def _(s):
                        DS["yst"].flush(s)
                        DS["ust"].flush(s)
                        DS["wos"].flush(s)

                    x1st_v = []
                    for tp in range(4):
                        RTP = RT_prev if tp else None
                        RT = {}

                        @blk.sync
                        def _(s, tp=tp):
                            if tp >= 1:
                                s.wait_ge(sP.h, RT_prev["mm%d" % (NKD - 1)])
                            toks = []
                            for q in range(4):
                                toks.append(DS["ynl"].dma(
                                    s, ynsl[:, q * 4096:(q + 1) * 4096]
                                    .rearrange("p (g w) -> p g w", g=8),
                                    AP(ynorm_s, (q * 8 * 128) * T + tp * 512,
                                       [[T, 128], [128 * T, 8], [1, 512]])))
                            RT["yn"] = toks

                        mev = []
                        for m in range(NKD):
                            mpar = m % 2

                            @blk.sync
                            def _(s, tp=tp, m=m, mpar=mpar):
                                if m >= 4:
                                    s.wait_ge(sP.h, mev[m - 4][0])
                                elif tp >= 1:
                                    s.wait_ge(sP.h, RTP["mm%d" % (NKD - 4 + m)])
                                if m >= 2:
                                    s.wait_ge(sV.h, mev[m - 2][1])
                                elif tp >= 1:
                                    s.wait_ge(sV.h, RTP["rb%d" % (NKD - 2 + m)])
                                RT["w%d" % m] = DS["wom"].dma(
                                    s, wobm[:, (m % 4) * 4096:(m % 4 + 1) * 4096],
                                    AP(wob_s, m * 128 * DIN, [[DIN, 128], [1, 4096]]))
                                utoks = []
                                for ut in range(4):
                                    utoks.append(DS["ub"].dma(
                                        s, ubuf[:, mpar * 512 + ut * 128:mpar * 512 + (ut + 1) * 128],
                                        AP(uT_s, (4 * tp + ut) * 128 * DM + m * 128,
                                           [[DM, 128], [1, 128]])))
                                RT["u%d" % m] = utoks

                            @blk.tensor
                            def _(t, tp=tp, m=m, mpar=mpar):
                                wt(t, RT["w%d" % m])
                                for tok in RT["yn"]:
                                    wt(t, tok)
                                if m >= 4:
                                    t.wait_ge(sV.h, mev[m - 4][1])
                                elif tp >= 1:
                                    t.wait_ge(sV.h, RTP["rb%d" % (NKD - 4 + m)])
                                for k in range(NKI):
                                    mm = t.matmul(psO[:, (m % 4) * 512:(m % 4 + 1) * 512],
                                                  wobm[:, (m % 4) * 4096 + k * 128:(m % 4) * 4096 + (k + 1) * 128],
                                                  ynsl[:, k * 512:(k + 1) * 512],
                                                  start=(k == 0), stop=(k == NKI - 1))
                                pi(mm)
                                RT["mm%d" % m] = sP.v

                            @blk.vector
                            def _(v, tp=tp, m=m, mpar=mpar):
                                v.wait_ge(sP.h, RT["mm%d" % m])
                                for tok in RT["u%d" % m]:
                                    wt(v, tok)
                                if m >= 2:
                                    v.wait_ge(sP.h, RT["st%d" % (m - 2)])
                                elif tp >= 1:
                                    v.wait_ge(sP.h, RTP["st%d" % (NKD - 2 + m)])
                                v.wait_ge(sV.h, vi(v.tensor_add(rsl[:, m * 512:(m + 1) * 512],
                                                                psO[:, (m % 4) * 512:(m % 4 + 1) * 512],
                                                                ubuf[:, mpar * 512:(mpar + 1) * 512])))
                                vi(v.tensor_copy(rbsl[:, mpar * 512:(mpar + 1) * 512],
                                                 rsl[:, m * 512:(m + 1) * 512]))
                                v.wait_ge(sV.h, sV.v)
                                RT["rb%d" % m] = sV.v

                            @blk.scalar
                            def _(sc, tp=tp, m=m, mpar=mpar):
                                sc.wait_ge(sV.h, RT["rb%d" % m])
                                si(sc.activation(sqsl[:, mpar * 512:(mpar + 1) * 512],
                                                 rbsl[:, mpar * 512:(mpar + 1) * 512], AF.Square))
                                RT["sq%d" % m] = sS.v

                            @blk.tensor
                            def _(t, tp=tp, m=m, mpar=mpar):
                                t.wait_ge(sS.h, RT["sq%d" % m])
                                if m == 0 and tp >= 1:
                                    t.wait_ge(sV.h, RT_prev["var"])
                                mm = t.matmul(psS3[:, 0:512], onecol[:, :],
                                              rbsl[:, mpar * 512:(mpar + 1) * 512],
                                              start=(m == 0), stop=(m == NKD - 1))
                                mm = t.matmul(psS3[:, 512:1024], onecol[:, :],
                                              sqsl[:, mpar * 512:(mpar + 1) * 512],
                                              start=(m == 0), stop=(m == NKD - 1))
                                pi(mm)
                                mev.append((sP.v, RT["rb%d" % m]))
                                RT["st%d" % m] = sP.v

                            RT_prev = RT

                        @blk.vector
                        def _(v, tp=tp):
                            v.wait_ge(sP.h, RT["st%d" % (NKD - 1)])
                            v.wait_ge(sV.h, vi(v.tensor_scalar_mul(rowsb[:, 0:512], psS3[:, 0:512], -1.0 / DM)))
                            v.wait_ge(sV.h, vi(v.tensor_scalar_mul(rowsb[:, 512:1024], psS3[:, 512:1024], 1.0 / DM)))
                            v.wait_ge(sV.h, vi(v.tensor_mul(rowsb[:, 1024:1536], rowsb[:, 0:512], rowsb[:, 0:512])))
                            v.wait_ge(sV.h, vi(v.tensor_sub(rowsb[:, 1024:1536], rowsb[:, 512:1024], rowsb[:, 1024:1536])))
                            RT["var"] = sV.v

                        @blk.scalar
                        def _(sc, tp=tp):
                            sc.wait_ge(sV.h, RT["var"])
                            sc.wait_ge(sS.h, si(sc.activation(rowsb[:, 1536:2048], rowsb[:, 1024:1536],
                                                              AF.Ln, bias=ceps[0:1, :])))
                            si(sc.activation(rowsb[:, 1536:2048], rowsb[:, 1536:2048], AF.Exp, scale=chalf[0:1, :]))
                            RT["rstd"] = sS.v

                        @blk.gpsimd
                        def _(g, tp=tp):
                            g.wait_ge(sS.h, RT["rstd"])
                            t1 = DS["r3b"].dma(g, row_s[1:2, 0:512], rowsb[:, 0:512])
                            t2 = DS["r3b"].dma(g, row_s[2:3, 0:512], rowsb[:, 1536:2048])
                            RT["rowb"] = [t1, t2]

                        @blk.sync
                        def _(s, tp=tp):
                            for tok in RT["rowb"]:
                                wt(s, tok)
                            t1 = DS["bc3"].dma(s, rnb[:, :], AP(row_s, 2048, [[0, 128], [1, 512]]))
                            t2 = DS["bc3"].dma(s, rsb[:, :], AP(row_s, 4096, [[0, 128], [1, 512]]))
                            RT["bc"] = [t1, t2]

                        for m in range(NKD):
                            mpar = m % 2

                            @blk.vector
                            def _(v, tp=tp, m=m, mpar=mpar):
                                for tok in RT["bc"]:
                                    wt(v, tok)
                                v.wait_ge(sV.h, vi(v.tensor_add(tmpn[:, :], rsl[:, m * 512:(m + 1) * 512], rnb[:, :])))
                                gidx_st = tp * NKD + m - 2
                                if gidx_st >= 0:
                                    wt(v, x1st_v[gidx_st])
                                v.wait_ge(sV.h, vi(v.tensor_mul(x1stg[:, mpar * 512:(mpar + 1) * 512],
                                                                tmpn[:, :], rsb[:, :])))
                                vi(v.tensor_copy(x1T[:, m * T + tp * 512:m * T + (tp + 1) * 512],
                                                 x1stg[:, mpar * 512:(mpar + 1) * 512]))
                                v.wait_ge(sV.h, sV.v)
                                RT["x1%d" % m] = sV.v

                            @blk.sync
                            def _(s, tp=tp, m=m, mpar=mpar):
                                s.wait_ge(sV.h, RT["x1%d" % m])
                                x1st_v.append(DS["x1s"].dma(
                                    s, AP(x1f_s, m * 128 * T + tp * 512, [[T, 128], [1, 512]]),
                                    x1stg[:, mpar * 512:(mpar + 1) * 512]))

                        RT["x1done"] = RT["x1%d" % (NKD - 1)]
                        RT_prev = RT
                    flush_stores(blk, ["x1s", "r3b", "ynl", "wom", "ub", "bc3"])

        if stage == "full":
            with ExitStack() as st4:
                wgb = st4.enter_context(nc.sbuf_tensor("wgb", [128, 2 * 2048], BF16))
                wub = st4.enter_context(nc.sbuf_tensor("wub", [128, 2 * 2048], BF16))
                sgb = st4.enter_context(nc.sbuf_tensor("sgb", [128, 2 * 512], BF16))
                hstg = st4.enter_context(nc.sbuf_tensor("hstg", [128, 2 * 512], BF16))
                psG = st4.enter_context(nc.psum_tensor("psG", [128, 2 * 512], F32))
                psU = st4.enter_context(nc.psum_tensor("psU", [128, 2 * 512], F32))
                galloc(st4)

                with nc.Block() as blk:
                    wload = []
                    hmul_v = {}
                    hst_v = {}
                    for m in range(NMO):
                        @blk.sync
                        def _(s, m=m):
                            if m >= 2:
                                s.wait_ge(sP.h, mm_last[m - 2])
                            wt(s, wg_store_v[m])
                            wt(s, wu_store_v[m])
                            t1 = DS["wg"].dma(s, wgb[:, (m % 2) * 2048:(m % 2 + 1) * 2048],
                                              AP(wgT_s, m * 128 * DM, [[DM, 128], [1, 2048]]))
                            t2 = DS["wu"].dma(s, wub[:, (m % 2) * 2048:(m % 2 + 1) * 2048],
                                              AP(wuT_s, m * 128 * DM, [[DM, 128], [1, 2048]]))
                            wload.append([t1, t2])

                        mm_last = mm_last if m else {}
                        for tt in range(4):
                            tpar = tt % 2

                            @blk.tensor
                            def _(t, m=m, tt=tt, tpar=tpar):
                                for tok in wload[m]:
                                    wt(t, tok)
                                if (m, tt) != (0, 0):
                                    pm, ptt = (m, tt - 2) if tt >= 2 else (m - 1, tt + 2)
                                    if ptt >= 0 and (pm, ptt) in hmul_v:
                                        t.wait_ge(sV.h, hmul_v[(pm, ptt)])
                                for k in range(NKD):
                                    mm = t.matmul(psG[:, tpar * 512:(tpar + 1) * 512],
                                                  wgb[:, (m % 2) * 2048 + k * 128:(m % 2) * 2048 + (k + 1) * 128],
                                                  x1T[:, k * T + tt * 512:k * T + (tt + 1) * 512],
                                                  start=(k == 0), stop=(k == NKD - 1))
                                for k in range(NKD):
                                    mm = t.matmul(psU[:, tpar * 512:(tpar + 1) * 512],
                                                  wub[:, (m % 2) * 2048 + k * 128:(m % 2) * 2048 + (k + 1) * 128],
                                                  x1T[:, k * T + tt * 512:k * T + (tt + 1) * 512],
                                                  start=(k == 0), stop=(k == NKD - 1))
                                pi(mm)
                                mm_last[m] = sP.v
                                hmul_v[(m, tt, "mm")] = sP.v

                            @blk.scalar
                            def _(sc, m=m, tt=tt, tpar=tpar):
                                sc.wait_ge(sP.h, hmul_v[(m, tt, "mm")])
                                pm, ptt = (m, tt - 2) if tt >= 2 else (m - 1, tt + 2)
                                if (pm, ptt) in hmul_v:
                                    sc.wait_ge(sV.h, hmul_v[(pm, ptt)])
                                si(sc.activation(sgb[:, tpar * 512:(tpar + 1) * 512],
                                                 psG[:, tpar * 512:(tpar + 1) * 512], AF.Silu))
                                hmul_v[(m, tt, "sg")] = sS.v

                            @blk.vector
                            def _(v, m=m, tt=tt, tpar=tpar):
                                v.wait_ge(sS.h, hmul_v[(m, tt, "sg")])
                                pm, ptt = (m, tt - 2) if tt >= 2 else (m - 1, tt + 2)
                                if (pm, ptt) in hst_v:
                                    wt(v, hst_v[(pm, ptt)])
                                vi(v.tensor_mul(hstg[:, tpar * 512:(tpar + 1) * 512],
                                                sgb[:, tpar * 512:(tpar + 1) * 512],
                                                psU[:, tpar * 512:(tpar + 1) * 512]))
                                v.wait_ge(sV.h, sV.v)
                                hmul_v[(m, tt)] = sV.v
                                g_tree_ops(v, 6)

                            @blk.sync
                            def _(s, m=m, tt=tt, tpar=tpar):
                                s.wait_ge(sV.h, hmul_v[(m, tt)])
                                hst_v[(m, tt)] = DS["hst"].dma(
                                    s, AP(h_s, m * 128 * T + tt * 512, [[T, 128], [1, 512]]),
                                    hstg[:, tpar * 512:(tpar + 1) * 512])

                        @blk.gpsimd
                        def _(g):
                            g_load(g, 2)

                        @blk.tensor
                        def _(t):
                            g_tp(t, 2)

                        @blk.scalar
                        def _(sc):
                            g_store(sc, 2)

                        @blk.vector
                        def _(v):
                            g_evac(v, 2)
                    g_drain(blk, all_tiles=True)
                    flush_stores(blk, ["hst", "gst", "gix", "wg", "wu"])

            stX.close()
            stR = ExitStack()
            r2x = stR.enter_context(nc.sbuf_tensor("r2x", [128, NKD * T], F32))
            with ExitStack() as st5:
                with nc.Block() as blk:
                    @blk.sync
                    def _(s):
                        DS["hst"].flush(s)
                        DS["x1s"].flush(s)
                        DS["gst"].flush(s)

                    with (
                        nc.sbuf_tensor("wdb", [128, 4 * INTER], BF16) as wdb,
                        nc.sbuf_tensor("hbb", [128, 8 * 512], BF16) as hbb,
                        nc.sbuf_tensor("x1fb", [128, 2 * 512], F32) as x1fb,
                        nc.psum_tensor("psD", [128, 8 * 512], F32) as psD,
                    ):
                        hld = {}
                        dn_ev = {}
                        for qd in range(4):
                            RQ = {}

                            @blk.sync
                            def _(s, qd=qd):
                                if qd >= 1:
                                    for rdl in range(4):
                                        s.wait_ge(sP.h, RQ_prev["mm_%d_3" % rdl])
                                toks = []
                                for rdl in range(4):
                                    toks.append(DS["wd"].dma(
                                        s, wdb[:, rdl * INTER:(rdl + 1) * INTER],
                                        AP(wdT_s, (qd * 4 + rdl) * 128 * INTER,
                                           [[INTER, 128], [1, INTER]])))
                                RQ["wd"] = toks

                            for tt in range(4):
                                tpar = tt % 2

                                for k in range(NMO):
                                    @blk.sync
                                    def _(s, qd=qd, tt=tt, k=k):
                                        if k >= 8:
                                            s.wait_ge(sP.h, RQ["kmm_%d_%d" % (tt, k - 8)])
                                        elif tt >= 1:
                                            s.wait_ge(sP.h, RQ["kmm_%d_%d" % (tt - 1, NMO - 8 + k)])
                                        elif qd >= 1:
                                            s.wait_ge(sP.h, RQ_prev["kmm_%d_%d" % (3, NMO - 8 + k)])
                                        RQ["hld_%d_%d" % (tt, k)] = DS["hl"].dma(
                                            s, hbb[:, (k % 8) * 512:(k % 8 + 1) * 512],
                                            AP(h_s, k * 128 * T + tt * 512,
                                               [[T, 128], [1, 512]]))

                                    @blk.tensor
                                    def _(t, qd=qd, tt=tt, k=k, tpar=tpar):
                                        wt(t, RQ["hld_%d_%d" % (tt, k)])
                                        if k == 0:
                                            for tok in RQ["wd"]:
                                                wt(t, tok)
                                            if (qd, tt) != (0, 0):
                                                for rdl in range(4):
                                                    key = (qd, tt - 2, rdl) if tt >= 2 else (qd - 1, tt + 2, rdl)
                                                    if key in dn_ev:
                                                        t.wait_ge(sV.h, dn_ev[key])
                                        for rdl in range(4):
                                            mm = t.matmul(
                                                psD[:, (tpar * 4 + rdl) * 512:(tpar * 4 + rdl + 1) * 512],
                                                wdb[:, rdl * INTER + k * 128:rdl * INTER + (k + 1) * 128],
                                                hbb[:, (k % 8) * 512:(k % 8 + 1) * 512],
                                                start=(k == 0), stop=(k == NMO - 1))
                                        pi(mm)
                                        RQ["kmm_%d_%d" % (tt, k)] = sP.v
                                        if k == NMO - 1:
                                            for rdl in range(4):
                                                RQ["mm_%d_%d" % (rdl, tt)] = sP.v

                                for rdl in range(4):
                                    rd = qd * 4 + rdl

                                    @blk.sync
                                    def _(s, qd=qd, tt=tt, rdl=rdl, rd=rd, tpar=tpar):
                                        if rdl >= 2:
                                            key = (qd, tt, rdl - 2)
                                        elif tt >= 1:
                                            key = (qd, tt - 1, rdl + 2)
                                        elif qd >= 1:
                                            key = (qd - 1, 3, rdl + 2)
                                        else:
                                            key = None
                                        if key in dn_ev:
                                            s.wait_ge(sV.h, dn_ev[key])
                                        RQ["x1f_%d_%d" % (rdl, tt)] = DS["xfl"].dma(
                                            s, x1fb[:, (rdl % 2) * 512:(rdl % 2 + 1) * 512],
                                            AP(x1f_s, rd * 128 * T + tt * 512,
                                               [[T, 128], [1, 512]]))

                                    @blk.vector
                                    def _(v, qd=qd, tt=tt, rdl=rdl, rd=rd, tpar=tpar):
                                        v.wait_ge(sP.h, RQ["mm_%d_%d" % (rdl, tt)])
                                        wt(v, RQ["x1f_%d_%d" % (rdl, tt)])
                                        vi(v.tensor_add(r2x[:, rd * T + tt * 512:rd * T + (tt + 1) * 512],
                                                        psD[:, (tpar * 4 + rdl) * 512:(tpar * 4 + rdl + 1) * 512],
                                                        x1fb[:, (rdl % 2) * 512:(rdl % 2 + 1) * 512]))
                                        v.wait_ge(sV.h, sV.v)
                                        dn_ev[(qd, tt, rdl)] = sV.v

                            RQ_prev = RQ

                    with (
                        nc.psum_tensor("psS4a", [1, 2048], F32) as psS4a,
                        nc.psum_tensor("psS4b", [1, 2048], F32) as psS4b,
                        nc.sbuf_tensor("tb4", [128, 2 * 2048], BF16) as tb4,
                        nc.sbuf_tensor("rows4", [1, 4 * 2048], F32) as rows4,
                    ):
                        st4v = {}
                        for m in range(NKD):
                            @blk.vector
                            def _(v, m=m):
                                if m >= 1:
                                    v.wait_ge(sS.h, st4v[("sq", m - 1)])
                                    v.wait_ge(sP.h, st4v[("mm", m - 1)])
                                v.wait_ge(sV.h, vi(v.tensor_copy(tb4[:, 0:2048],
                                                                 r2x[:, m * T:(m + 1) * T])))
                                st4v[("rb", m)] = sV.v

                            @blk.scalar
                            def _(sc, m=m):
                                sc.wait_ge(sV.h, st4v[("rb", m)])
                                si(sc.activation(tb4[:, 2048:4096], tb4[:, 0:2048], AF.Square))
                                st4v[("sq", m)] = sS.v

                            @blk.tensor
                            def _(t, m=m):
                                t.wait_ge(sS.h, st4v[("sq", m)])
                                for w in range(4):
                                    mm = t.matmul(psS4a[:, w * 512:(w + 1) * 512], onecol[:, :],
                                                  tb4[:, w * 512:(w + 1) * 512],
                                                  start=(m == 0), stop=(m == NKD - 1))
                                    mm = t.matmul(psS4b[:, w * 512:(w + 1) * 512], onecol[:, :],
                                                  tb4[:, 2048 + w * 512:2048 + (w + 1) * 512],
                                                  start=(m == 0), stop=(m == NKD - 1))
                                pi(mm)
                                st4v[("mm", m)] = sP.v

                        @blk.vector
                        def _(v):
                            v.wait_ge(sP.h, st4v[("mm", NKD - 1)])
                            v.wait_ge(sV.h, vi(v.tensor_scalar_mul(rows4[:, 0:2048], psS4a[:, :], -1.0 / DM)))
                            v.wait_ge(sV.h, vi(v.tensor_scalar_mul(rows4[:, 2048:4096], psS4b[:, :], 1.0 / DM)))
                            v.wait_ge(sV.h, vi(v.tensor_mul(rows4[:, 4096:6144], rows4[:, 0:2048], rows4[:, 0:2048])))
                            v.wait_ge(sV.h, vi(v.tensor_sub(rows4[:, 4096:6144], rows4[:, 2048:4096], rows4[:, 4096:6144])))
                        v_var4 = sV.v

                        @blk.scalar
                        def _(sc):
                            sc.wait_ge(sV.h, v_var4)
                            sc.wait_ge(sS.h, si(sc.activation(rows4[:, 6144:8192], rows4[:, 4096:6144],
                                                              AF.Ln, bias=ceps[0:1, :])))
                            si(sc.activation(rows4[:, 6144:8192], rows4[:, 6144:8192], AF.Exp, scale=chalf[0:1, :]))
                        v_rstd4 = sS.v

                        rows4_toks = []

                        @blk.gpsimd
                        def _(g):
                            g.wait_ge(sS.h, v_rstd4)
                            rows4_toks.append(DS["r4b"].dma(g, row_s[1:2, :], rows4[:, 0:2048]))
                            rows4_toks.append(DS["r4b"].dma(g, row_s[2:3, :], rows4[:, 6144:8192]))


            with ExitStack() as st6:
                ostg = st6.enter_context(nc.sbuf_tensor("ostg", [128, 2 * 2048], F32))
                rsc = st6.enter_context(nc.sbuf_tensor("rsc", [128, 16], F32))
                rnc = st6.enter_context(nc.sbuf_tensor("rnc", [128, 16], F32))
                rnrs = st6.enter_context(nc.sbuf_tensor("rnrs", [128, 16], F32))
                psFa = st6.enter_context(nc.psum_tensor("psFa", [128, 2048], F32))
                psFb = st6.enter_context(nc.psum_tensor("psFb", [128, 2048], F32))

                with nc.Block() as blk:
                    colt = []

                    @blk.sync
                    def _(s):
                        for tok in rows4_toks:
                            wt(s, tok)
                        with nc.allow_non_contiguous_dma(reason="tiny per-token scale cols"):
                            colt.append(DS["bc4"].dma(s, rnc[:, :],
                                                      AP(row_s, 2048, [[1, 128], [128, 16]])))
                            colt.append(DS["bc4"].dma(s, rsc[:, :],
                                                      AP(row_s, 4096, [[1, 128], [128, 16]])))

                    @blk.vector
                    def _(v):
                        for tok in colt:
                            wt(v, tok)
                        vi(v.tensor_mul(rnrs[:, :], rnc[:, :], rsc[:, :]))
                        v.wait_ge(sV.h, sV.v)
                    v_rnrs = sV.v

                    oev = []
                    ost = []
                    ostq = []
                    for ti in range(NKD):
                        ps = psFa if ti % 2 == 0 else psFb

                        @blk.tensor
                        def _(t, ti=ti, ps=ps):
                            if ti >= 2:
                                t.wait_ge(sS.h, oev[ti - 2])
                            for j in range(NKD):
                                mm = t.transpose(ps[:, j * 128:(j + 1) * 128],
                                                 r2x[:, j * T + ti * 128:j * T + (ti + 1) * 128],
                                                 ident_f[:, :])
                            pi(mm)
                            oev.append(None)
                            ost.append(sP.v)

                        @blk.scalar
                        def _(sc, ti=ti, ps=ps):
                            sc.wait_ge(sP.h, ost[ti])
                            sc.wait_ge(sV.h, v_rnrs)
                            if ti >= 2:
                                wt(sc, ostq[ti - 2])
                            si(sc.activation(ostg[:, (ti % 2) * 2048:(ti % 2 + 1) * 2048],
                                             ps[:, :], AF.Identity,
                                             bias=rnrs[:, ti:ti + 1], scale=rsc[:, ti:ti + 1]))
                            oev[ti] = sS.v

                        @blk.sync
                        def _(s, ti=ti):
                            s.wait_ge(sS.h, oev[ti])
                            ostq.append(DS["out"].dma(
                                s, out[ti * 128:(ti + 1) * 128, :],
                                ostg[:, (ti % 2) * 2048:(ti % 2 + 1) * 2048]))

            stR.close()

        with nc.Block() as blk:
            @blk.sync
            def _(s):
                for name, dst in dump_outs.items():
                    if name in dump_specs:
                        DS["dmp"].dma(s, dst.ap(), dump_specs[name].ap())
                for st_ in DS.values():
                    st_.flush(s)

    return nc


def _in_map(inputs, b):
    return {
        "u": np.ascontiguousarray(inputs["L0_attn_u"][b]),
        "w_in": inputs["L0_attn_W_in"],
        "conv_w": inputs["L0_attn_conv_w"],
        "conv_b": inputs["L0_attn_conv_b"].reshape(CONVD, 1),
        "dt_bias": inputs["L0_attn_dt_bias"].reshape(NH, 1),
        "a_log": inputs["L0_attn_A_log"].reshape(NH, 1),
        "d_vec": inputs["L0_attn_D"].reshape(1, NH),
        "norm_w": inputs["L0_attn_norm_w"].reshape(DIN, 1),
        "w_out": inputs["L0_attn_W_out"],
        "lut": inputs["L0_ffn_lut"].reshape(1, 16),
        "gate_idx": inputs["L0_ffn_gate_idx"],
        "up_idx": inputs["L0_ffn_up_idx"],
        "down_idx": inputs["L0_ffn_down_idx"],
    }


def kernel(**inputs):
    if "nc" not in _CACHE:
        _CACHE["nc"] = build("full")
    nc = _CACHE["nc"]
    in_maps = [_in_map(inputs, b) for b in range(8)]
    res = run_bass_kernel_spmd(nc, in_maps, core_ids=list(range(8)))
    return np.stack([res.results[b]["out"] for b in range(8)], axis=0)
